# version 05
# baseline (speedup 1.0000x reference)
import numpy as np
import ml_dtypes
import concourse.bass as bass
import concourse.mybir as mybir
from concourse.bass_utils import run_bass_kernel_spmd

F32 = mybir.dt.float32
BF16 = mybir.dt.bfloat16
AF = mybir.ActivationFunctionType
ALU = mybir.AluOpType
AX = mybir.AxisListType

RND_C = float(1.5 * 2 ** 23)
LN_EPS = 1e-5


class Cfg:
    def __init__(self, T=2048, D=1024, N=16, DI=3072, TTL=512):
        self.T, self.D, self.N, self.DI, self.TTL = T, D, N, DI, TTL
        self.NTT = T // TTL
        self.DC = D // 128
        self.IC = DI // 128
        self.KC = D // 128


class Sem:
    def __init__(self, es, nc, name):
        self.h = es.enter_context(nc.semaphore(name))
        self.n = 0

    def inc(self, k=1):
        self.n += k
        return self.n


class Arena:
    def __init__(self, nc, name, cols, dtype):
        self.t = nc.alloc_sbuf_tensor(name, [128, cols], dtype)
        self.free_list = [(0, cols)]

    def alloc(self, n, align=8):
        for i, (s, e) in enumerate(self.free_list):
            a = (s + align - 1) // align * align
            if a + n <= e:
                segs = []
                if a > s:
                    segs.append((s, a))
                if a + n < e:
                    segs.append((a + n, e))
                self.free_list[i:i + 1] = segs
                return a
        raise RuntimeError(f"arena OOM: need {n}, free={self.free_list}")

    def free(self, off, n):
        self.free_list.append((off, off + n))
        self.free_list.sort()
        m = []
        for s, e in self.free_list:
            if m and m[-1][1] >= s:
                m[-1] = (m[-1][0], max(m[-1][1], e))
            else:
                m.append((s, e))
        self.free_list = [(s, e) for s, e in m]


class Buf:
    def __init__(self, arena, cols, align=8):
        self.arena = arena
        self.off = arena.alloc(cols, align)
        self.cols = cols

    def ap(self, c0=0, n=None, rows=128, row0=0):
        if n is None:
            n = self.cols - c0
        return self.arena.t.ap()[row0:row0 + rows, self.off + c0:self.off + c0 + n]

    def release(self):
        self.arena.free(self.off, self.cols)


def build_nc(cfg, dbg_spills=()):
    import contextlib
    es = contextlib.ExitStack()
    nc = bass.Bass("TRN2", target_bir_lowering=False, debug=False,
                   detect_race_conditions=False)
    T, D, N, DI, TTL = cfg.T, cfg.D, cfg.N, cfg.DI, cfg.TTL
    DC, IC, KC, NTT = cfg.DC, cfg.IC, cfg.KC, cfg.NTT
    TJ = TTL // 128

    hbm = {}
    hbm["x_in"] = nc.dram_tensor("x_in", [T, D], F32, kind="ExternalInput").ap()
    for wn, shape in [("W_in", [D, 2 * D]), ("W_x", [D, D + 2 * N]),
                      ("W_dt", [D, D]), ("W_out", [D, D]),
                      ("w_up", [DI, D]), ("w_gate", [DI, D]), ("w_down", [D, DI])]:
        hbm[wn] = nc.dram_tensor(wn, shape, F32, kind="ExternalInput").ap()
    for cn, shape, dt in [("ident", [128, 128], F32), ("ones1", [128, 128], F32),
                          ("identb", [128, 128], BF16),
                          ("sel", [2 * N, 2 * N * 128], BF16),
                          ("bdt_col", [128, DC], F32), ("alog_col", [128, DC * N], F32),
                          ("dpar_col", [128, DC], F32), ("lng_col", [128, DC], F32),
                          ("lnb_col", [128, DC], F32), ("mask_col", [128, 2], F32)]:
        hbm[cn] = nc.dram_tensor(cn, shape, dt, kind="ExternalInput").ap()
    hbm["out"] = nc.dram_tensor("out", [T, D], F32, kind="ExternalOutput").ap()

    for k, shape, dt in [
        ("sp_xt_f32", [DC, 128, T], F32), ("sp_xt_hi", [DC, 128, T], BF16),
        ("sp_xt_lo", [DC, 128, T], BF16),
        ("sp_x_f32", [DC, 128, T], F32), ("sp_x_hi", [DC, 128, T], BF16),
        ("sp_x_lo", [DC, 128, T], BF16),
        ("sp_sz", [DC, 128, T], F32),
        ("sp_dr_hi", [DC, 128, T], BF16), ("sp_dr_lo", [DC, 128, T], BF16),
        ("sp_S", [DC, 128, T], F32), ("sp_y", [DC, 128, T], F32),
        ("sp_x1", [DC, 128, T], F32),
        ("sp_up", [IC, 128, T], F32), ("sp_gate", [IC, 128, T], F32),
        ("sp_wdn_hi", [IC, 128, D], BF16), ("sp_wdn_lo", [IC, 128, D], BF16),
        ("sp_pg", [IC, 128, T], F32), ("sp_po", [IC, 128, T], F32),
    ]:
        kind = "ExternalOutput" if k in dbg_spills else "Internal"
        hbm[k] = nc.dram_tensor(k, list(shape), dt, kind=kind).ap()

    dbg_rd = nc.dram_tensor("dbg_rd", [2, 128, 24], F32,
                            kind="ExternalOutput" if "dbg_rd" in dbg_spills else "Internal").ap()
    dbg_xq = nc.dram_tensor("dbg_xq", [2, 128, DC * T], BF16,
                            kind="ExternalOutput" if "dbg_xq" in dbg_spills else "Internal").ap()
    dbg_wq = nc.dram_tensor("dbg_wq", [2, 128, KC * 128], BF16,
                            kind="ExternalOutput" if "dbg_wq" in dbg_spills else "Internal").ap()
    dbg_sc = nc.dram_tensor("dbg_sc", [4, 128, 24], F32,
                            kind="ExternalOutput" if "dbg_sc" in dbg_spills else "Internal").ap()
    cc_carry_in = nc.dram_tensor("cc_carry_in", [128, N * DC], F32)
    cc_carry_out = nc.dram_tensor("cc_carry_out", [128, N * DC], F32)
    cc_max_in = nc.dram_tensor("cc_max_in", [1, 16], F32)
    cc_max_out = nc.dram_tensor("cc_max_out", [1, 16], F32)

    sb = {}
    for cn, shape, dt in [("ident", [128, 128], F32), ("ones1", [128, 128], F32),
                          ("identb", [128, 128], BF16),
                          ("sel", [2 * N, 2 * N * 128], BF16),
                          ("bdt_col", [128, DC], F32), ("alog_col", [128, DC * N], F32),
                          ("dpar_col", [128, DC], F32), ("lng_col", [128, DC], F32),
                          ("lnb_col", [128, DC], F32), ("mask_col", [128, 2], F32),
                          ("A", [128, DC * N], F32), ("zeros", [128, TTL], F32),
                          ("rndc", [128, 1], F32), ("state", [128, N * DC], F32),
                          ("scar", [128, DC], F32),
                          ("bch", [2 * N, T], BF16), ("bcl", [2 * N, T], BF16),
                          ("lns", [1, 4 * max(TTL, 256)], F32),
                          ("red", [128, 8], F32), ("redt", [128, 1], F32),
                          ("qs", [1, 16], F32), ("qcol", [128, 8], F32),
                          ("wbs", [128, 32], F32), ("wbr", [128, 32], F32)]:
        sb[cn] = nc.alloc_sbuf_tensor("sb_" + cn, shape, dt)

    psum = [es.enter_context(nc.psum_tensor(f"pb{i}", [128, 512], F32)) for i in range(8)]

    af = Arena(nc, "af32", 17024, F32)
    ab = Arena(nc, "abf16", 43000, BF16)

    ops = {"pe": [], "dv": [], "ac": [], "gp": [], "sy": []}
    sW = Sem(es, nc, "sW")
    sG = Sem(es, nc, "sG")
    sPE = Sem(es, nc, "sPE")
    sDV = Sem(es, nc, "sDV")
    sAC = Sem(es, nc, "sAC")
    sCC = Sem(es, nc, "sCC")

    def E(q, fn, sem=None, inc=1):
        if sem is None:
            ops[q].append(fn)
            return None
        ops[q].append(lambda e, f=fn, h=sem.h, k=inc: f(e).then_inc(h, k))
        return sem.inc(inc)

    def W(q, sem, v):
        if v and v > 0:
            ops[q].append(lambda e, h=sem.h, v=v: e.wait_ge(h, v))

    def DMA(q, dst, src):
        sem = sW if q == "sy" else sG
        ops[q].append(lambda e, d=dst, s=src: e.dma_start(out=d, in_=s).then_inc(sem.h, 16))
        return sem.inc(16)

    bank_free = [None] * 8

    def bank_wait(j):
        if bank_free[j] is not None:
            sem, v = bank_free[j]
            W("pe", sem, v)

    def phase_barrier():
        for q in ("sy", "pe", "dv", "ac", "gp"):
            for sem in (sW, sG, sPE, sDV, sAC, sCC):
                W(q, sem, sem.n)

    for name in ["ident", "ones1", "identb", "sel", "bdt_col", "alog_col",
                 "dpar_col", "lng_col", "lnb_col", "mask_col"]:
        DMA("sy", sb[name].ap(), hbm[name])
    w_consts = sW.n

    E("dv", lambda e: e.memset(sb["zeros"].ap(), 0.0))
    E("dv", lambda e: e.memset(sb["rndc"].ap(), RND_C))
    E("dv", lambda e: e.memset(sb["state"].ap(), 0.0))
    E("dv", lambda e: e.memset(sb["scar"].ap(), 0.0))
    E("dv", lambda e: e.memset(sb["qs"].ap(), 0.0))
    E("dv", lambda e: e.memset(sb["red"].ap(), 0.0), sDV)
    W("ac", sW, w_consts)
    va = E("ac", lambda e: e.activation(sb["A"].ap(), sb["alog_col"].ap(), AF.Exp), sAC)
    W("dv", sAC, va)
    E("dv", lambda e: e.tensor_scalar(sb["A"].ap(), sb["A"].ap(), -1.0, None, ALU.mult), sDV)

    def load_split_weight(wname, rows, cols, f32buf, hibuf, lobuf):
        nkc = rows // 128
        v = DMA("sy", f32buf.ap().rearrange("p (k m) -> p k m", k=nkc),
                hbm[wname].rearrange("(k p) m -> p k m", p=128))
        W("ac", sW, v)
        va = E("ac", lambda e, o=hibuf.ap(), i=f32buf.ap(): e.copy(o, i), sAC)
        W("dv", sAC, va)
        vd = E("dv", lambda e, o=lobuf.ap(), a=f32buf.ap(), b=hibuf.ap():
               e.tensor_tensor(o, a, b, ALU.subtract), sDV)
        W("sy", sDV, vd)
        W("gp", sDV, vd)
        return vd

    def mm3(bank, nkc, get_w, get_x, mw=128, nw=None):
        if nw is None:
            nw = TTL
        for kc in range(nkc):
            wh, wl = get_w(kc)
            xh, xl = get_x(kc)
            E("pe", lambda e, o=psum[bank].ap()[0:mw, 0:nw], a=wh, b=xh, s=(kc == 0):
              e.matmul(o, a, b, start=s, stop=False))
            E("pe", lambda e, o=psum[bank].ap()[0:mw, 0:nw], a=wh, b=xl:
              e.matmul(o, a, b, start=False, stop=False))
            v = E("pe", lambda e, o=psum[bank].ap()[0:mw, 0:nw], a=wl, b=xh,
                  st=(kc == nkc - 1): e.matmul(o, a, b, start=False, stop=st),
                  sPE if kc == nkc - 1 else None)
        return v

    w_in_f32 = Buf(af, 0) if False else None
    xt_f = Buf(af, DC * TTL)
    xt_h = Buf(ab, DC * TTL)
    xt_l = Buf(ab, DC * TTL)
    xrow = [Buf(af, D), Buf(af, D)]
    W("pe", sW, w_consts)
    vpe_row = [0, 0]
    va_evac = 0
    for tt in range(NTT):
        for j in range(TJ):
            par = j % 2
            trow = tt * TJ + j
            W("sy", sPE, vpe_row[par])
            vld = DMA("sy", xrow[par].ap(), hbm["x_in"][trow * 128:(trow + 1) * 128, :])
            W("pe", sW, vld)
            for dc in range(DC):
                bank = dc % 4
                bank_wait(bank)
                vpe = E("pe", lambda e, o=psum[bank].ap()[:, 0:128],
                        i=xrow[par].ap(dc * 128, 128), idn=sb["ident"].ap():
                        e.transpose(o, i, idn), sPE)
                W("ac", sPE, vpe)
                va = E("ac", lambda e, o=xt_f.ap(dc * TTL + j * 128, 128),
                       i=psum[bank].ap()[:, 0:128]: e.copy(o, i), sAC)
                bank_free[bank] = (sAC, va)
            vpe_row[par] = vpe
            va_evac = va
        W("dv", sAC, va_evac)
        E("dv", lambda e, o=xt_h.ap(), i=xt_f.ap(): e.tensor_copy(o, i))
        vd = E("dv", lambda e, o=xt_l.ap(), a=xt_f.ap(), b=xt_h.ap():
               e.tensor_tensor(o, a, b, ALU.subtract), sDV)
        W("gp", sDV, vd)
        for dc in range(DC):
            DMA("gp", hbm["sp_xt_f32"][dc, :, tt * TTL:(tt + 1) * TTL], xt_f.ap(dc * TTL, TTL))
            DMA("gp", hbm["sp_xt_hi"][dc, :, tt * TTL:(tt + 1) * TTL], xt_h.ap(dc * TTL, TTL))
            vg = DMA("gp", hbm["sp_xt_lo"][dc, :, tt * TTL:(tt + 1) * TTL], xt_l.ap(dc * TTL, TTL))
        W("ac", sG, vg)
        W("dv", sG, vg)
    for b_ in xrow:
        b_.release()
    xt_f.release()
    xt_h.release()
    xt_l.release()
    spill_xt = sG.n
    phase_barrier()

    wf = Buf(af, 0) if False else None
    w_f = Buf(af, KC * 128)
    w_f.release()

    w1f = Buf(af, 0) if False else None
    w1_f32 = Buf(af, 0) if False else None

    win_f = Buf(af, KC * 2 * D // 4) if False else None
    win_hi = Buf(ab, KC * 2 * D)
    win_lo = Buf(ab, KC * 2 * D)
    wstage = Buf(af, KC * D)
    for half in range(2):
        v = DMA("sy", wstage.ap().rearrange("p (k m) -> p k m", k=KC),
                hbm["W_in"][:, half * D:(half + 1) * D].rearrange("(k p) m -> p k m", p=128))
        W("ac", sW, v)
        va = E("ac", lambda e, o=win_hi.ap() if False else None: None) if False else None
        for kc in range(KC):
            va = E("ac", lambda e, o=win_hi.ap(kc * 2 * D + half * D, D),
                   i=wstage.ap(kc * D, D): e.copy(o, i), sAC)
        W("dv", sAC, va)
        for kc in range(KC):
            vd = E("dv", lambda e, o=win_lo.ap(kc * 2 * D + half * D, D),
                   a=wstage.ap(kc * D, D), b=win_hi.ap(kc * 2 * D + half * D, D):
                   e.tensor_tensor(o, a, b, ALU.subtract), sDV)
        W("sy", sDV, vd)
    v_win = vd
    W("gp", sDV, vd)
    wstage.release()

    xth = Buf(ab, DC * TTL)
    xtl = Buf(ab, DC * TTL)
    evf = Buf(af, TTL)
    evh = Buf(ab, TTL)
    evl = Buf(ab, TTL)
    W("pe", sDV, v_win)
    W("sy", sG, spill_xt)
    vpe_tt = 0
    for tt in range(NTT):
        W("sy", sPE, vpe_tt)
        for dc in range(DC):
            DMA("sy", xth.ap(dc * TTL, TTL), hbm["sp_xt_hi"][dc, :, tt * TTL:(tt + 1) * TTL])
            vld = DMA("sy", xtl.ap(dc * TTL, TTL), hbm["sp_xt_lo"][dc, :, tt * TTL:(tt + 1) * TTL])
        W("pe", sW, vld)
        for mc in range(2 * DC):
            bank = mc % 2
            bank_wait(bank)
            vpe_tt = mm3(bank, KC,
                         lambda kc, mc=mc: (win_hi.ap(kc * 2 * D + mc * 128, 128),
                                            win_lo.ap(kc * 2 * D + mc * 128, 128)),
                         lambda kc: (xth.ap(kc * TTL, TTL), xtl.ap(kc * TTL, TTL)))
            W("ac", sPE, vpe_tt)
            if mc < DC:
                va = E("ac", lambda e, o=evf.ap(), i=psum[bank].ap()[:, 0:TTL]: e.copy(o, i), sAC)
                W("dv", sAC, va)
                E("dv", lambda e, o=evh.ap(), i=evf.ap(): e.tensor_copy(o, i))
                vd = E("dv", lambda e, o=evl.ap(), a=evf.ap(), b=evh.ap():
                       e.tensor_tensor(o, a, b, ALU.subtract), sDV)
                bank_free[bank] = (sAC, va)
                W("gp", sDV, vd)
                DMA("gp", hbm["sp_x_f32"][mc, :, tt * TTL:(tt + 1) * TTL], evf.ap())
                DMA("gp", hbm["sp_x_hi"][mc, :, tt * TTL:(tt + 1) * TTL], evh.ap())
                vg = DMA("gp", hbm["sp_x_lo"][mc, :, tt * TTL:(tt + 1) * TTL], evl.ap())
                W("ac", sG, vg)
                W("dv", sG, vg)
            else:
                va = E("ac", lambda e, o=evf.ap(), i=psum[bank].ap()[:, 0:TTL]:
                       e.activation(o, i, AF.Silu), sAC)
                bank_free[bank] = (sAC, va)
                W("gp", sAC, va)
                vg = DMA("gp", hbm["sp_sz"][mc - DC, :, tt * TTL:(tt + 1) * TTL], evf.ap())
                W("ac", sG, vg)
    win_hi.release()
    win_lo.release()
    spill_p1 = sG.n
    phase_barrier()

    MX = D + 2 * N
    wx_f32 = Buf(af, KC * MX)
    wx_hi = Buf(ab, KC * MX)
    wx_lo = Buf(ab, KC * MX)
    v_wx = load_split_weight("W_x", D, MX, wx_f32, wx_hi, wx_lo)
    wx_f32.release()
    bcf = Buf(af, T)
    W("pe", sDV, v_wx)
    W("sy", sG, spill_p1)
    vpe_tt = 0
    for tt in range(NTT):
        W("sy", sPE, vpe_tt)
        for dc in range(DC):
            DMA("sy", xth.ap(dc * TTL, TTL), hbm["sp_x_hi"][dc, :, tt * TTL:(tt + 1) * TTL])
            vld = DMA("sy", xtl.ap(dc * TTL, TTL), hbm["sp_x_lo"][dc, :, tt * TTL:(tt + 1) * TTL])
        W("pe", sW, vld)
        for mc in range(DC + 1):
            bank = mc % 2
            mw = 128 if mc < DC else 2 * N
            bank_wait(bank)
            vpe_tt = mm3(bank, KC,
                         lambda kc, mc=mc, mw=mw: (wx_hi.ap(kc * MX + mc * 128, mw),
                                                   wx_lo.ap(kc * MX + mc * 128, mw)),
                         lambda kc: (xth.ap(kc * TTL, TTL), xtl.ap(kc * TTL, TTL)),
                         mw=mw)
            W("ac", sPE, vpe_tt)
            if mc < DC:
                va = E("ac", lambda e, o=evh.ap(), i=psum[bank].ap()[:, 0:TTL]: e.copy(o, i), sAC)
                W("dv", sAC, va)
                W("dv", sPE, vpe_tt)
                vd = E("dv", lambda e, o=evl.ap(), a=psum[bank].ap()[:, 0:TTL], b=evh.ap():
                       e.tensor_tensor(o, a, b, ALU.subtract), sDV)
                bank_free[bank] = (sDV, vd)
                W("gp", sDV, vd)
                DMA("gp", hbm["sp_dr_hi"][mc, :, tt * TTL:(tt + 1) * TTL], evh.ap())
                vg = DMA("gp", hbm["sp_dr_lo"][mc, :, tt * TTL:(tt + 1) * TTL], evl.ap())
                W("ac", sG, vg)
                W("dv", sG, vg)
            else:
                va = E("ac", lambda e, o=bcf.ap(tt * TTL, TTL, rows=2 * N),
                       i=psum[bank].ap()[0:2 * N, 0:TTL]: e.copy(o, i), sAC)
                W("dv", sAC, va)
                E("dv", lambda e, o=sb["bch"].ap()[:, tt * TTL:(tt + 1) * TTL],
                  i=bcf.ap(tt * TTL, TTL, rows=2 * N): e.tensor_copy(o, i))
                vd = E("dv", lambda e, o=sb["bcl"].ap()[:, tt * TTL:(tt + 1) * TTL],
                       a=bcf.ap(tt * TTL, TTL, rows=2 * N),
                       b=sb["bch"].ap()[:, tt * TTL:(tt + 1) * TTL]:
                       e.tensor_tensor(o, a, b, ALU.subtract), sDV)
                bank_free[bank] = (sDV, vd)
    xth.release()
    xtl.release()
    wx_hi.release()
    wx_lo.release()
    bcf.release()
    bc_ready = sDV.n
    spill_p2 = sG.n
    phase_barrier()

    wdt_f32 = Buf(af, KC * D)
    wdt_hi = Buf(ab, KC * D)
    wdt_lo = Buf(ab, KC * D)
    v_wdt = load_split_weight("W_dt", D, D, wdt_f32, wdt_hi, wdt_lo)
    wdt_f32.release()

    drh = Buf(ab, DC * TTL)
    drl = Buf(ab, DC * TTL)
    delta = Buf(af, DC * TTL)
    dx = Buf(af, DC * TTL)
    y_tt = Buf(af, DC * TTL)
    s_tc = Buf(af, TTL)
    x_rl = Buf(af, TTL)
    e_tmp = Buf(af, TTL)
    dA_t = [Buf(af, TTL), Buf(af, TTL)]
    u_t = Buf(af, TTL)
    h_t = Buf(af, TTL)
    hc_t = Buf(af, TTL)

    W("pe", sDV, v_wdt)
    W("sy", sG, spill_p2)
    vpe_tt = 0
    vd_scan = [0, 0]
    for tt in range(NTT):
        ts0 = tt * TTL
        W("sy", sPE, vpe_tt)
        for dc in range(DC):
            DMA("sy", drh.ap(dc * TTL, TTL), hbm["sp_dr_hi"][dc, :, ts0:ts0 + TTL])
            vld = DMA("sy", drl.ap(dc * TTL, TTL), hbm["sp_dr_lo"][dc, :, ts0:ts0 + TTL])
        W("pe", sW, vld)
        for dc in range(DC):
            bank = dc % 2
            bank_wait(bank)
            vpe_tt = mm3(bank, KC,
                         lambda kc, dc=dc: (wdt_hi.ap(kc * D + dc * 128, 128),
                                            wdt_lo.ap(kc * D + dc * 128, 128)),
                         lambda kc: (drh.ap(kc * TTL, TTL), drl.ap(kc * TTL, TTL)))
            W("ac", sPE, vpe_tt)
            E("ac", lambda e, o=e_tmp.ap(), i=psum[bank].ap()[:, 0:TTL],
              b=sb["bdt_col"].ap()[:, dc:dc + 1]:
              e.activation(o, i, AF.Exp, bias=b))
            va = E("ac", lambda e, o=delta.ap(dc * TTL, TTL), i=e_tmp.ap():
                   e.activation(o, i, AF.Ln, bias=1.0), sAC)

            bank_free[bank] = (sAC, va)
            vx = DMA("sy", x_rl.ap(), hbm["sp_x_f32"][dc, :, ts0:ts0 + TTL])
            W("dv", sAC, va)
            W("dv", sW, vx)
            E("dv", lambda e, o=dx.ap(dc * TTL, TTL), a=delta.ap(dc * TTL, TTL),
              b=x_rl.ap(): e.tensor_tensor(o, a, b, ALU.mult))
            E("dv", lambda e, o=y_tt.ap(dc * TTL, TTL), a=x_rl.ap(),
              s=sb["dpar_col"].ap()[:, dc:dc + 1]:
              e.tensor_scalar(o, a, s, None, ALU.mult))
            vsc = E("dv", lambda e, o=s_tc.ap(), d0=delta.ap(dc * TTL, TTL),
                    z=sb["zeros"].ap(), ini=sb["scar"].ap()[:, dc:dc + 1]:
                    e.tensor_tensor_scan(o, d0, z, ini, ALU.add, ALU.add), sDV)
            W("dv", sDV, vsc)
            vd = E("dv", lambda e, o=sb["scar"].ap()[:, dc:dc + 1], i=s_tc.ap(TTL - 1, 1):
                   e.tensor_copy(o, i), sDV)
            W("sy", sDV, vd)
            W("gp", sDV, vd)
            vg = DMA("gp", hbm["sp_S"][dc, :, ts0:ts0 + TTL], s_tc.ap())
            W("dv", sG, vg)
        for n in range(N):
            pb, pc = 6, 7
            bank_wait(pb)
            bank_wait(pc)
            if tt == 0 and n == 0:
                W("pe", sDV, bc_ready)
            E("pe", lambda e, o=psum[pb].ap()[:, 0:TTL], a=sb["sel"].ap()[:, n * 128:(n + 1) * 128],
              b=sb["bch"].ap()[:, ts0:ts0 + TTL]: e.matmul(o, a, b, start=True, stop=False))
            E("pe", lambda e, o=psum[pb].ap()[:, 0:TTL], a=sb["sel"].ap()[:, n * 128:(n + 1) * 128],
              b=sb["bcl"].ap()[:, ts0:ts0 + TTL]: e.matmul(o, a, b, start=False, stop=True))
            E("pe", lambda e, o=psum[pc].ap()[:, 0:TTL], a=sb["sel"].ap()[:, (N + n) * 128:(N + n + 1) * 128],
              b=sb["bch"].ap()[:, ts0:ts0 + TTL]: e.matmul(o, a, b, start=True, stop=False))
            vbc = E("pe", lambda e, o=psum[pc].ap()[:, 0:TTL], a=sb["sel"].ap()[:, (N + n) * 128:(N + n + 1) * 128],
                    b=sb["bcl"].ap()[:, ts0:ts0 + TTL]: e.matmul(o, a, b, start=False, stop=True), sPE)
            for dc in range(DC):
                par = dc % 2
                W("ac", sDV, vd_scan[par])
                va = E("ac", lambda e, o=dA_t[par].ap(), i=delta.ap(dc * TTL, TTL),
                       s=sb["A"].ap()[:, dc * N + n:dc * N + n + 1]:
                       e.activation(o, i, AF.Exp, scale=s), sAC)
                W("dv", sPE, vbc)
                E("dv", lambda e, o=u_t.ap(), a=dx.ap(dc * TTL, TTL), b=psum[pb].ap()[:, 0:TTL]:
                  e.tensor_tensor(o, a, b, ALU.mult))
                W("dv", sAC, va)
                col = n * DC + dc
                vsc = E("dv", lambda e, o=h_t.ap(), d0=dA_t[par].ap(), d1=u_t.ap(),
                        ini=sb["state"].ap()[:, col:col + 1]:
                        e.tensor_tensor_scan(o, d0, d1, ini, ALU.mult, ALU.add), sDV)
                W("dv", sDV, vsc)
                E("dv", lambda e, o=sb["state"].ap()[:, col:col + 1], i=h_t.ap(TTL - 1, 1):
                  e.tensor_copy(o, i))
                E("dv", lambda e, o=hc_t.ap(), a=h_t.ap(), b=psum[pc].ap()[:, 0:TTL]:
                  e.tensor_tensor(o, a, b, ALU.mult))
                vd = E("dv", lambda e, o=y_tt.ap(dc * TTL, TTL), a=y_tt.ap(dc * TTL, TTL),
                       b=hc_t.ap(): e.tensor_tensor(o, a, b, ALU.add), sDV)
                vd_scan[par] = vd
            bank_free[pb] = (sDV, vd)
            bank_free[pc] = (sDV, vd)
        W("gp", sDV, vd)
        for dc in range(DC):
            vg = DMA("gp", hbm["sp_y"][dc, :, ts0:ts0 + TTL], y_tt.ap(dc * TTL, TTL))
        W("dv", sG, vg)
    drh.release()
    drl.release()
    wdt_hi.release()
    wdt_lo.release()
    delta.release()
    dx.release()
    s_tc.release()
    x_rl.release()
    e_tmp.release()
    u_t.release()
    h_t.release()
    hc_t.release()
    for b_ in dA_t:
        b_.release()
    y_tt.release()
    spill_y = sG.n
    phase_barrier()

    hfin = Buf(af, N * DC)
    phase_barrier()
    vd = E("dv", lambda e, o=hfin.ap(), a=sb["state"].ap(),
           s=sb["mask_col"].ap()[:, 0:1]: e.tensor_scalar(o, a, s, None, ALU.mult), sDV)
    W("gp", sDV, vd)
    vg = DMA("gp", cc_carry_in.ap(), hfin.ap())
    W("gp", sG, vg)
    vcc = E("gp", lambda e: e.collective_compute(
        "AllReduce", ALU.add,
        replica_groups=[[0, 1], [2, 3], [4, 5], [6, 7]],
        ins=[cc_carry_in.ap().opt()],
        outs=[cc_carry_out.ap().opt()]), sCC)
    W("gp", sCC, vcc)
    vg = DMA("gp", hfin.ap(), cc_carry_out.ap())
    W("dv", sG, vg)
    v_hin = E("dv", lambda e, o=hfin.ap(), a=hfin.ap(),
              s=sb["mask_col"].ap()[:, 1:2]: e.tensor_scalar(o, a, s, None, ALU.mult), sDV)

    y_c = Buf(af, DC * TTL)
    s_c = Buf(af, DC * TTL)
    p_t = Buf(af, TTL)
    cp_t = Buf(af, TTL)
    W("dv", sDV, v_hin)
    W("sy", sG, spill_y)
    va_p = 0
    for tt in range(NTT):
        ts0 = tt * TTL
        for dc in range(DC):
            DMA("sy", y_c.ap(dc * TTL, TTL), hbm["sp_y"][dc, :, ts0:ts0 + TTL])
            vld = DMA("sy", s_c.ap(dc * TTL, TTL), hbm["sp_S"][dc, :, ts0:ts0 + TTL])
        W("ac", sW, vld)
        W("dv", sW, vld)
        for n in range(N):
            pc = 6 if n % 2 == 0 else 7
            bank_wait(pc)
            E("pe", lambda e, o=psum[pc].ap()[:, 0:TTL], a=sb["sel"].ap()[:, (N + n) * 128:(N + n + 1) * 128],
              b=sb["bch"].ap()[:, ts0:ts0 + TTL]: e.matmul(o, a, b, start=True, stop=False))
            vbc = E("pe", lambda e, o=psum[pc].ap()[:, 0:TTL], a=sb["sel"].ap()[:, (N + n) * 128:(N + n + 1) * 128],
                    b=sb["bcl"].ap()[:, ts0:ts0 + TTL]: e.matmul(o, a, b, start=False, stop=True), sPE)
            for dc in range(DC):
                W("ac", sDV, va_p)
                va = E("ac", lambda e, o=p_t.ap(), i=s_c.ap(dc * TTL, TTL),
                       s=sb["A"].ap()[:, dc * N + n:dc * N + n + 1]:
                       e.activation(o, i, AF.Exp, scale=s), sAC)
                W("dv", sAC, va)
                W("dv", sPE, vbc)
                E("dv", lambda e, o=cp_t.ap(), a=p_t.ap(), b=psum[pc].ap()[:, 0:TTL]:
                  e.tensor_tensor(o, a, b, ALU.mult))
                col = n * DC + dc
                vd = E("dv", lambda e, o=y_c.ap(dc * TTL, TTL), a=cp_t.ap(),
                       s=hfin.ap()[:, col:col + 1], b=y_c.ap(dc * TTL, TTL):
                       e.scalar_tensor_tensor(o, a, s, b, ALU.mult, ALU.add), sDV)
                va_p = vd
            bank_free[pc] = (sDV, vd)
        W("gp", sDV, vd)
        for dc in range(DC):
            vg = DMA("gp", hbm["sp_y"][dc, :, ts0:ts0 + TTL], y_c.ap(dc * TTL, TTL))
        W("sy", sG, vg)
    y_c.release()
    s_c.release()
    p_t.release()
    cp_t.release()
    hfin.release()
    spill_y2 = sG.n
    phase_barrier()

    phase_barrier()
    wo_f32 = Buf(af, KC * D)
    wo_hi = Buf(ab, KC * D)
    wo_lo = Buf(ab, KC * D)
    v_wo = load_split_weight("W_out", D, D, wo_f32, wo_hi, wo_lo)
    wo_f32.release()

    y2h = Buf(ab, DC * TTL)
    y2l = Buf(ab, DC * TTL)
    y2f = Buf(af, TTL)
    sz_rl = Buf(af, TTL)
    xt_rl = Buf(af, DC * TTL)
    res1 = Buf(af, DC * TTL)
    x1_t = Buf(af, DC * TTL)
    sq_t = [Buf(af, TTL), Buf(af, TTL)]

    def emit_ln(src_fn, out_fn, affine, src_ready=0):
        nonlocal va_sq, vpe_sq
        bank_wait(5)
        bank_wait(3)
        W("ac", sDV, src_ready)
        for dc in range(DC):
            par = dc % 2
            W("ac", sPE, vpe_sq[par])
            va = E("ac", lambda e, o=sq_t[par].ap(), i=src_fn(dc):
                   e.activation(o, i, AF.Square), sAC)
            W("pe", sAC, va)
            E("pe", lambda e, o=psum[5].ap()[0:1, 0:TTL], a=sb["ones1"].ap()[:, 0:1],
              b=src_fn(dc), s=(dc == 0), st=(dc == DC - 1): e.matmul(o, a, b, start=s, stop=st))
            vpe = E("pe", lambda e, o=psum[3].ap()[0:1, 0:TTL], a=sb["ones1"].ap()[:, 0:1],
                    b=sq_t[par].ap(), s=(dc == 0), st=(dc == DC - 1):
                    e.matmul(o, a, b, start=s, stop=st), sPE)
            vpe_sq[par] = vpe
        W("dv", sPE, vpe)
        L = sb["lns"].ap()
        E("dv", lambda e, o=L[0:1, 0:TTL], i=psum[5].ap()[0:1, 0:TTL]:
          e.tensor_scalar(o, i, 1.0 / D, None, ALU.mult))
        E("dv", lambda e, o=L[0:1, TTL:2 * TTL], i=psum[3].ap()[0:1, 0:TTL]:
          e.tensor_scalar(o, i, 1.0 / D, None, ALU.mult))
        E("dv", lambda e, o=L[0:1, 2 * TTL:3 * TTL], a=L[0:1, 0:TTL], b=L[0:1, 0:TTL]:
          e.tensor_tensor(o, a, b, ALU.mult))
        E("dv", lambda e, o=L[0:1, TTL:2 * TTL], a=L[0:1, TTL:2 * TTL],
          b=L[0:1, 2 * TTL:3 * TTL]: e.tensor_tensor(o, a, b, ALU.subtract))
        vd = E("dv", lambda e, o=L[0:1, TTL:2 * TTL], i=L[0:1, TTL:2 * TTL]:
               e.tensor_scalar(o, i, LN_EPS, None, ALU.add), sDV)
        W("ac", sDV, vd)
        va = E("ac", lambda e, o=L[0:1, TTL:2 * TTL], i=L[0:1, TTL:2 * TTL]:
               e.activation(o, i, AF.Sqrt), sAC)
        W("dv", sAC, va)
        vf = E("dv", lambda e, o=L[0:1, 2 * TTL:3 * TTL], i=L[0:1, TTL:2 * TTL]:
               e.reciprocal(o, i), sDV)
        W("dv", sDV, vf)
        vd = E("dv", lambda e, o=L[0:1, 3 * TTL:4 * TTL], a=L[0:1, 0:TTL],
               b=L[0:1, 2 * TTL:3 * TTL]: e.tensor_tensor(o, a, b, ALU.mult), sDV)
        bank_wait(4)
        W("pe", sDV, vd)
        E("pe", lambda e, o=psum[4].ap()[:, 0:TTL], a=sb["ones1"].ap()[0:1, :],
          b=L[0:1, 2 * TTL:3 * TTL]: e.matmul(o, a, b, start=True, stop=True))
        vpe = E("pe", lambda e, o=psum[3].ap()[:, 0:TTL], a=sb["ones1"].ap()[0:1, :],
                b=L[0:1, 3 * TTL:4 * TTL]: e.matmul(o, a, b, start=True, stop=True), sPE)
        W("dv", sPE, vpe)
        for dc in range(DC):
            E("dv", lambda e, o=out_fn(dc), a=src_fn(dc), b=psum[4].ap()[:, 0:TTL]:
              e.tensor_tensor(o, a, b, ALU.mult))
            E("dv", lambda e, o=out_fn(dc), a=out_fn(dc), b=psum[3].ap()[:, 0:TTL]:
              e.tensor_tensor(o, a, b, ALU.subtract))
            if affine:
                vd = E("dv", lambda e, o=out_fn(dc), i=out_fn(dc),
                       g=sb["lng_col"].ap()[:, dc:dc + 1], bb=sb["lnb_col"].ap()[:, dc:dc + 1]:
                       e.tensor_scalar(o, i, g, bb, ALU.mult, ALU.add), sDV)
            else:
                vd = sDV.n
        if not affine:
            vd = E("dv", lambda e: e.engine_nop(), sDV)
        bank_free[3] = (sDV, vd)
        bank_free[4] = (sDV, vd)
        bank_free[5] = (sDV, vd)
        return vd

    va_sq = 0
    vpe_sq = [0, 0]
    W("pe", sDV, v_wo)
    W("sy", sG, spill_y2)
    vpe_tt = 0
    for tt in range(NTT):
        ts0 = tt * TTL
        W("sy", sPE, vpe_tt)
        W("sy", sDV, sDV.n)
        for dc in range(DC):
            DMA("sy", xt_rl.ap(dc * TTL, TTL), hbm["sp_xt_f32"][dc, :, ts0:ts0 + TTL])
        for dc in range(DC):
            vy = DMA("sy", y2f.ap(), hbm["sp_y"][dc, :, ts0:ts0 + TTL])
            vsz = DMA("sy", sz_rl.ap(), hbm["sp_sz"][dc, :, ts0:ts0 + TTL])
            W("dv", sW, vsz)
            E("dv", lambda e, o=y2f.ap(), a=y2f.ap(), b=sz_rl.ap():
              e.tensor_tensor(o, a, b, ALU.mult))
            E("dv", lambda e, o=y2h.ap(dc * TTL, TTL), i=y2f.ap(): e.tensor_copy(o, i))
            vd = E("dv", lambda e, o=y2l.ap(dc * TTL, TTL), a=y2f.ap(),
                   b=y2h.ap(dc * TTL, TTL): e.tensor_tensor(o, a, b, ALU.subtract), sDV)
            W("sy", sDV, vd)
        W("pe", sDV, vd)
        vldx = sW.n
        for mc in range(DC):
            bank = mc % 2
            bank_wait(bank)
            vpe_tt = mm3(bank, KC,
                         lambda kc, mc=mc: (wo_hi.ap(kc * D + mc * 128, 128),
                                            wo_lo.ap(kc * D + mc * 128, 128)),
                         lambda kc: (y2h.ap(kc * TTL, TTL), y2l.ap(kc * TTL, TTL)))
            W("dv", sPE, vpe_tt)
            W("dv", sW, vldx)
            vd = E("dv", lambda e, o=res1.ap(mc * TTL, TTL), a=psum[bank].ap()[:, 0:TTL],
                   b=xt_rl.ap(mc * TTL, TTL): e.tensor_tensor(o, a, b, ALU.add), sDV)
            bank_free[bank] = (sDV, vd)
        emit_ln(lambda dc: res1.ap(dc * TTL, TTL), lambda dc: res1.ap(dc * TTL, TTL),
                affine=True, src_ready=vd)
        for dc in range(DC):
            vd = E("dv", lambda e, o=res1.ap(dc * TTL, TTL), a=res1.ap(dc * TTL, TTL),
                   b=xt_rl.ap(dc * TTL, TTL): e.tensor_tensor(o, a, b, ALU.add), sDV)
        emit_ln(lambda dc: res1.ap(dc * TTL, TTL), lambda dc: x1_t.ap(dc * TTL, TTL),
                affine=False, src_ready=vd)
        vd = sDV.n
        W("gp", sDV, vd)
        for dc in range(DC):
            vg = DMA("gp", hbm["sp_x1"][dc, :, ts0:ts0 + TTL], x1_t.ap(dc * TTL, TTL))
        W("dv", sG, vg)
    wo_hi.release()
    wo_lo.release()
    y2h.release()
    y2l.release()
    y2f.release()
    sz_rl.release()
    xt_rl.release()
    res1.release()
    x1_t.release()
    for b_ in sq_t:
        b_.release()
    spill_x1 = sG.n
    phase_barrier()

    g_rl = Buf(af, TTL)

    def absmax_init(col):
        E("dv", lambda e, o=sb["red"].ap()[:, col:col + 1]: e.memset(o, 0.0))

    def absmax_acc(src_ap, col):
        vf = E("dv", lambda e, o=sb["redt"].ap(), i=src_ap:
               e.tensor_reduce(o, i, AX.X, ALU.max, apply_absolute_value=True), sDV)
        W("dv", sDV, vf)
        E("dv", lambda e, o=sb["red"].ap()[:, col:col + 1], a=sb["red"].ap()[:, col:col + 1],
          b=sb["redt"].ap(): e.tensor_tensor(o, a, b, ALU.max))

    def gmax_finish(col, slot):
        bank_wait(2)
        vd = E("dv", lambda e: e.engine_nop(), sDV)
        W("pe", sDV, vd)
        vpe = E("pe", lambda e, o=psum[2].ap()[0:1, 0:128], i=sb["red"].ap()[:, col:col + 1],
                idn=sb["ident"].ap(): e.transpose(o, i, idn), sPE)
        W("dv", sPE, vpe)
        vd = E("dv", lambda e, o=sb["qs"].ap()[0:1, slot:slot + 1], i=psum[2].ap()[0:1, 0:128]:
               e.tensor_reduce(o, i, AX.X, ALU.max), sDV)
        W("dv", sDV, vd)
        bank_free[2] = (sDV, vd)
        return vd

    def coll_max():
        vd = sDV.n
        W("gp", sDV, vd)
        vg = DMA("gp", cc_max_in.ap(), sb["qs"].ap())
        W("gp", sG, vg)
        vcc = E("gp", lambda e: e.collective_compute(
            "AllReduce", ALU.max, replica_groups=[list(range(8))],
            ins=[cc_max_in.ap().opt()], outs=[cc_max_out.ap().opt()]), sCC)
        W("gp", sCC, vcc)
        return DMA("gp", sb["qs"].ap(), cc_max_out.ap())

    def scale_calc(slot_max, slot_s, slot_r, qmax, vbi):
        W("dv", sG, vbi)
        Q = sb["qs"].ap()
        vf = E("dv", lambda e, o=Q[0:1, slot_s:slot_s + 1], i=Q[0:1, slot_max:slot_max + 1]:
               e.tensor_scalar(o, i, 1.0 / qmax, 1e-12, ALU.mult, ALU.max), sDV)
        W("dv", sDV, vf)
        vr = E("dv", lambda e, o=Q[0:1, slot_r:slot_r + 1], i=Q[0:1, slot_s:slot_s + 1]:
               e.reciprocal(o, i), sDV)
        W("dv", sDV, vr)
        return vr

    def col_bcast(slot, col, vd):
        bank_wait(2)
        W("pe", sDV, vd)
        vpe = E("pe", lambda e, o=psum[2].ap()[:, 0:1], a=sb["ones1"].ap()[0:1, :],
                b=sb["qs"].ap()[0:1, slot:slot + 1]:
                e.matmul(o, a, b, start=True, stop=True), sPE)
        W("dv", sPE, vpe)
        vd2 = E("dv", lambda e, o=sb["qcol"].ap()[:, col:col + 1], i=psum[2].ap()[:, 0:1]:
                e.tensor_copy(o, i), sDV)
        bank_free[2] = (sDV, vd2)
        return vd2

    absmax_init(0)
    W("sy", sG, spill_x1)
    for dc in range(DC):
        for tt in range(NTT):
            vld = DMA("sy", g_rl.ap(), hbm["sp_x1"][dc, :, tt * TTL:(tt + 1) * TTL])
            W("dv", sW, vld)
            absmax_acc(g_rl.ap(), 0)
            vd = E("dv", lambda e: e.engine_nop(), sDV)
            W("sy", sDV, vd)
    gmax_finish(0, 0)
    vbi = coll_max()
    vd = scale_calc(0, 1, 2, 32767.0, vbi)
    vd = col_bcast(1, 0, vd)
    vd = col_bcast(2, 1, vd)
    scales_x1 = vd

    W("gp", sDV, scales_x1)
    DMA("gp", dbg_sc[0, :, 0:8], sb["qcol"].ap())
    DMA("gp", dbg_sc[0, 0:1, 8:24], sb["qs"].ap())

    xq_hi = Buf(ab, DC * T)
    xq_lo = Buf(ab, DC * T)
    t_t = Buf(af, TTL)
    q_t = Buf(af, TTL)
    va_q = 0
    for dc in range(DC):
        for tt in range(NTT):
            vld = DMA("sy", g_rl.ap(), hbm["sp_x1"][dc, :, tt * TTL:(tt + 1) * TTL])
            W("ac", sW, vld)
            W("ac", sDV, scales_x1)
            W("ac", sDV, va_q)
            va = E("ac", lambda e, o=t_t.ap(), i=g_rl.ap(), r=sb["qcol"].ap()[:, 1:2],
                   c=sb["rndc"].ap()[:, 0:1]:
                   e.activation(o, i, AF.Identity, bias=c, scale=r), sAC)
            W("dv", sAC, va)
            E("dv", lambda e, o=q_t.ap(), i=t_t.ap():
              e.tensor_scalar(o, i, -RND_C, 32767.0, ALU.add, ALU.min))
            E("dv", lambda e, o=q_t.ap(), i=q_t.ap():
              e.tensor_scalar(o, i, -32768.0, None, ALU.max))
            E("dv", lambda e, o=xq_hi.ap(dc * T + tt * TTL, TTL), i=q_t.ap(),
              s=sb["qcol"].ap()[:, 0:1]: e.tensor_scalar(o, i, s, None, ALU.mult))
            vq = E("dv", lambda e, o=xq_lo.ap(dc * T + tt * TTL, TTL), a=q_t.ap(),
                   s=sb["qcol"].ap()[:, 0:1], b=xq_hi.ap(dc * T + tt * TTL, TTL):
                   e.scalar_tensor_tensor(o, a, s, b, ALU.mult, ALU.subtract), sDV)
            va_q = vq
            W("sy", sAC, va)
    xq_done = sDV.n
    W("gp", sDV, xq_done)
    DMA("gp", dbg_xq[0], xq_hi.ap())
    DMA("gp", dbg_xq[1], xq_lo.ap())
    phase_barrier()

    wrow = Buf(af, 1024)
    wu_s = Buf(af, 1024)
    vd_wprep = 0
    vpe_wt = 0

    def ffn_wprep_section(wname, oc, sec, dst_hi, dst_lo, dst_off_fn):
        nonlocal vd_wprep, vpe_wt
        nb = 1024 // 32
        W("sy", sDV, vd_wprep)
        v = DMA("sy", wrow.ap(), hbm[wname][oc * 128:(oc + 1) * 128,
                                            sec * 1024:(sec + 1) * 1024])
        W("dv", sW, v)
        vf = E("dv", lambda e, o=sb["wbs"].ap()[:, 0:nb].unsqueeze(2),
               i=wrow.ap().rearrange("p (b i) -> p b i", b=nb):
               e.tensor_reduce(o, i, AX.X, ALU.max, apply_absolute_value=True), sDV)
        W("dv", sDV, vf)
        vf = E("dv", lambda e, o=sb["wbs"].ap()[:, 0:nb], i=sb["wbs"].ap()[:, 0:nb]:
               e.tensor_scalar(o, i, 1.0 / 7.0, 1e-12, ALU.mult, ALU.max), sDV)
        W("dv", sDV, vf)
        vf = E("dv", lambda e, o=sb["wbr"].ap()[:, 0:nb], i=sb["wbs"].ap()[:, 0:nb]:
               e.reciprocal(o, i), sDV)
        W("dv", sDV, vf)
        W("dv", sPE, vpe_wt)
        vd = E("dv", lambda e, o=wu_s.ap().rearrange("p (b i) -> p b i", b=nb),
               a=wrow.ap().rearrange("p (b i) -> p b i", b=nb),
               b=sb["wbr"].ap()[:, 0:nb].unsqueeze(2).broadcast_to([128, nb, 32]):
               e.tensor_tensor(o, a, b, ALU.mult), sDV)
        vd_wprep = vd
        W("ac", sDV, vd)
        va = E("ac", lambda e, o=wu_s.ap(), i=wu_s.ap(), c=sb["rndc"].ap()[:, 0:1]:
               e.activation(o, i, AF.Identity, bias=c), sAC)
        W("dv", sAC, va)
        E("dv", lambda e, o=wu_s.ap(), i=wu_s.ap():
          e.tensor_scalar(o, i, -RND_C, 7.0, ALU.add, ALU.min))
        E("dv", lambda e, o=wu_s.ap(), i=wu_s.ap():
          e.tensor_scalar(o, i, -8.0, None, ALU.max))
        vd = E("dv", lambda e, o=wu_s.ap().rearrange("p (b i) -> p b i", b=nb),
               a=wu_s.ap().rearrange("p (b i) -> p b i", b=nb),
               b=sb["wbs"].ap()[:, 0:nb].unsqueeze(2).broadcast_to([128, nb, 32]):
               e.tensor_tensor(o, a, b, ALU.mult), sDV)
        W("pe", sDV, vd)
        for kcl in range(8):
            bank = 2 + (kcl % 2)
            bank_wait(bank)
            vpe = E("pe", lambda e, o=psum[bank].ap()[:, 0:128],
                    i=wu_s.ap(kcl * 128, 128), idn=sb["ident"].ap():
                    e.transpose(o, i, idn), sPE)
            W("ac", sPE, vpe)
            va = E("ac", lambda e, o=dst_hi.ap(dst_off_fn(kcl), 128),
                   i=psum[bank].ap()[:, 0:128]: e.copy(o, i), sAC)
            W("dv", sAC, va)
            vdl = E("dv", lambda e, o=dst_lo.ap(dst_off_fn(kcl), 128),
                    a=psum[bank].ap()[:, 0:128], b=dst_hi.ap(dst_off_fn(kcl), 128):
                    e.tensor_tensor(o, a, b, ALU.subtract), sDV)
            bank_free[bank] = (sDV, vdl)
        vpe_wt = vpe
        return vdl

    wqh = Buf(ab, KC * 128)
    wql = Buf(ab, KC * 128)
    up_ev = Buf(af, TTL)
    absmax_init(1)
    absmax_init(2)
    vpe_b = 0
    for wname, spname, rcol in (("w_up", "sp_up", 1), ("w_gate", "sp_gate", 2)):
        for oc in range(IC):
            vprep = ffn_wprep_section(wname, oc, 0, wqh, wql, lambda kcl: kcl * 128)
            if wname == "w_up" and oc == 0:
                W("gp", sDV, vprep)
                DMA("gp", dbg_wq[0], wqh.ap())
                vgw = DMA("gp", dbg_wq[1], wql.ap())
                W("pe", sG, vgw)
            W("pe", sDV, vprep)
            if oc == 0:
                W("pe", sDV, xq_done)
            for tt in range(NTT):
                bank = tt % 2
                bank_wait(bank)
                vpe_b = mm3(bank, KC,
                            lambda kc: (wqh.ap(kc * 128, 128), wql.ap(kc * 128, 128)),
                            lambda kc, tt=tt: (xq_hi.ap(kc * T + tt * TTL, TTL),
                                               xq_lo.ap(kc * T + tt * TTL, TTL)))
                W("ac", sPE, vpe_b)
                va2 = E("ac", lambda e, o=up_ev.ap(), i=psum[bank].ap()[:, 0:TTL]: e.copy(o, i), sAC)
                W("dv", sAC, va2)
                absmax_acc(up_ev.ap(), rcol)
                vd = E("dv", lambda e: e.engine_nop(), sDV)
                bank_free[bank] = (sDV, vd)
                W("gp", sDV, vd)
                vg = DMA("gp", hbm[spname][oc, :, tt * TTL:(tt + 1) * TTL], up_ev.ap())
                W("ac", sG, vg)
    xq_hi.release()
    xq_lo.release()
    gmax_finish(1, 3)
    gmax_finish(2, 4)
    vbi = coll_max()
    vd = scale_calc(3, 5, 6, 32767.0, vbi)
    vd = scale_calc(4, 7, 8, 32767.0, vd * 0 + vbi)
    vd = col_bcast(5, 2, vd)
    vd = col_bcast(6, 3, vd)
    vd = col_bcast(7, 4, vd)
    vd = col_bcast(8, 5, vd)
    scales_ug = vd

    W("gp", sDV, scales_ug)
    DMA("gp", dbg_sc[1, :, 0:8], sb["qcol"].ap())
    DMA("gp", dbg_sc[1, 0:1, 8:24], sb["qs"].ap())
    phase_barrier()

    for oc in range(DC):
        for sec in range(3):
            vprep = ffn_wprep_section(
                "w_down", oc, sec, wqh, wql,
                lambda kcl: kcl * 128)
            W("gp", sDV, vprep)
            DMA("gp", hbm["sp_wdn_hi"].rearrange("c p m -> p c m")[:, sec * 8:(sec + 1) * 8, oc * 128:(oc + 1) * 128],
                wqh.ap().rearrange("p (k m) -> p k m", k=8))
            vg = DMA("gp", hbm["sp_wdn_lo"].rearrange("c p m -> p c m")[:, sec * 8:(sec + 1) * 8, oc * 128:(oc + 1) * 128],
                     wql.ap().rearrange("p (k m) -> p k m", k=8))
            W("ac", sG, vg)
    wdn_spilled = sG.n
    phase_barrier()

    phase_barrier()
    sig_t = Buf(af, TTL)
    pg_t = Buf(af, TTL)
    absmax_init(3)
    vd_d = 0
    for oc in range(IC):
        for tt in range(NTT):
            vld = DMA("sy", g_rl.ap(), hbm["sp_gate"][oc, :, tt * TTL:(tt + 1) * TTL])
            W("ac", sW, vld)
            if oc == 0 and tt == 0:
                W("ac", sDV, scales_ug)
            W("ac", sDV, vd_d)
            va = E("ac", lambda e, o=t_t.ap(), i=g_rl.ap(), r=sb["qcol"].ap()[:, 5:6],
                   c=sb["rndc"].ap()[:, 0:1]:
                   e.activation(o, i, AF.Identity, bias=c, scale=r), sAC)
            W("dv", sAC, va)
            E("dv", lambda e, o=q_t.ap(), i=t_t.ap():
              e.tensor_scalar(o, i, -RND_C, 32767.0, ALU.add, ALU.min))
            vd = E("dv", lambda e, o=q_t.ap(), i=q_t.ap():
                   e.tensor_scalar(o, i, -32768.0, None, ALU.max), sDV)
            W("ac", sDV, vd)
            E("ac", lambda e, o=sig_t.ap(), i=q_t.ap(), s=sb["qcol"].ap()[:, 4:5]:
              e.activation(o, i, AF.Sigmoid, scale=s))
            va = E("ac", lambda e, o=sig_t.ap(), i=sig_t.ap(), c=sb["rndc"].ap()[:, 0:1]:
                   e.activation(o, i, AF.Identity, bias=c, scale=65536.0), sAC)
            W("dv", sAC, va)
            E("dv", lambda e, o=sig_t.ap(), i=sig_t.ap():
              e.tensor_scalar(o, i, -RND_C, 65535.0, ALU.add, ALU.min))
            E("dv", lambda e, o=sig_t.ap(), i=sig_t.ap():
              e.tensor_scalar(o, i, 0.0, None, ALU.max))
            E("dv", lambda e, o=pg_t.ap(), a=q_t.ap(), b=sig_t.ap():
              e.tensor_tensor(o, a, b, ALU.mult))
            absmax_acc(pg_t.ap(), 3)
            vd_d = E("dv", lambda e: e.engine_nop(), sDV)
            W("gp", sDV, vd_d)
            vg = DMA("gp", hbm["sp_pg"][oc, :, tt * TTL:(tt + 1) * TTL], pg_t.ap())
            W("dv", sG, vg)
            W("sy", sAC, va)
    vgm = gmax_finish(3, 9)
    W("gp", sDV, vgm)
    DMA("gp", dbg_rd[0, :, 0:8], sb["red"].ap())
    vdbg = DMA("gp", dbg_rd[0, 0:1, 8:24], sb["qs"].ap())
    W("gp", sG, vdbg)
    vbi = coll_max()
    W("gp", sCC, sCC.n)
    DMA("gp", dbg_rd[1, 0:1, 8:24], sb["qs"].ap())
    Q = sb["qs"].ap()
    W("dv", sG, vbi)
    for fn in [
        lambda e, o=Q[0:1, 10:11], i=Q[0:1, 7:8]:
            e.tensor_scalar(o, i, 1.0 / 65536.0, None, ALU.mult),
        lambda e, o=Q[0:1, 11:12], a=Q[0:1, 9:10], b=Q[0:1, 10:11]:
            e.tensor_tensor(o, a, b, ALU.mult),
        lambda e, o=Q[0:1, 11:12], i=Q[0:1, 11:12]:
            e.tensor_scalar(o, i, 1.0 / 32767.0, 1e-12, ALU.mult, ALU.max),
        lambda e, o=Q[0:1, 12:13], i=Q[0:1, 11:12]: e.reciprocal(o, i),
        lambda e, o=Q[0:1, 12:13], a=Q[0:1, 10:11], b=Q[0:1, 12:13]:
            e.tensor_tensor(o, a, b, ALU.mult),
    ]:
        vd = E("dv", fn, sDV)
        W("dv", sDV, vd)
    vd = col_bcast(12, 6, vd)
    scale2_done = vd

    W("gp", sDV, scale2_done)
    DMA("gp", dbg_sc[2, :, 0:8], sb["qcol"].ap())
    DMA("gp", dbg_sc[2, 0:1, 8:24], sb["qs"].ap())
    phase_barrier()

    phase_barrier()
    pou = Buf(af, TTL)
    qu_t = Buf(af, TTL)
    absmax_init(4)
    vd_f = 0
    va_f = 0
    for oc in range(IC):
        for tt in range(NTT):
            W("sy", sDV, vd_f)
            W("sy", sAC, va_f)
            vld1 = DMA("sy", g_rl.ap(), hbm["sp_pg"][oc, :, tt * TTL:(tt + 1) * TTL])
            vld2 = DMA("sy", pou.ap(), hbm["sp_up"][oc, :, tt * TTL:(tt + 1) * TTL])
            W("ac", sW, vld2)
            if oc == 0 and tt == 0:
                W("ac", sDV, scale2_done)
            W("ac", sDV, vd_f)
            va = E("ac", lambda e, o=t_t.ap(), i=g_rl.ap(), r=sb["qcol"].ap()[:, 6:7],
                   c=sb["rndc"].ap()[:, 0:1]:
                   e.activation(o, i, AF.Identity, bias=c, scale=r), sAC)
            W("dv", sAC, va)
            vtt = E("dv", lambda e, o=q_t.ap(), i=t_t.ap():
                    e.tensor_scalar(o, i, -RND_C, 32767.0, ALU.add, ALU.min), sDV)
            E("dv", lambda e, o=q_t.ap(), i=q_t.ap():
              e.tensor_scalar(o, i, -32768.0, None, ALU.max))
            W("ac", sDV, vtt)
            va_f = E("ac", lambda e, o=t_t.ap(), i=pou.ap(), r=sb["qcol"].ap()[:, 3:4],
                     c=sb["rndc"].ap()[:, 0:1]:
                     e.activation(o, i, AF.Identity, bias=c, scale=r), sAC)
            W("dv", sAC, va_f)
            E("dv", lambda e, o=qu_t.ap(), i=t_t.ap():
              e.tensor_scalar(o, i, -RND_C, 32767.0, ALU.add, ALU.min))
            E("dv", lambda e, o=qu_t.ap(), i=qu_t.ap():
              e.tensor_scalar(o, i, -32768.0, None, ALU.max))
            E("dv", lambda e, o=pou.ap(), a=q_t.ap(), b=qu_t.ap():
              e.tensor_tensor(o, a, b, ALU.mult))
            absmax_acc(pou.ap(), 4)
            vd_f = E("dv", lambda e: e.engine_nop(), sDV)
            W("gp", sDV, vd_f)
            vg = DMA("gp", hbm["sp_po"][oc, :, tt * TTL:(tt + 1) * TTL], pou.ap())
            W("dv", sG, vg)
            W("sy", sG, vg)
    gmax_finish(4, 13)
    vbi = coll_max()
    W("dv", sG, vbi)
    for fn in [
        lambda e, o=Q[0:1, 14:15], a=Q[0:1, 11:12], b=Q[0:1, 5:6]:
            e.tensor_tensor(o, a, b, ALU.mult),
        lambda e, o=Q[0:1, 15:16], a=Q[0:1, 13:14], b=Q[0:1, 14:15]:
            e.tensor_tensor(o, a, b, ALU.mult),
        lambda e, o=Q[0:1, 15:16], i=Q[0:1, 15:16]:
            e.tensor_scalar(o, i, 1.0 / 32767.0, 1e-12, ALU.mult, ALU.max),
        lambda e, o=Q[0:1, 0:1], i=Q[0:1, 15:16]: e.reciprocal(o, i),
        lambda e, o=Q[0:1, 0:1], a=Q[0:1, 14:15], b=Q[0:1, 0:1]:
            e.tensor_tensor(o, a, b, ALU.mult),
    ]:
        vd = E("dv", fn, sDV)
        W("dv", sDV, vd)
    vd = col_bcast(0, 7, vd)
    vd = col_bcast(15, 1, vd)
    scaleo_done = vd

    W("gp", sDV, scaleo_done)
    DMA("gp", dbg_sc[3, :, 0:8], sb["qcol"].ap())
    DMA("gp", dbg_sc[3, 0:1, 8:24], sb["qs"].ap())
    phase_barrier()

    phase_barrier()
    TH = min(256, T)
    NTH = T // TH
    oq_hi = Buf(ab, IC * TH)
    oq_lo = Buf(ab, IC * TH)
    wdnb_hi = [Buf(ab, IC * 128), Buf(ab, IC * 128)]
    wdnb_lo = [Buf(ab, IC * 128), Buf(ab, IC * 128)]
    vpe_wdnb = [0, 0]
    res3 = Buf(af, DC * TH)
    x2_t = Buf(af, DC * TH)
    x1row = Buf(af, TH)
    orow = Buf(af, D)
    sqh = [Buf(af, TH), Buf(af, TH)]

    LS = max(TTL, 256)

    def emit_ln_h(src_fn, out_fn, src_ready=0):
        nonlocal vpe_sqh
        bank_wait(5)
        bank_wait(3)
        W("ac", sDV, src_ready)
        for dc in range(DC):
            par = dc % 2
            W("ac", sPE, vpe_sqh[par])
            va = E("ac", lambda e, o=sqh[par].ap(), i=src_fn(dc):
                   e.activation(o, i, AF.Square), sAC)
            W("pe", sAC, va)
            E("pe", lambda e, o=psum[5].ap()[0:1, 0:TH], a=sb["ones1"].ap()[:, 0:1],
              b=src_fn(dc), s=(dc == 0), st=(dc == DC - 1): e.matmul(o, a, b, start=s, stop=st))
            vpe = E("pe", lambda e, o=psum[3].ap()[0:1, 0:TH], a=sb["ones1"].ap()[:, 0:1],
                    b=sqh[par].ap(), s=(dc == 0), st=(dc == DC - 1):
                    e.matmul(o, a, b, start=s, stop=st), sPE)
            vpe_sqh[par] = vpe
        W("dv", sPE, vpe)
        L = sb["lns"].ap()
        E("dv", lambda e, o=L[0:1, 0:TH], i=psum[5].ap()[0:1, 0:TH]:
          e.tensor_scalar(o, i, 1.0 / D, None, ALU.mult))
        E("dv", lambda e, o=L[0:1, LS:LS + TH], i=psum[3].ap()[0:1, 0:TH]:
          e.tensor_scalar(o, i, 1.0 / D, None, ALU.mult))
        E("dv", lambda e, o=L[0:1, 2 * LS:2 * LS + TH], a=L[0:1, 0:TH], b=L[0:1, 0:TH]:
          e.tensor_tensor(o, a, b, ALU.mult))
        E("dv", lambda e, o=L[0:1, LS:LS + TH], a=L[0:1, LS:LS + TH],
          b=L[0:1, 2 * LS:2 * LS + TH]: e.tensor_tensor(o, a, b, ALU.subtract))
        vd = E("dv", lambda e, o=L[0:1, LS:LS + TH], i=L[0:1, LS:LS + TH]:
               e.tensor_scalar(o, i, LN_EPS, None, ALU.add), sDV)
        W("ac", sDV, vd)
        va = E("ac", lambda e, o=L[0:1, LS:LS + TH], i=L[0:1, LS:LS + TH]:
               e.activation(o, i, AF.Sqrt), sAC)
        W("dv", sAC, va)
        vf = E("dv", lambda e, o=L[0:1, 2 * LS:2 * LS + TH], i=L[0:1, LS:LS + TH]:
               e.reciprocal(o, i), sDV)
        W("dv", sDV, vf)
        vd = E("dv", lambda e, o=L[0:1, 3 * LS:3 * LS + TH], a=L[0:1, 0:TH],
               b=L[0:1, 2 * LS:2 * LS + TH]: e.tensor_tensor(o, a, b, ALU.mult), sDV)
        bank_wait(4)
        W("pe", sDV, vd)
        E("pe", lambda e, o=psum[4].ap()[:, 0:TH], a=sb["ones1"].ap()[0:1, :],
          b=L[0:1, 2 * LS:2 * LS + TH]: e.matmul(o, a, b, start=True, stop=True))
        vpe = E("pe", lambda e, o=psum[3].ap()[:, 0:TH], a=sb["ones1"].ap()[0:1, :],
                b=L[0:1, 3 * LS:3 * LS + TH]: e.matmul(o, a, b, start=True, stop=True), sPE)
        W("dv", sPE, vpe)
        for dc in range(DC):
            E("dv", lambda e, o=out_fn(dc), a=src_fn(dc), b=psum[4].ap()[:, 0:TH]:
              e.tensor_tensor(o, a, b, ALU.mult))
            vd = E("dv", lambda e, o=out_fn(dc), a=out_fn(dc), b=psum[3].ap()[:, 0:TH]:
                   e.tensor_tensor(o, a, b, ALU.subtract), sDV)
        bank_free[3] = (sDV, vd)
        bank_free[4] = (sDV, vd)
        bank_free[5] = (sDV, vd)
        return vd

    vpe_sqh = [0, 0]
    va_h = 0
    vst = 0
    for th in range(NTH):
        hs0 = th * TH
        for oc in range(IC):
            W("sy", sDV, va_q if False else sDV.n * 0) if False else None
            vld = DMA("sy", pou.ap(0, TH), hbm["sp_po"][oc, :, hs0:hs0 + TH])
            W("ac", sW, vld)
            if th == 0 and oc == 0:
                W("ac", sDV, scaleo_done)
            W("ac", sDV, va_h)
            va = E("ac", lambda e, o=t_t.ap(0, TH), i=pou.ap(0, TH),
                   r=sb["qcol"].ap()[:, 7:8], c=sb["rndc"].ap()[:, 0:1]:
                   e.activation(o, i, AF.Identity, bias=c, scale=r), sAC)
            W("dv", sAC, va)
            E("dv", lambda e, o=q_t.ap(0, TH), i=t_t.ap(0, TH):
              e.tensor_scalar(o, i, -RND_C, 32767.0, ALU.add, ALU.min))
            E("dv", lambda e, o=q_t.ap(0, TH), i=q_t.ap(0, TH):
              e.tensor_scalar(o, i, -32768.0, None, ALU.max))
            E("dv", lambda e, o=oq_hi.ap(oc * TH, TH), i=q_t.ap(0, TH),
              s=sb["qcol"].ap()[:, 1:2]: e.tensor_scalar(o, i, s, None, ALU.mult))
            vq = E("dv", lambda e, o=oq_lo.ap(oc * TH, TH), a=q_t.ap(0, TH),
                   s=sb["qcol"].ap()[:, 1:2], b=oq_hi.ap(oc * TH, TH):
                   e.scalar_tensor_tensor(o, a, s, b, ALU.mult, ALU.subtract), sDV)
            va_h = vq
            W("sy", sDV, vq)
        W("pe", sDV, vq)
        if th == 0:
            W("sy", sG, wdn_spilled)
        for mc in range(DC):
            bank = mc % 2
            par = mc % 2
            W("sy", sPE, vpe_wdnb[par])
            for q4 in range(4):
                vwl = DMA("sy", wdnb_hi[par].ap().rearrange("p (k m) -> p k m", k=IC)[:, q4 * 6:(q4 + 1) * 6, :],
                          hbm["sp_wdn_hi"].rearrange("c p m -> p c m")[:, q4 * 6:(q4 + 1) * 6, mc * 128:(mc + 1) * 128])
                vwl = DMA("sy", wdnb_lo[par].ap().rearrange("p (k m) -> p k m", k=IC)[:, q4 * 6:(q4 + 1) * 6, :],
                          hbm["sp_wdn_lo"].rearrange("c p m -> p c m")[:, q4 * 6:(q4 + 1) * 6, mc * 128:(mc + 1) * 128])
            W("pe", sW, vwl)
            bank_wait(bank)
            vpe_h = mm3(bank, IC,
                        lambda kc, par=par: (wdnb_hi[par].ap(kc * 128, 128),
                                             wdnb_lo[par].ap(kc * 128, 128)),
                        lambda kc: (oq_hi.ap(kc * TH, TH), oq_lo.ap(kc * TH, TH)),
                        nw=TH)
            vpe_wdnb[par] = vpe_h
            vldx = DMA("sy", x1row.ap(), hbm["sp_x1"][mc, :, hs0:hs0 + TH])
            W("dv", sPE, vpe_h)
            W("dv", sW, vldx)
            vd = E("dv", lambda e, o=res3.ap(mc * TH, TH), a=psum[bank].ap()[:, 0:TH],
                   b=x1row.ap(): e.tensor_tensor(o, a, b, ALU.add), sDV)
            bank_free[bank] = (sDV, vd)
            W("sy", sDV, vd)
        emit_ln_h(lambda dc: res3.ap(dc * TH, TH), lambda dc: x2_t.ap(dc * TH, TH),
                  src_ready=vd)
        vdl = sDV.n
        W("pe", sDV, vdl)
        for j in range(TH // 128):
            for dc in range(DC):
                bank = 6 + (dc % 2)
                bank_wait(bank)
                vpe = E("pe", lambda e, o=psum[bank].ap()[:, 0:128],
                        i=x2_t.ap(dc * TH + j * 128, 128), idn=sb["ident"].ap():
                        e.transpose(o, i, idn), sPE)
                W("ac", sPE, vpe)
                W("ac", sW, vst)
                va = E("ac", lambda e, o=orow.ap(dc * 128, 128),
                       i=psum[bank].ap()[:, 0:128]: e.copy(o, i), sAC)
                bank_free[bank] = (sAC, va)
            W("sy", sAC, va)
            trow = th * (TH // 128) + j
            vst = DMA("sy", hbm["out"][trow * 128:(trow + 1) * 128, :], orow.ap())

    ops["sy"].append(lambda e, v=sW.n: e.wait_ge(sW.h, v))
    ops["gp"].append(lambda e, v=sG.n: e.wait_ge(sG.h, v))

    with nc.Block() as block:
        @block.sync
        def _(eng):
            for f in ops["sy"]:
                f(eng)

        @block.tensor
        def _(eng):
            for f in ops["pe"]:
                f(eng)

        @block.vector
        def _(eng):
            for f in ops["dv"]:
                f(eng)

        @block.scalar
        def _(eng):
            for f in ops["ac"]:
                f(eng)

        @block.gpsimd
        def _(eng):
            for f in ops["gp"]:
                f(eng)

    es.close()
    return nc


def col_layout(v, DC):
    return np.ascontiguousarray(np.asarray(v, np.float32).reshape(DC, 128).T)


def kernel(L0_attn_x_in, L0_attn_W_in, L0_attn_W_x, L0_attn_W_dt, L0_attn_b_dt,
           L0_attn_A_log, L0_attn_D_param, L0_attn_W_out, L0_attn_ln_g,
           L0_attn_ln_b, L0_ffn_w_gate, L0_ffn_w_up, L0_ffn_w_down):
    cfg = Cfg()
    x = np.ascontiguousarray(np.asarray(L0_attn_x_in, dtype=np.float32))
    Bb, L, D = x.shape
    T, N, DC = cfg.T, cfg.N, cfg.DC

    sel = np.zeros((2 * N, 2 * N * 128), dtype=np.float32)
    for j in range(2 * N):
        sel[j, j * 128:(j + 1) * 128] = 1.0
    base = {
        "W_in": np.ascontiguousarray(L0_attn_W_in, dtype=np.float32),
        "W_x": np.ascontiguousarray(L0_attn_W_x, dtype=np.float32),
        "W_dt": np.ascontiguousarray(L0_attn_W_dt, dtype=np.float32),
        "W_out": np.ascontiguousarray(L0_attn_W_out, dtype=np.float32),
        "w_up": np.ascontiguousarray(L0_ffn_w_up, dtype=np.float32),
        "w_gate": np.ascontiguousarray(L0_ffn_w_gate, dtype=np.float32),
        "w_down": np.ascontiguousarray(L0_ffn_w_down, dtype=np.float32),
        "bdt_col": col_layout(L0_attn_b_dt, DC),
        "alog_col": np.ascontiguousarray(
            np.asarray(L0_attn_A_log, np.float32).reshape(DC, 128, N)
            .transpose(1, 0, 2).reshape(128, DC * N)),
        "dpar_col": col_layout(L0_attn_D_param, DC),
        "lng_col": col_layout(L0_attn_ln_g, DC),
        "lnb_col": col_layout(L0_attn_ln_b, DC),
        "ident": np.eye(128, dtype=np.float32),
        "ones1": np.ones((128, 128), dtype=np.float32),
        "identb": np.eye(128, dtype=ml_dtypes.bfloat16),
        "sel": sel.astype(ml_dtypes.bfloat16),
    }
    in_maps = []
    for c in range(8):
        b, h = c // 2, c % 2
        im = dict(base)
        im["x_in"] = np.ascontiguousarray(x[b, h * T:(h + 1) * T, :])
        mask = np.zeros((128, 2), np.float32)
        mask[:, 0] = 1.0 if h == 0 else 0.0
        mask[:, 1] = 1.0 if h == 1 else 0.0
        im["mask_col"] = mask
        in_maps.append(im)

    nc = build_nc(cfg)
    res = run_bass_kernel_spmd(nc, in_maps, core_ids=list(range(8)))
    out = np.empty((Bb, L, D), np.float32)
    for c in range(8):
        b, h = c // 2, c % 2
        out[b, h * T:(h + 1) * T, :] = res.results[c]["out"]
    return out


# version 06
# speedup vs baseline: 1.2681x; 1.2681x over previous
import numpy as np
import ml_dtypes
import concourse.bass as bass
import concourse.mybir as mybir
from concourse.bass_utils import run_bass_kernel_spmd

F32 = mybir.dt.float32
BF16 = mybir.dt.bfloat16
AF = mybir.ActivationFunctionType
ALU = mybir.AluOpType
AX = mybir.AxisListType

RND_C = float(1.5 * 2 ** 23)
LN_EPS = 1e-5


class Cfg:
    def __init__(self, T=2048, D=1024, N=16, DI=3072, TTL=512):
        self.T, self.D, self.N, self.DI, self.TTL = T, D, N, DI, TTL
        self.NTT = T // TTL
        self.DC = D // 128
        self.IC = DI // 128
        self.KC = D // 128


class Sem:
    def __init__(self, es, nc, name):
        self.h = es.enter_context(nc.semaphore(name))
        self.n = 0

    def inc(self, k=1):
        self.n += k
        return self.n


class Arena:
    def __init__(self, nc, name, cols, dtype):
        self.t = nc.alloc_sbuf_tensor(name, [128, cols], dtype)
        self.free_list = [(0, cols)]

    def alloc(self, n, align=8):
        for i, (s, e) in enumerate(self.free_list):
            a = (s + align - 1) // align * align
            if a + n <= e:
                segs = []
                if a > s:
                    segs.append((s, a))
                if a + n < e:
                    segs.append((a + n, e))
                self.free_list[i:i + 1] = segs
                return a
        raise RuntimeError(f"arena OOM: need {n}, free={self.free_list}")

    def free(self, off, n):
        self.free_list.append((off, off + n))
        self.free_list.sort()
        m = []
        for s, e in self.free_list:
            if m and m[-1][1] >= s:
                m[-1] = (m[-1][0], max(m[-1][1], e))
            else:
                m.append((s, e))
        self.free_list = [(s, e) for s, e in m]


class Buf:
    def __init__(self, arena, cols, align=8):
        self.arena = arena
        self.off = arena.alloc(cols, align)
        self.cols = cols

    def ap(self, c0=0, n=None, rows=128, row0=0):
        if n is None:
            n = self.cols - c0
        return self.arena.t.ap()[row0:row0 + rows, self.off + c0:self.off + c0 + n]

    def release(self):
        self.arena.free(self.off, self.cols)


def build_nc(cfg, dbg_spills=()):
    import contextlib
    es = contextlib.ExitStack()
    nc = bass.Bass("TRN2", target_bir_lowering=False, debug=False,
                   detect_race_conditions=False)
    T, D, N, DI, TTL = cfg.T, cfg.D, cfg.N, cfg.DI, cfg.TTL
    DC, IC, KC, NTT = cfg.DC, cfg.IC, cfg.KC, cfg.NTT
    TJ = TTL // 128

    hbm = {}
    hbm["x_in"] = nc.dram_tensor("x_in", [T, D], F32, kind="ExternalInput").ap()
    for wn, shape in [("W_in", [D, 2 * D]), ("W_x", [D, D + 2 * N]),
                      ("W_dt", [D, D]), ("W_out", [D, D]),
                      ("w_up", [DI, D]), ("w_gate", [DI, D]), ("w_down", [D, DI])]:
        hbm[wn] = nc.dram_tensor(wn, shape, F32, kind="ExternalInput").ap()
    for cn, shape, dt in [("ident", [128, 128], F32), ("ones1", [128, 128], F32),
                          ("identb", [128, 128], BF16),
                          ("sel", [2 * N, 2 * N * 128], BF16),
                          ("bdt_col", [128, DC], F32), ("alog_col", [128, DC * N], F32),
                          ("dpar_col", [128, DC], F32), ("lng_col", [128, DC], F32),
                          ("lnb_col", [128, DC], F32), ("mask_col", [128, 2], F32)]:
        hbm[cn] = nc.dram_tensor(cn, shape, dt, kind="ExternalInput").ap()
    hbm["out"] = nc.dram_tensor("out", [T, D], F32, kind="ExternalOutput").ap()

    for k, shape, dt in [
        ("sp_xt_f32", [DC, 128, T], F32), ("sp_xt_hi", [DC, 128, T], BF16),
        ("sp_xt_lo", [DC, 128, T], BF16),
        ("sp_x_f32", [DC, 128, T], F32), ("sp_x_hi", [DC, 128, T], BF16),
        ("sp_x_lo", [DC, 128, T], BF16),
        ("sp_sz", [DC, 128, T], F32),
        ("sp_dr_hi", [DC, 128, T], BF16), ("sp_dr_lo", [DC, 128, T], BF16),
        ("sp_S", [DC, 128, T], F32), ("sp_y", [DC, 128, T], F32),
        ("sp_x1", [DC, 128, T], F32),
        ("sp_up", [IC, 128, T], F32), ("sp_gate", [IC, 128, T], F32),
        ("sp_wdn_hi", [DC, IC, 128, 128], BF16), ("sp_wdn_lo", [DC, IC, 128, 128], BF16),
        ("sp_pg", [IC, 128, T], F32), ("sp_po", [IC, 128, T], F32),
    ]:
        kind = "ExternalOutput" if k in dbg_spills else "Internal"
        hbm[k] = nc.dram_tensor(k, list(shape), dt, kind=kind).ap()

    cc_carry_in = nc.dram_tensor("cc_carry_in", [128, N * DC], F32)
    cc_carry_out = nc.dram_tensor("cc_carry_out", [128, N * DC], F32)
    cc_max_in = nc.dram_tensor("cc_max_in", [1, 16], F32)
    cc_max_out = nc.dram_tensor("cc_max_out", [1, 16], F32)

    sb = {}
    for cn, shape, dt in [("ident", [128, 128], F32), ("ones1", [128, 128], F32),
                          ("identb", [128, 128], BF16),
                          ("sel", [2 * N, 2 * N * 128], BF16),
                          ("bdt_col", [128, DC], F32), ("alog_col", [128, DC * N], F32),
                          ("dpar_col", [128, DC], F32), ("lng_col", [128, DC], F32),
                          ("lnb_col", [128, DC], F32), ("mask_col", [128, 2], F32),
                          ("A", [128, DC * N], F32), ("zeros", [128, TTL], F32),
                          ("rndc", [128, 1], F32), ("state", [128, N * DC], F32),
                          ("scar", [128, DC], F32),
                          ("bch", [2 * N, T], BF16), ("bcl", [2 * N, T], BF16),
                          ("lns", [1, 4 * max(TTL, 256)], F32),
                          ("red", [128, 8], F32), ("redt", [128, 1], F32),
                          ("qs", [1, 16], F32), ("qcol", [128, 8], F32),
                          ("wbs", [128, 32], F32), ("wbr", [128, 32], F32)]:
        sb[cn] = nc.alloc_sbuf_tensor("sb_" + cn, shape, dt)

    psum = [es.enter_context(nc.psum_tensor(f"pb{i}", [128, 512], F32)) for i in range(8)]

    af = Arena(nc, "af32", 17600, F32)
    ab = Arena(nc, "abf16", 45000, BF16)

    ops = {"pe": [], "dv": [], "ac": [], "gp": [], "sy": []}
    sW = Sem(es, nc, "sW")
    sG = Sem(es, nc, "sG")
    sPE = Sem(es, nc, "sPE")
    sDV = Sem(es, nc, "sDV")
    sAC = Sem(es, nc, "sAC")
    sCC = Sem(es, nc, "sCC")

    def E(q, fn, sem=None, inc=1):
        if sem is None:
            ops[q].append(fn)
            return None
        ops[q].append(lambda e, f=fn, h=sem.h, k=inc: f(e).then_inc(h, k))
        return sem.inc(inc)

    def W(q, sem, v):
        if v and v > 0:
            ops[q].append(lambda e, h=sem.h, v=v: e.wait_ge(h, v))

    def DMA(q, dst, src):
        sem = sW if q == "sy" else sG
        ops[q].append(lambda e, d=dst, s=src: e.dma_start(out=d, in_=s).then_inc(sem.h, 16))
        return sem.inc(16)

    bank_free = [None] * 8

    def bank_wait(j):
        if bank_free[j] is not None:
            sem, v = bank_free[j]
            W("pe", sem, v)

    def phase_barrier():
        for q in ("sy", "pe", "dv", "ac", "gp"):
            for sem in (sW, sG, sPE, sDV, sAC, sCC):
                W(q, sem, sem.n)

    for name in ["ident", "ones1", "identb", "sel", "bdt_col", "alog_col",
                 "dpar_col", "lng_col", "lnb_col", "mask_col"]:
        DMA("sy", sb[name].ap(), hbm[name])
    w_consts = sW.n

    E("dv", lambda e: e.memset(sb["zeros"].ap(), 0.0))
    E("dv", lambda e: e.memset(sb["rndc"].ap(), RND_C))
    E("dv", lambda e: e.memset(sb["state"].ap(), 0.0))
    E("dv", lambda e: e.memset(sb["scar"].ap(), 0.0))
    E("dv", lambda e: e.memset(sb["qs"].ap(), 0.0))
    E("dv", lambda e: e.memset(sb["red"].ap(), 0.0), sDV)
    W("ac", sW, w_consts)
    va = E("ac", lambda e: e.activation(sb["A"].ap(), sb["alog_col"].ap(), AF.Exp), sAC)
    W("dv", sAC, va)
    E("dv", lambda e: e.tensor_scalar(sb["A"].ap(), sb["A"].ap(), -1.0, None, ALU.mult), sDV)

    def load_split_weight(wname, rows, cols, f32buf, hibuf, lobuf):
        nkc = rows // 128
        v = DMA("sy", f32buf.ap().rearrange("p (k m) -> p k m", k=nkc),
                hbm[wname].rearrange("(k p) m -> p k m", p=128))
        W("ac", sW, v)
        va = E("ac", lambda e, o=hibuf.ap(), i=f32buf.ap(): e.copy(o, i), sAC)
        W("dv", sAC, va)
        vd = E("dv", lambda e, o=lobuf.ap(), a=f32buf.ap(), b=hibuf.ap():
               e.tensor_tensor(o, a, b, ALU.subtract), sDV)
        W("sy", sDV, vd)
        W("gp", sDV, vd)
        return vd

    def mm3(bank, nkc, get_w, get_x, mw=128, nw=None):
        if nw is None:
            nw = TTL
        for kc in range(nkc):
            wh, wl = get_w(kc)
            xh, xl = get_x(kc)
            E("pe", lambda e, o=psum[bank].ap()[0:mw, 0:nw], a=wh, b=xh, s=(kc == 0):
              e.matmul(o, a, b, start=s, stop=False))
            E("pe", lambda e, o=psum[bank].ap()[0:mw, 0:nw], a=wh, b=xl:
              e.matmul(o, a, b, start=False, stop=False))
            v = E("pe", lambda e, o=psum[bank].ap()[0:mw, 0:nw], a=wl, b=xh,
                  st=(kc == nkc - 1): e.matmul(o, a, b, start=False, stop=st),
                  sPE if kc == nkc - 1 else None)
        return v

    w_in_f32 = Buf(af, 0) if False else None
    xt_f = Buf(af, DC * TTL)
    xt_h = Buf(ab, DC * TTL)
    xt_l = Buf(ab, DC * TTL)
    xrow = [Buf(af, D), Buf(af, D)]
    W("pe", sW, w_consts)
    vpe_row = [0, 0]
    va_evac = 0
    for tt in range(NTT):
        for j in range(TJ):
            par = j % 2
            trow = tt * TJ + j
            W("sy", sPE, vpe_row[par])
            vld = DMA("sy", xrow[par].ap(), hbm["x_in"][trow * 128:(trow + 1) * 128, :])
            W("pe", sW, vld)
            for dc in range(DC):
                bank = dc % 4
                bank_wait(bank)
                vpe = E("pe", lambda e, o=psum[bank].ap()[:, 0:128],
                        i=xrow[par].ap(dc * 128, 128), idn=sb["ident"].ap():
                        e.transpose(o, i, idn), sPE)
                W("ac", sPE, vpe)
                va = E("ac", lambda e, o=xt_f.ap(dc * TTL + j * 128, 128),
                       i=psum[bank].ap()[:, 0:128]: e.copy(o, i), sAC)
                bank_free[bank] = (sAC, va)
            vpe_row[par] = vpe
            va_evac = va
        W("dv", sAC, va_evac)
        E("dv", lambda e, o=xt_h.ap(), i=xt_f.ap(): e.tensor_copy(o, i))
        vd = E("dv", lambda e, o=xt_l.ap(), a=xt_f.ap(), b=xt_h.ap():
               e.tensor_tensor(o, a, b, ALU.subtract), sDV)
        W("gp", sDV, vd)
        DMA("gp", hbm["sp_xt_f32"].rearrange("c p t -> p c t")[:, :, tt * TTL:(tt + 1) * TTL],
            xt_f.ap().rearrange("p (c t) -> p c t", c=DC))
        DMA("gp", hbm["sp_xt_hi"].rearrange("c p t -> p c t")[:, :, tt * TTL:(tt + 1) * TTL],
            xt_h.ap().rearrange("p (c t) -> p c t", c=DC))
        vg = DMA("gp", hbm["sp_xt_lo"].rearrange("c p t -> p c t")[:, :, tt * TTL:(tt + 1) * TTL],
                 xt_l.ap().rearrange("p (c t) -> p c t", c=DC))
        W("ac", sG, vg)
        W("dv", sG, vg)
    for b_ in xrow:
        b_.release()
    xt_f.release()
    xt_h.release()
    xt_l.release()
    spill_xt = sG.n
    phase_barrier()

    wf = Buf(af, 0) if False else None
    w_f = Buf(af, KC * 128)
    w_f.release()

    w1f = Buf(af, 0) if False else None
    w1_f32 = Buf(af, 0) if False else None

    win_f = Buf(af, KC * 2 * D // 4) if False else None
    win_hi = Buf(ab, KC * 2 * D)
    win_lo = Buf(ab, KC * 2 * D)
    wstage = Buf(af, KC * D)
    for half in range(2):
        v = DMA("sy", wstage.ap().rearrange("p (k m) -> p k m", k=KC),
                hbm["W_in"][:, half * D:(half + 1) * D].rearrange("(k p) m -> p k m", p=128))
        W("ac", sW, v)
        va = E("ac", lambda e, o=win_hi.ap() if False else None: None) if False else None
        for kc in range(KC):
            va = E("ac", lambda e, o=win_hi.ap(kc * 2 * D + half * D, D),
                   i=wstage.ap(kc * D, D): e.copy(o, i), sAC)
        W("dv", sAC, va)
        for kc in range(KC):
            vd = E("dv", lambda e, o=win_lo.ap(kc * 2 * D + half * D, D),
                   a=wstage.ap(kc * D, D), b=win_hi.ap(kc * 2 * D + half * D, D):
                   e.tensor_tensor(o, a, b, ALU.subtract), sDV)
        W("sy", sDV, vd)
    v_win = vd
    W("gp", sDV, vd)
    wstage.release()

    xth = Buf(ab, DC * TTL)
    xtl = Buf(ab, DC * TTL)
    evf = [Buf(af, TTL), Buf(af, TTL)]
    evh = [Buf(ab, TTL), Buf(ab, TTL)]
    evl = [Buf(ab, TTL), Buf(ab, TTL)]
    vg_par = [0, 0]
    W("pe", sDV, v_win)
    W("sy", sG, spill_xt)
    vpe_tt = 0
    for tt in range(NTT):
        W("sy", sPE, vpe_tt)
        DMA("sy", xth.ap().rearrange("p (c t) -> p c t", c=DC),
            hbm["sp_xt_hi"].rearrange("c p t -> p c t")[:, :, tt * TTL:(tt + 1) * TTL])
        vld = DMA("sy", xtl.ap().rearrange("p (c t) -> p c t", c=DC),
                  hbm["sp_xt_lo"].rearrange("c p t -> p c t")[:, :, tt * TTL:(tt + 1) * TTL])
        W("pe", sW, vld)
        for mc in range(2 * DC):
            bank = mc % 2
            bank_wait(bank)
            vpe_tt = mm3(bank, KC,
                         lambda kc, mc=mc: (win_hi.ap(kc * 2 * D + mc * 128, 128),
                                            win_lo.ap(kc * 2 * D + mc * 128, 128)),
                         lambda kc: (xth.ap(kc * TTL, TTL), xtl.ap(kc * TTL, TTL)))
            W("ac", sPE, vpe_tt)
            par = mc % 2
            W("ac", sG, vg_par[par])
            W("dv", sG, vg_par[par])
            if mc < DC:
                va = E("ac", lambda e, o=evf[par].ap(), i=psum[bank].ap()[:, 0:TTL]: e.copy(o, i), sAC)
                W("dv", sAC, va)
                E("dv", lambda e, o=evh[par].ap(), i=evf[par].ap(): e.tensor_copy(o, i))
                vd = E("dv", lambda e, o=evl[par].ap(), a=evf[par].ap(), b=evh[par].ap():
                       e.tensor_tensor(o, a, b, ALU.subtract), sDV)
                bank_free[bank] = (sAC, va)
                W("gp", sDV, vd)
                DMA("gp", hbm["sp_x_f32"][mc, :, tt * TTL:(tt + 1) * TTL], evf[par].ap())
                DMA("gp", hbm["sp_x_hi"][mc, :, tt * TTL:(tt + 1) * TTL], evh[par].ap())
                vg_par[par] = DMA("gp", hbm["sp_x_lo"][mc, :, tt * TTL:(tt + 1) * TTL], evl[par].ap())
            else:
                va = E("ac", lambda e, o=evf[par].ap(), i=psum[bank].ap()[:, 0:TTL]:
                       e.activation(o, i, AF.Silu), sAC)
                bank_free[bank] = (sAC, va)
                W("gp", sAC, va)
                vg_par[par] = DMA("gp", hbm["sp_sz"][mc - DC, :, tt * TTL:(tt + 1) * TTL], evf[par].ap())
    win_hi.release()
    win_lo.release()
    spill_p1 = sG.n
    phase_barrier()

    MX = D + 2 * N
    wx_f32 = Buf(af, KC * MX)
    wx_hi = Buf(ab, KC * MX)
    wx_lo = Buf(ab, KC * MX)
    v_wx = load_split_weight("W_x", D, MX, wx_f32, wx_hi, wx_lo)
    wx_f32.release()
    bcf = Buf(af, T)
    W("pe", sDV, v_wx)
    W("sy", sG, spill_p1)
    vpe_tt = 0
    for tt in range(NTT):
        W("sy", sPE, vpe_tt)
        DMA("sy", xth.ap().rearrange("p (c t) -> p c t", c=DC),
            hbm["sp_x_hi"].rearrange("c p t -> p c t")[:, :, tt * TTL:(tt + 1) * TTL])
        vld = DMA("sy", xtl.ap().rearrange("p (c t) -> p c t", c=DC),
                  hbm["sp_x_lo"].rearrange("c p t -> p c t")[:, :, tt * TTL:(tt + 1) * TTL])
        W("pe", sW, vld)
        for mc in range(DC + 1):
            bank = mc % 2
            mw = 128 if mc < DC else 2 * N
            bank_wait(bank)
            vpe_tt = mm3(bank, KC,
                         lambda kc, mc=mc, mw=mw: (wx_hi.ap(kc * MX + mc * 128, mw),
                                                   wx_lo.ap(kc * MX + mc * 128, mw)),
                         lambda kc: (xth.ap(kc * TTL, TTL), xtl.ap(kc * TTL, TTL)),
                         mw=mw)
            W("ac", sPE, vpe_tt)
            par = mc % 2
            if mc < DC:
                W("ac", sG, vg_par[par])
                W("dv", sG, vg_par[par])
                va = E("ac", lambda e, o=evh[par].ap(), i=psum[bank].ap()[:, 0:TTL]: e.copy(o, i), sAC)
                W("dv", sAC, va)
                W("dv", sPE, vpe_tt)
                vd = E("dv", lambda e, o=evl[par].ap(), a=psum[bank].ap()[:, 0:TTL], b=evh[par].ap():
                       e.tensor_tensor(o, a, b, ALU.subtract), sDV)
                bank_free[bank] = (sDV, vd)
                W("gp", sDV, vd)
                DMA("gp", hbm["sp_dr_hi"][mc, :, tt * TTL:(tt + 1) * TTL], evh[par].ap())
                vg_par[par] = DMA("gp", hbm["sp_dr_lo"][mc, :, tt * TTL:(tt + 1) * TTL], evl[par].ap())
            else:
                va = E("ac", lambda e, o=bcf.ap(tt * TTL, TTL, rows=2 * N),
                       i=psum[bank].ap()[0:2 * N, 0:TTL]: e.copy(o, i), sAC)
                W("dv", sAC, va)
                E("dv", lambda e, o=sb["bch"].ap()[:, tt * TTL:(tt + 1) * TTL],
                  i=bcf.ap(tt * TTL, TTL, rows=2 * N): e.tensor_copy(o, i))
                vd = E("dv", lambda e, o=sb["bcl"].ap()[:, tt * TTL:(tt + 1) * TTL],
                       a=bcf.ap(tt * TTL, TTL, rows=2 * N),
                       b=sb["bch"].ap()[:, tt * TTL:(tt + 1) * TTL]:
                       e.tensor_tensor(o, a, b, ALU.subtract), sDV)
                bank_free[bank] = (sDV, vd)
    xth.release()
    xtl.release()
    wx_hi.release()
    wx_lo.release()
    bcf.release()
    bc_ready = sDV.n
    spill_p2 = sG.n
    phase_barrier()

    wdt_f32 = Buf(af, KC * D)
    wdt_hi = Buf(ab, KC * D)
    wdt_lo = Buf(ab, KC * D)
    v_wdt = load_split_weight("W_dt", D, D, wdt_f32, wdt_hi, wdt_lo)
    wdt_f32.release()

    drh = Buf(ab, DC * TTL)
    drl = Buf(ab, DC * TTL)
    delta = Buf(af, DC * TTL)
    dx = Buf(af, DC * TTL)
    y_tt = Buf(af, DC * TTL)
    s_tc = Buf(af, TTL)
    x_rl = Buf(af, TTL)
    e_tmp = Buf(af, TTL)
    dA_t = [Buf(af, TTL), Buf(af, TTL)]
    u_t = Buf(af, TTL)
    h_t = Buf(af, TTL)
    hc_t = Buf(af, TTL)

    W("pe", sDV, v_wdt)
    W("sy", sG, spill_p2)
    vpe_tt = 0
    vd_scan = [0, 0]
    for tt in range(NTT):
        ts0 = tt * TTL
        W("sy", sPE, vpe_tt)
        DMA("sy", drh.ap().rearrange("p (c t) -> p c t", c=DC),
            hbm["sp_dr_hi"].rearrange("c p t -> p c t")[:, :, ts0:ts0 + TTL])
        vld = DMA("sy", drl.ap().rearrange("p (c t) -> p c t", c=DC),
                  hbm["sp_dr_lo"].rearrange("c p t -> p c t")[:, :, ts0:ts0 + TTL])
        W("pe", sW, vld)
        for dc in range(DC):
            bank = dc % 2
            bank_wait(bank)
            vpe_tt = mm3(bank, KC,
                         lambda kc, dc=dc: (wdt_hi.ap(kc * D + dc * 128, 128),
                                            wdt_lo.ap(kc * D + dc * 128, 128)),
                         lambda kc: (drh.ap(kc * TTL, TTL), drl.ap(kc * TTL, TTL)))
            W("ac", sPE, vpe_tt)
            E("ac", lambda e, o=e_tmp.ap(), i=psum[bank].ap()[:, 0:TTL],
              b=sb["bdt_col"].ap()[:, dc:dc + 1]:
              e.activation(o, i, AF.Exp, bias=b))
            va = E("ac", lambda e, o=delta.ap(dc * TTL, TTL), i=e_tmp.ap():
                   e.activation(o, i, AF.Ln, bias=1.0), sAC)

            bank_free[bank] = (sAC, va)
            vx = DMA("sy", x_rl.ap(), hbm["sp_x_f32"][dc, :, ts0:ts0 + TTL])
            W("dv", sAC, va)
            W("dv", sW, vx)
            E("dv", lambda e, o=dx.ap(dc * TTL, TTL), a=delta.ap(dc * TTL, TTL),
              b=x_rl.ap(): e.tensor_tensor(o, a, b, ALU.mult))
            E("dv", lambda e, o=y_tt.ap(dc * TTL, TTL), a=x_rl.ap(),
              s=sb["dpar_col"].ap()[:, dc:dc + 1]:
              e.tensor_scalar(o, a, s, None, ALU.mult))
            vsc = E("dv", lambda e, o=s_tc.ap(), d0=delta.ap(dc * TTL, TTL),
                    z=sb["zeros"].ap(), ini=sb["scar"].ap()[:, dc:dc + 1]:
                    e.tensor_tensor_scan(o, d0, z, ini, ALU.add, ALU.add), sDV)
            W("dv", sDV, vsc)
            vd = E("dv", lambda e, o=sb["scar"].ap()[:, dc:dc + 1], i=s_tc.ap(TTL - 1, 1):
                   e.tensor_copy(o, i), sDV)
            W("sy", sDV, vd)
            W("gp", sDV, vd)
            vg = DMA("gp", hbm["sp_S"][dc, :, ts0:ts0 + TTL], s_tc.ap())
            W("dv", sG, vg)
        for n in range(N):
            pb, pc = 6, 7
            bank_wait(pb)
            bank_wait(pc)
            if tt == 0 and n == 0:
                W("pe", sDV, bc_ready)
            E("pe", lambda e, o=psum[pb].ap()[:, 0:TTL], a=sb["sel"].ap()[:, n * 128:(n + 1) * 128],
              b=sb["bch"].ap()[:, ts0:ts0 + TTL]: e.matmul(o, a, b, start=True, stop=False))
            E("pe", lambda e, o=psum[pb].ap()[:, 0:TTL], a=sb["sel"].ap()[:, n * 128:(n + 1) * 128],
              b=sb["bcl"].ap()[:, ts0:ts0 + TTL]: e.matmul(o, a, b, start=False, stop=True))
            E("pe", lambda e, o=psum[pc].ap()[:, 0:TTL], a=sb["sel"].ap()[:, (N + n) * 128:(N + n + 1) * 128],
              b=sb["bch"].ap()[:, ts0:ts0 + TTL]: e.matmul(o, a, b, start=True, stop=False))
            vbc = E("pe", lambda e, o=psum[pc].ap()[:, 0:TTL], a=sb["sel"].ap()[:, (N + n) * 128:(N + n + 1) * 128],
                    b=sb["bcl"].ap()[:, ts0:ts0 + TTL]: e.matmul(o, a, b, start=False, stop=True), sPE)
            for dc in range(DC):
                par = dc % 2
                W("ac", sDV, vd_scan[par])
                va = E("ac", lambda e, o=dA_t[par].ap(), i=delta.ap(dc * TTL, TTL),
                       s=sb["A"].ap()[:, dc * N + n:dc * N + n + 1]:
                       e.activation(o, i, AF.Exp, scale=s), sAC)
                W("dv", sPE, vbc)
                E("dv", lambda e, o=u_t.ap(), a=dx.ap(dc * TTL, TTL), b=psum[pb].ap()[:, 0:TTL]:
                  e.tensor_tensor(o, a, b, ALU.mult))
                W("dv", sAC, va)
                col = n * DC + dc
                vsc = E("dv", lambda e, o=h_t.ap(), d0=dA_t[par].ap(), d1=u_t.ap(),
                        ini=sb["state"].ap()[:, col:col + 1]:
                        e.tensor_tensor_scan(o, d0, d1, ini, ALU.mult, ALU.add), sDV)
                W("dv", sDV, vsc)
                E("dv", lambda e, o=sb["state"].ap()[:, col:col + 1], i=h_t.ap(TTL - 1, 1):
                  e.tensor_copy(o, i))
                E("dv", lambda e, o=hc_t.ap(), a=h_t.ap(), b=psum[pc].ap()[:, 0:TTL]:
                  e.tensor_tensor(o, a, b, ALU.mult))
                vd = E("dv", lambda e, o=y_tt.ap(dc * TTL, TTL), a=y_tt.ap(dc * TTL, TTL),
                       b=hc_t.ap(): e.tensor_tensor(o, a, b, ALU.add), sDV)
                vd_scan[par] = vd
            bank_free[pb] = (sDV, vd)
            bank_free[pc] = (sDV, vd)
        W("gp", sDV, vd)
        vg = DMA("gp", hbm["sp_y"].rearrange("c p t -> p c t")[:, :, ts0:ts0 + TTL],
                 y_tt.ap().rearrange("p (c t) -> p c t", c=DC))
        W("dv", sG, vg)
    drh.release()
    drl.release()
    wdt_hi.release()
    wdt_lo.release()
    delta.release()
    dx.release()
    s_tc.release()
    x_rl.release()
    e_tmp.release()
    u_t.release()
    h_t.release()
    hc_t.release()
    for b_ in dA_t:
        b_.release()
    y_tt.release()
    spill_y = sG.n
    phase_barrier()

    hfin = Buf(af, N * DC)
    phase_barrier()
    vd = E("dv", lambda e, o=hfin.ap(), a=sb["state"].ap(),
           s=sb["mask_col"].ap()[:, 0:1]: e.tensor_scalar(o, a, s, None, ALU.mult), sDV)
    W("gp", sDV, vd)
    vg = DMA("gp", cc_carry_in.ap(), hfin.ap())
    W("gp", sG, vg)
    vcc = E("gp", lambda e: e.collective_compute(
        "AllReduce", ALU.add,
        replica_groups=[[0, 1], [2, 3], [4, 5], [6, 7]],
        ins=[cc_carry_in.ap().opt()],
        outs=[cc_carry_out.ap().opt()]), sCC)
    W("gp", sCC, vcc)
    vg = DMA("gp", hfin.ap(), cc_carry_out.ap())
    W("dv", sG, vg)
    v_hin = E("dv", lambda e, o=hfin.ap(), a=hfin.ap(),
              s=sb["mask_col"].ap()[:, 1:2]: e.tensor_scalar(o, a, s, None, ALU.mult), sDV)

    y_c = Buf(af, DC * TTL)
    s_c = Buf(af, DC * TTL)
    p_t = Buf(af, TTL)
    cp_t = Buf(af, TTL)
    W("dv", sDV, v_hin)
    W("sy", sG, spill_y)
    va_p = 0
    for tt in range(NTT):
        ts0 = tt * TTL
        DMA("sy", y_c.ap().rearrange("p (c t) -> p c t", c=DC),
            hbm["sp_y"].rearrange("c p t -> p c t")[:, :, ts0:ts0 + TTL])
        vld = DMA("sy", s_c.ap().rearrange("p (c t) -> p c t", c=DC),
                  hbm["sp_S"].rearrange("c p t -> p c t")[:, :, ts0:ts0 + TTL])
        W("ac", sW, vld)
        W("dv", sW, vld)
        for n in range(N):
            pc = 6 if n % 2 == 0 else 7
            bank_wait(pc)
            E("pe", lambda e, o=psum[pc].ap()[:, 0:TTL], a=sb["sel"].ap()[:, (N + n) * 128:(N + n + 1) * 128],
              b=sb["bch"].ap()[:, ts0:ts0 + TTL]: e.matmul(o, a, b, start=True, stop=False))
            vbc = E("pe", lambda e, o=psum[pc].ap()[:, 0:TTL], a=sb["sel"].ap()[:, (N + n) * 128:(N + n + 1) * 128],
                    b=sb["bcl"].ap()[:, ts0:ts0 + TTL]: e.matmul(o, a, b, start=False, stop=True), sPE)
            for dc in range(DC):
                W("ac", sDV, va_p)
                va = E("ac", lambda e, o=p_t.ap(), i=s_c.ap(dc * TTL, TTL),
                       s=sb["A"].ap()[:, dc * N + n:dc * N + n + 1]:
                       e.activation(o, i, AF.Exp, scale=s), sAC)
                W("dv", sAC, va)
                W("dv", sPE, vbc)
                E("dv", lambda e, o=cp_t.ap(), a=p_t.ap(), b=psum[pc].ap()[:, 0:TTL]:
                  e.tensor_tensor(o, a, b, ALU.mult))
                col = n * DC + dc
                vd = E("dv", lambda e, o=y_c.ap(dc * TTL, TTL), a=cp_t.ap(),
                       s=hfin.ap()[:, col:col + 1], b=y_c.ap(dc * TTL, TTL):
                       e.scalar_tensor_tensor(o, a, s, b, ALU.mult, ALU.add), sDV)
                va_p = vd
            bank_free[pc] = (sDV, vd)
        W("gp", sDV, vd)
        vg = DMA("gp", hbm["sp_y"].rearrange("c p t -> p c t")[:, :, ts0:ts0 + TTL],
                 y_c.ap().rearrange("p (c t) -> p c t", c=DC))
        W("sy", sG, vg)
    y_c.release()
    s_c.release()
    p_t.release()
    cp_t.release()
    hfin.release()
    spill_y2 = sG.n
    phase_barrier()

    phase_barrier()
    wo_f32 = Buf(af, KC * D)
    wo_hi = Buf(ab, KC * D)
    wo_lo = Buf(ab, KC * D)
    v_wo = load_split_weight("W_out", D, D, wo_f32, wo_hi, wo_lo)
    wo_f32.release()

    y2h = Buf(ab, DC * TTL)
    y2l = Buf(ab, DC * TTL)
    xt_rl = Buf(af, DC * TTL)
    res1 = Buf(af, DC * TTL)
    x1_t = Buf(af, DC * TTL)
    sq_t = [Buf(af, TTL), Buf(af, TTL)]

    def emit_ln(src_fn, out_fn, affine, src_ready=0):
        nonlocal va_sq, vpe_sq
        bank_wait(5)
        bank_wait(3)
        W("ac", sDV, src_ready)
        for dc in range(DC):
            par = dc % 2
            W("ac", sPE, vpe_sq[par])
            va = E("ac", lambda e, o=sq_t[par].ap(), i=src_fn(dc):
                   e.activation(o, i, AF.Square), sAC)
            W("pe", sAC, va)
            E("pe", lambda e, o=psum[5].ap()[0:1, 0:TTL], a=sb["ones1"].ap()[:, 0:1],
              b=src_fn(dc), s=(dc == 0), st=(dc == DC - 1): e.matmul(o, a, b, start=s, stop=st))
            vpe = E("pe", lambda e, o=psum[3].ap()[0:1, 0:TTL], a=sb["ones1"].ap()[:, 0:1],
                    b=sq_t[par].ap(), s=(dc == 0), st=(dc == DC - 1):
                    e.matmul(o, a, b, start=s, stop=st), sPE)
            vpe_sq[par] = vpe
        W("dv", sPE, vpe)
        L = sb["lns"].ap()
        E("dv", lambda e, o=L[0:1, 0:TTL], i=psum[5].ap()[0:1, 0:TTL]:
          e.tensor_scalar(o, i, 1.0 / D, None, ALU.mult))
        E("dv", lambda e, o=L[0:1, TTL:2 * TTL], i=psum[3].ap()[0:1, 0:TTL]:
          e.tensor_scalar(o, i, 1.0 / D, None, ALU.mult))
        E("dv", lambda e, o=L[0:1, 2 * TTL:3 * TTL], a=L[0:1, 0:TTL], b=L[0:1, 0:TTL]:
          e.tensor_tensor(o, a, b, ALU.mult))
        E("dv", lambda e, o=L[0:1, TTL:2 * TTL], a=L[0:1, TTL:2 * TTL],
          b=L[0:1, 2 * TTL:3 * TTL]: e.tensor_tensor(o, a, b, ALU.subtract))
        vd = E("dv", lambda e, o=L[0:1, TTL:2 * TTL], i=L[0:1, TTL:2 * TTL]:
               e.tensor_scalar(o, i, LN_EPS, None, ALU.add), sDV)
        W("ac", sDV, vd)
        va = E("ac", lambda e, o=L[0:1, TTL:2 * TTL], i=L[0:1, TTL:2 * TTL]:
               e.activation(o, i, AF.Sqrt), sAC)
        W("dv", sAC, va)
        vf = E("dv", lambda e, o=L[0:1, 2 * TTL:3 * TTL], i=L[0:1, TTL:2 * TTL]:
               e.reciprocal(o, i), sDV)
        W("dv", sDV, vf)
        vd = E("dv", lambda e, o=L[0:1, 3 * TTL:4 * TTL], a=L[0:1, 0:TTL],
               b=L[0:1, 2 * TTL:3 * TTL]: e.tensor_tensor(o, a, b, ALU.mult), sDV)
        bank_wait(4)
        W("pe", sDV, vd)
        E("pe", lambda e, o=psum[4].ap()[:, 0:TTL], a=sb["ones1"].ap()[0:1, :],
          b=L[0:1, 2 * TTL:3 * TTL]: e.matmul(o, a, b, start=True, stop=True))
        vpe = E("pe", lambda e, o=psum[3].ap()[:, 0:TTL], a=sb["ones1"].ap()[0:1, :],
                b=L[0:1, 3 * TTL:4 * TTL]: e.matmul(o, a, b, start=True, stop=True), sPE)
        W("dv", sPE, vpe)
        for dc in range(DC):
            E("dv", lambda e, o=out_fn(dc), a=src_fn(dc), b=psum[4].ap()[:, 0:TTL]:
              e.tensor_tensor(o, a, b, ALU.mult))
            E("dv", lambda e, o=out_fn(dc), a=out_fn(dc), b=psum[3].ap()[:, 0:TTL]:
              e.tensor_tensor(o, a, b, ALU.subtract))
            if affine:
                vd = E("dv", lambda e, o=out_fn(dc), i=out_fn(dc),
                       g=sb["lng_col"].ap()[:, dc:dc + 1], bb=sb["lnb_col"].ap()[:, dc:dc + 1]:
                       e.tensor_scalar(o, i, g, bb, ALU.mult, ALU.add), sDV)
            else:
                vd = sDV.n
        if not affine:
            vd = E("dv", lambda e: e.engine_nop(), sDV)
        bank_free[3] = (sDV, vd)
        bank_free[4] = (sDV, vd)
        bank_free[5] = (sDV, vd)
        return vd

    va_sq = 0
    vpe_sq = [0, 0]
    W("pe", sDV, v_wo)
    W("sy", sG, spill_y2)
    vpe_tt = 0
    for tt in range(NTT):
        ts0 = tt * TTL
        W("sy", sPE, vpe_tt)
        W("sy", sDV, sDV.n)
        DMA("sy", xt_rl.ap().rearrange("p (c t) -> p c t", c=DC),
            hbm["sp_xt_f32"].rearrange("c p t -> p c t")[:, :, ts0:ts0 + TTL])
        DMA("sy", res1.ap().rearrange("p (c t) -> p c t", c=DC),
            hbm["sp_y"].rearrange("c p t -> p c t")[:, :, ts0:ts0 + TTL])
        vsz = DMA("sy", x1_t.ap().rearrange("p (c t) -> p c t", c=DC),
                  hbm["sp_sz"].rearrange("c p t -> p c t")[:, :, ts0:ts0 + TTL])
        W("dv", sW, vsz)
        E("dv", lambda e, o=res1.ap(), a=res1.ap(), b=x1_t.ap():
          e.tensor_tensor(o, a, b, ALU.mult))
        E("dv", lambda e, o=y2h.ap(), i=res1.ap(): e.tensor_copy(o, i))
        vd = E("dv", lambda e, o=y2l.ap(), a=res1.ap(), b=y2h.ap():
               e.tensor_tensor(o, a, b, ALU.subtract), sDV)
        W("sy", sDV, vd)
        W("pe", sDV, vd)
        vldx = sW.n
        for mc in range(DC):
            bank = mc % 2
            bank_wait(bank)
            vpe_tt = mm3(bank, KC,
                         lambda kc, mc=mc: (wo_hi.ap(kc * D + mc * 128, 128),
                                            wo_lo.ap(kc * D + mc * 128, 128)),
                         lambda kc: (y2h.ap(kc * TTL, TTL), y2l.ap(kc * TTL, TTL)))
            W("dv", sPE, vpe_tt)
            W("dv", sW, vldx)
            vd = E("dv", lambda e, o=res1.ap(mc * TTL, TTL), a=psum[bank].ap()[:, 0:TTL],
                   b=xt_rl.ap(mc * TTL, TTL): e.tensor_tensor(o, a, b, ALU.add), sDV)
            bank_free[bank] = (sDV, vd)
        emit_ln(lambda dc: res1.ap(dc * TTL, TTL), lambda dc: res1.ap(dc * TTL, TTL),
                affine=True, src_ready=vd)
        for dc in range(DC):
            vd = E("dv", lambda e, o=res1.ap(dc * TTL, TTL), a=res1.ap(dc * TTL, TTL),
                   b=xt_rl.ap(dc * TTL, TTL): e.tensor_tensor(o, a, b, ALU.add), sDV)
        emit_ln(lambda dc: res1.ap(dc * TTL, TTL), lambda dc: x1_t.ap(dc * TTL, TTL),
                affine=False, src_ready=vd)
        vd = sDV.n
        W("gp", sDV, vd)
        vg = DMA("gp", hbm["sp_x1"].rearrange("c p t -> p c t")[:, :, ts0:ts0 + TTL],
                 x1_t.ap().rearrange("p (c t) -> p c t", c=DC))
        W("dv", sG, vg)
    wo_hi.release()
    wo_lo.release()
    y2h.release()
    y2l.release()
    xt_rl.release()
    res1.release()
    x1_t.release()
    for b_ in sq_t:
        b_.release()
    spill_x1 = sG.n
    phase_barrier()


    def absmax_init(col):
        E("dv", lambda e, o=sb["red"].ap()[:, col:col + 1]: e.memset(o, 0.0))

    def absmax_acc(src_ap, col):
        vf = E("dv", lambda e, o=sb["redt"].ap(), i=src_ap:
               e.tensor_reduce(o, i, AX.X, ALU.max, apply_absolute_value=True), sDV)
        W("dv", sDV, vf)
        E("dv", lambda e, o=sb["red"].ap()[:, col:col + 1], a=sb["red"].ap()[:, col:col + 1],
          b=sb["redt"].ap(): e.tensor_tensor(o, a, b, ALU.max))

    def gmax_finish(col, slot):
        bank_wait(2)
        vd = E("dv", lambda e: e.engine_nop(), sDV)
        W("pe", sDV, vd)
        vpe = E("pe", lambda e, o=psum[2].ap()[0:1, 0:128], i=sb["red"].ap()[:, col:col + 1],
                idn=sb["ident"].ap(): e.transpose(o, i, idn), sPE)
        W("dv", sPE, vpe)
        vd = E("dv", lambda e, o=sb["qs"].ap()[0:1, slot:slot + 1], i=psum[2].ap()[0:1, 0:128]:
               e.tensor_reduce(o, i, AX.X, ALU.max), sDV)
        W("dv", sDV, vd)
        bank_free[2] = (sDV, vd)
        return vd

    def coll_max():
        vd = sDV.n
        W("gp", sDV, vd)
        vg = DMA("gp", cc_max_in.ap(), sb["qs"].ap())
        W("gp", sG, vg)
        vcc = E("gp", lambda e: e.collective_compute(
            "AllReduce", ALU.max, replica_groups=[list(range(8))],
            ins=[cc_max_in.ap().opt()], outs=[cc_max_out.ap().opt()]), sCC)
        W("gp", sCC, vcc)
        return DMA("gp", sb["qs"].ap(), cc_max_out.ap())

    def scale_calc(slot_max, slot_s, slot_r, qmax, vbi):
        W("dv", sG, vbi)
        Q = sb["qs"].ap()
        vf = E("dv", lambda e, o=Q[0:1, slot_s:slot_s + 1], i=Q[0:1, slot_max:slot_max + 1]:
               e.tensor_scalar(o, i, 1.0 / qmax, 1e-12, ALU.mult, ALU.max), sDV)
        W("dv", sDV, vf)
        vr = E("dv", lambda e, o=Q[0:1, slot_r:slot_r + 1], i=Q[0:1, slot_s:slot_s + 1]:
               e.reciprocal(o, i), sDV)
        W("dv", sDV, vr)
        return vr

    def col_bcast(slot, col, vd):
        bank_wait(2)
        W("pe", sDV, vd)
        vpe = E("pe", lambda e, o=psum[2].ap()[:, 0:1], a=sb["ones1"].ap()[0:1, :],
                b=sb["qs"].ap()[0:1, slot:slot + 1]:
                e.matmul(o, a, b, start=True, stop=True), sPE)
        W("dv", sPE, vpe)
        vd2 = E("dv", lambda e, o=sb["qcol"].ap()[:, col:col + 1], i=psum[2].ap()[:, 0:1]:
                e.tensor_copy(o, i), sDV)
        bank_free[2] = (sDV, vd2)
        return vd2

    x1w = [Buf(af, DC * TTL), Buf(af, DC * TTL)]
    qw = Buf(af, DC * TTL)
    absmax_init(0)
    W("sy", sG, spill_x1)
    va_w = [0, 0]
    for tt in range(NTT):
        par = tt % 2
        W("sy", sDV, va_w[par])
        vld = DMA("sy", x1w[par].ap().rearrange("p (c t) -> p c t", c=DC),
                  hbm["sp_x1"].rearrange("c p t -> p c t")[:, :, tt * TTL:(tt + 1) * TTL])
        W("dv", sW, vld)
        absmax_acc(x1w[par].ap(), 0)
        va_w[par] = E("dv", lambda e: e.engine_nop(), sDV)
    gmax_finish(0, 0)
    vbi = coll_max()
    vd = scale_calc(0, 1, 2, 32767.0, vbi)
    vd = col_bcast(1, 0, vd)
    vd = col_bcast(2, 1, vd)
    scales_x1 = vd

    xq_hi = Buf(ab, DC * T)
    xq_lo = Buf(ab, DC * T)
    va_q = [sDV.n, sDV.n]
    for tt in range(NTT):
        par = tt % 2
        W("sy", sDV, va_q[par])
        vld = DMA("sy", x1w[par].ap().rearrange("p (c t) -> p c t", c=DC),
                  hbm["sp_x1"].rearrange("c p t -> p c t")[:, :, tt * TTL:(tt + 1) * TTL])
        W("ac", sW, vld)
        if tt == 0:
            W("ac", sDV, scales_x1)
        va = E("ac", lambda e, o=x1w[par].ap(), i=x1w[par].ap(), r=sb["qcol"].ap()[:, 1:2],
               c=sb["rndc"].ap()[:, 0:1]:
               e.activation(o, i, AF.Identity, bias=c, scale=r), sAC)
        W("dv", sAC, va)
        E("dv", lambda e, o=qw.ap(), i=x1w[par].ap():
          e.tensor_scalar(o, i, -RND_C, 32767.0, ALU.add, ALU.min))
        E("dv", lambda e, o=qw.ap(), i=qw.ap():
          e.tensor_scalar(o, i, -32768.0, None, ALU.max))
        E("dv", lambda e, o=xq_hi.ap().rearrange("p (c u t) -> p c u t", c=DC, u=NTT)[:, :, tt, :],
          i=qw.ap().rearrange("p (c t) -> p c t", c=DC),
          s=sb["qcol"].ap()[:, 0:1]: e.tensor_scalar(o, i, s, None, ALU.mult))
        vq = E("dv", lambda e, o=xq_lo.ap().rearrange("p (c u t) -> p c u t", c=DC, u=NTT)[:, :, tt, :],
               a=qw.ap().rearrange("p (c t) -> p c t", c=DC),
               s=sb["qcol"].ap()[:, 0:1],
               b=xq_hi.ap().rearrange("p (c u t) -> p c u t", c=DC, u=NTT)[:, :, tt, :]:
               e.scalar_tensor_tensor(o, a, s, b, ALU.mult, ALU.subtract), sDV)
        va_q[par] = vq
    for b_ in x1w:
        b_.release()
    qw.release()
    xq_done = sDV.n
    phase_barrier()

    wrow = Buf(af, 1024)
    wu_s = Buf(af, 1024)
    vd_wprep = 0
    vpe_wt = 0

    def ffn_wprep_section(wname, oc, sec, dst_hi, dst_lo, dst_off_fn):
        nonlocal vd_wprep, vpe_wt
        nb = 1024 // 32
        W("sy", sDV, vd_wprep)
        v = DMA("sy", wrow.ap(), hbm[wname][oc * 128:(oc + 1) * 128,
                                            sec * 1024:(sec + 1) * 1024])
        W("dv", sW, v)
        vf = E("dv", lambda e, o=sb["wbs"].ap()[:, 0:nb].unsqueeze(2),
               i=wrow.ap().rearrange("p (b i) -> p b i", b=nb):
               e.tensor_reduce(o, i, AX.X, ALU.max, apply_absolute_value=True), sDV)
        W("dv", sDV, vf)
        vf = E("dv", lambda e, o=sb["wbs"].ap()[:, 0:nb], i=sb["wbs"].ap()[:, 0:nb]:
               e.tensor_scalar(o, i, 1.0 / 7.0, 1e-12, ALU.mult, ALU.max), sDV)
        W("dv", sDV, vf)
        vf = E("dv", lambda e, o=sb["wbr"].ap()[:, 0:nb], i=sb["wbs"].ap()[:, 0:nb]:
               e.reciprocal(o, i), sDV)
        W("dv", sDV, vf)
        W("dv", sPE, vpe_wt)
        vd = E("dv", lambda e, o=wu_s.ap().rearrange("p (b i) -> p b i", b=nb),
               a=wrow.ap().rearrange("p (b i) -> p b i", b=nb),
               b=sb["wbr"].ap()[:, 0:nb].unsqueeze(2).broadcast_to([128, nb, 32]):
               e.tensor_tensor(o, a, b, ALU.mult), sDV)
        vd_wprep = vd
        W("ac", sDV, vd)
        va = E("ac", lambda e, o=wu_s.ap(), i=wu_s.ap(), c=sb["rndc"].ap()[:, 0:1]:
               e.activation(o, i, AF.Identity, bias=c), sAC)
        W("dv", sAC, va)
        E("dv", lambda e, o=wu_s.ap(), i=wu_s.ap():
          e.tensor_scalar(o, i, -RND_C, 7.0, ALU.add, ALU.min))
        E("dv", lambda e, o=wu_s.ap(), i=wu_s.ap():
          e.tensor_scalar(o, i, -8.0, None, ALU.max))
        vd = E("dv", lambda e, o=wu_s.ap().rearrange("p (b i) -> p b i", b=nb),
               a=wu_s.ap().rearrange("p (b i) -> p b i", b=nb),
               b=sb["wbs"].ap()[:, 0:nb].unsqueeze(2).broadcast_to([128, nb, 32]):
               e.tensor_tensor(o, a, b, ALU.mult), sDV)
        W("pe", sDV, vd)
        for kcl in range(8):
            bank = 2 + (kcl % 2)
            bank_wait(bank)
            vpe = E("pe", lambda e, o=psum[bank].ap()[:, 0:128],
                    i=wu_s.ap(kcl * 128, 128), idn=sb["ident"].ap():
                    e.transpose(o, i, idn), sPE)
            W("ac", sPE, vpe)
            va = E("ac", lambda e, o=dst_hi.ap(dst_off_fn(kcl), 128),
                   i=psum[bank].ap()[:, 0:128]: e.copy(o, i), sAC)
            W("dv", sAC, va)
            vdl = E("dv", lambda e, o=dst_lo.ap(dst_off_fn(kcl), 128),
                    a=psum[bank].ap()[:, 0:128], b=dst_hi.ap(dst_off_fn(kcl), 128):
                    e.tensor_tensor(o, a, b, ALU.subtract), sDV)
            bank_free[bank] = (sDV, vdl)
        vpe_wt = vpe
        return vdl

    wqh = Buf(ab, KC * 128)
    wql = Buf(ab, KC * 128)
    up_ev = Buf(af, TTL)
    absmax_init(1)
    absmax_init(2)
    vpe_b = 0
    for wname, spname, rcol in (("w_up", "sp_up", 1), ("w_gate", "sp_gate", 2)):
        for oc in range(IC):
            vprep = ffn_wprep_section(wname, oc, 0, wqh, wql, lambda kcl: kcl * 128)
            W("pe", sDV, vprep)
            if oc == 0:
                W("pe", sDV, xq_done)
            for tt in range(NTT):
                bank = tt % 2
                bank_wait(bank)
                vpe_b = mm3(bank, KC,
                            lambda kc: (wqh.ap(kc * 128, 128), wql.ap(kc * 128, 128)),
                            lambda kc, tt=tt: (xq_hi.ap(kc * T + tt * TTL, TTL),
                                               xq_lo.ap(kc * T + tt * TTL, TTL)))
                W("ac", sPE, vpe_b)
                va2 = E("ac", lambda e, o=up_ev.ap(), i=psum[bank].ap()[:, 0:TTL]: e.copy(o, i), sAC)
                W("dv", sAC, va2)
                absmax_acc(up_ev.ap(), rcol)
                vd = E("dv", lambda e: e.engine_nop(), sDV)
                bank_free[bank] = (sDV, vd)
                W("gp", sDV, vd)
                vg = DMA("gp", hbm[spname][oc, :, tt * TTL:(tt + 1) * TTL], up_ev.ap())
                W("ac", sG, vg)
    xq_hi.release()
    xq_lo.release()
    gmax_finish(1, 3)
    gmax_finish(2, 4)
    vbi = coll_max()
    vd = scale_calc(3, 5, 6, 32767.0, vbi)
    vd = scale_calc(4, 7, 8, 32767.0, vd * 0 + vbi)
    vd = col_bcast(5, 2, vd)
    vd = col_bcast(6, 3, vd)
    vd = col_bcast(7, 4, vd)
    vd = col_bcast(8, 5, vd)
    scales_ug = vd

    phase_barrier()

    for oc in range(DC):
        for sec in range(3):
            vprep = ffn_wprep_section(
                "w_down", oc, sec, wqh, wql,
                lambda kcl: kcl * 128)
            W("gp", sDV, vprep)
            DMA("gp", hbm["sp_wdn_hi"].rearrange("c k p m -> p c k m")[:, oc, sec * 8:(sec + 1) * 8, :],
                wqh.ap().rearrange("p (k m) -> p k m", k=8))
            vg = DMA("gp", hbm["sp_wdn_lo"].rearrange("c k p m -> p c k m")[:, oc, sec * 8:(sec + 1) * 8, :],
                     wql.ap().rearrange("p (k m) -> p k m", k=8))
            W("ac", sG, vg)
    wdn_spilled = sG.n
    phase_barrier()

    phase_barrier()
    gw = [Buf(af, T), Buf(af, T)]
    qgw = Buf(af, T)
    sigw = Buf(af, T)
    absmax_init(3)
    va_d = [0, 0]
    vg_d = [0, 0]
    for oc in range(IC):
        par = oc % 2
        W("sy", sAC, va_d[par])
        W("sy", sG, vg_d[par])
        vld = DMA("sy", gw[par].ap(), hbm["sp_gate"][oc])
        W("ac", sW, vld)
        if oc == 0:
            W("ac", sDV, scales_ug)
        va = E("ac", lambda e, o=gw[par].ap(), i=gw[par].ap(), r=sb["qcol"].ap()[:, 5:6],
               c=sb["rndc"].ap()[:, 0:1]:
               e.activation(o, i, AF.Identity, bias=c, scale=r), sAC)
        va_d[par] = va
        W("dv", sAC, va)
        E("dv", lambda e, o=qgw.ap(), i=gw[par].ap():
          e.tensor_scalar(o, i, -RND_C, 32767.0, ALU.add, ALU.min))
        vd = E("dv", lambda e, o=qgw.ap(), i=qgw.ap():
               e.tensor_scalar(o, i, -32768.0, None, ALU.max), sDV)
        W("ac", sDV, vd)
        E("ac", lambda e, o=sigw.ap(), i=qgw.ap(), s=sb["qcol"].ap()[:, 4:5]:
          e.activation(o, i, AF.Sigmoid, scale=s))
        va = E("ac", lambda e, o=sigw.ap(), i=sigw.ap(), c=sb["rndc"].ap()[:, 0:1]:
               e.activation(o, i, AF.Identity, bias=c, scale=65536.0), sAC)
        W("dv", sAC, va)
        E("dv", lambda e, o=sigw.ap(), i=sigw.ap():
          e.tensor_scalar(o, i, -RND_C, 65535.0, ALU.add, ALU.min))
        E("dv", lambda e, o=sigw.ap(), i=sigw.ap():
          e.tensor_scalar(o, i, 0.0, None, ALU.max))
        W("dv", sG, vg_d[par])
        E("dv", lambda e, o=gw[par].ap(), a=qgw.ap(), b=sigw.ap():
          e.tensor_tensor(o, a, b, ALU.mult))
        absmax_acc(gw[par].ap(), 3)
        vdn = E("dv", lambda e: e.engine_nop(), sDV)
        W("gp", sDV, vdn)
        vg_d[par] = DMA("gp", hbm["sp_pg"][oc], gw[par].ap())
    gmax_finish(3, 9)
    vbi = coll_max()
    Q = sb["qs"].ap()
    W("dv", sG, vbi)
    for fn in [
        lambda e, o=Q[0:1, 10:11], i=Q[0:1, 7:8]:
            e.tensor_scalar(o, i, 1.0 / 65536.0, None, ALU.mult),
        lambda e, o=Q[0:1, 11:12], a=Q[0:1, 9:10], b=Q[0:1, 10:11]:
            e.tensor_tensor(o, a, b, ALU.mult),
        lambda e, o=Q[0:1, 11:12], i=Q[0:1, 11:12]:
            e.tensor_scalar(o, i, 1.0 / 32767.0, 1e-12, ALU.mult, ALU.max),
        lambda e, o=Q[0:1, 12:13], i=Q[0:1, 11:12]: e.reciprocal(o, i),
        lambda e, o=Q[0:1, 12:13], a=Q[0:1, 10:11], b=Q[0:1, 12:13]:
            e.tensor_tensor(o, a, b, ALU.mult),
    ]:
        vd = E("dv", fn, sDV)
        W("dv", sDV, vd)
    vd = col_bcast(12, 6, vd)
    scale2_done = vd

    phase_barrier()

    phase_barrier()
    uw = [Buf(af, T), Buf(af, T)]
    quw = sigw
    absmax_init(4)
    va_f = [0, 0]
    va_fu = [0, 0]
    vg_f = [0, 0]
    for oc in range(IC):
        par = oc % 2
        W("sy", sAC, va_d[par])
        W("sy", sG, vg_d[par])
        vld1 = DMA("sy", gw[par].ap(), hbm["sp_pg"][oc])
        W("sy", sAC, va_fu[par])
        W("sy", sG, vg_f[par])
        vld2 = DMA("sy", uw[par].ap(), hbm["sp_up"][oc])
        W("ac", sW, vld1)
        if oc == 0:
            W("ac", sDV, scale2_done)
        va = E("ac", lambda e, o=gw[par].ap(), i=gw[par].ap(), r=sb["qcol"].ap()[:, 6:7],
               c=sb["rndc"].ap()[:, 0:1]:
               e.activation(o, i, AF.Identity, bias=c, scale=r), sAC)
        va_d[par] = va
        W("dv", sAC, va)
        E("dv", lambda e, o=qgw.ap(), i=gw[par].ap():
          e.tensor_scalar(o, i, -RND_C, 32767.0, ALU.add, ALU.min))
        vtt = E("dv", lambda e, o=qgw.ap(), i=qgw.ap():
                e.tensor_scalar(o, i, -32768.0, None, ALU.max), sDV)
        W("ac", sW, vld2)
        W("ac", sDV, vtt)
        va2 = E("ac", lambda e, o=uw[par].ap(), i=uw[par].ap(), r=sb["qcol"].ap()[:, 3:4],
                c=sb["rndc"].ap()[:, 0:1]:
                e.activation(o, i, AF.Identity, bias=c, scale=r), sAC)
        va_fu[par] = va2
        W("dv", sAC, va2)
        E("dv", lambda e, o=quw.ap(), i=uw[par].ap():
          e.tensor_scalar(o, i, -RND_C, 32767.0, ALU.add, ALU.min))
        E("dv", lambda e, o=quw.ap(), i=quw.ap():
          e.tensor_scalar(o, i, -32768.0, None, ALU.max))
        W("dv", sG, vg_f[par])
        E("dv", lambda e, o=uw[par].ap(), a=qgw.ap(), b=quw.ap():
          e.tensor_tensor(o, a, b, ALU.mult))
        absmax_acc(uw[par].ap(), 4)
        vdn = E("dv", lambda e: e.engine_nop(), sDV)
        W("gp", sDV, vdn)
        vg_f[par] = DMA("gp", hbm["sp_po"][oc], uw[par].ap())
    for b_ in gw + uw:
        b_.release()
    qgw.release()
    sigw.release()
    gmax_finish(4, 13)
    vbi = coll_max()
    W("dv", sG, vbi)
    for fn in [
        lambda e, o=Q[0:1, 14:15], a=Q[0:1, 11:12], b=Q[0:1, 5:6]:
            e.tensor_tensor(o, a, b, ALU.mult),
        lambda e, o=Q[0:1, 15:16], a=Q[0:1, 13:14], b=Q[0:1, 14:15]:
            e.tensor_tensor(o, a, b, ALU.mult),
        lambda e, o=Q[0:1, 15:16], i=Q[0:1, 15:16]:
            e.tensor_scalar(o, i, 1.0 / 32767.0, 1e-12, ALU.mult, ALU.max),
        lambda e, o=Q[0:1, 0:1], i=Q[0:1, 15:16]: e.reciprocal(o, i),
        lambda e, o=Q[0:1, 0:1], a=Q[0:1, 14:15], b=Q[0:1, 0:1]:
            e.tensor_tensor(o, a, b, ALU.mult),
    ]:
        vd = E("dv", fn, sDV)
        W("dv", sDV, vd)
    vd = col_bcast(0, 7, vd)
    vd = col_bcast(15, 1, vd)
    scaleo_done = vd

    phase_barrier()

    phase_barrier()
    TH = min(256, T)
    NTH = T // TH
    oq_hi = Buf(ab, IC * TH)
    oq_lo = Buf(ab, IC * TH)
    wdnb_hi = [Buf(ab, IC * 128), Buf(ab, IC * 128)]
    wdnb_lo = [Buf(ab, IC * 128), Buf(ab, IC * 128)]
    vpe_wdnb = [0, 0]
    res3 = Buf(af, DC * TH)
    x2_t = Buf(af, DC * TH)
    x1row = Buf(af, TH)
    orow = Buf(af, D)
    sqh = [Buf(af, TH), Buf(af, TH)]

    LS = max(TTL, 256)

    def emit_ln_h(src_fn, out_fn, src_ready=0):
        nonlocal vpe_sqh
        bank_wait(5)
        bank_wait(3)
        W("ac", sDV, src_ready)
        for dc in range(DC):
            par = dc % 2
            W("ac", sPE, vpe_sqh[par])
            va = E("ac", lambda e, o=sqh[par].ap(), i=src_fn(dc):
                   e.activation(o, i, AF.Square), sAC)
            W("pe", sAC, va)
            E("pe", lambda e, o=psum[5].ap()[0:1, 0:TH], a=sb["ones1"].ap()[:, 0:1],
              b=src_fn(dc), s=(dc == 0), st=(dc == DC - 1): e.matmul(o, a, b, start=s, stop=st))
            vpe = E("pe", lambda e, o=psum[3].ap()[0:1, 0:TH], a=sb["ones1"].ap()[:, 0:1],
                    b=sqh[par].ap(), s=(dc == 0), st=(dc == DC - 1):
                    e.matmul(o, a, b, start=s, stop=st), sPE)
            vpe_sqh[par] = vpe
        W("dv", sPE, vpe)
        L = sb["lns"].ap()
        E("dv", lambda e, o=L[0:1, 0:TH], i=psum[5].ap()[0:1, 0:TH]:
          e.tensor_scalar(o, i, 1.0 / D, None, ALU.mult))
        E("dv", lambda e, o=L[0:1, LS:LS + TH], i=psum[3].ap()[0:1, 0:TH]:
          e.tensor_scalar(o, i, 1.0 / D, None, ALU.mult))
        E("dv", lambda e, o=L[0:1, 2 * LS:2 * LS + TH], a=L[0:1, 0:TH], b=L[0:1, 0:TH]:
          e.tensor_tensor(o, a, b, ALU.mult))
        E("dv", lambda e, o=L[0:1, LS:LS + TH], a=L[0:1, LS:LS + TH],
          b=L[0:1, 2 * LS:2 * LS + TH]: e.tensor_tensor(o, a, b, ALU.subtract))
        vd = E("dv", lambda e, o=L[0:1, LS:LS + TH], i=L[0:1, LS:LS + TH]:
               e.tensor_scalar(o, i, LN_EPS, None, ALU.add), sDV)
        W("ac", sDV, vd)
        va = E("ac", lambda e, o=L[0:1, LS:LS + TH], i=L[0:1, LS:LS + TH]:
               e.activation(o, i, AF.Sqrt), sAC)
        W("dv", sAC, va)
        vf = E("dv", lambda e, o=L[0:1, 2 * LS:2 * LS + TH], i=L[0:1, LS:LS + TH]:
               e.reciprocal(o, i), sDV)
        W("dv", sDV, vf)
        vd = E("dv", lambda e, o=L[0:1, 3 * LS:3 * LS + TH], a=L[0:1, 0:TH],
               b=L[0:1, 2 * LS:2 * LS + TH]: e.tensor_tensor(o, a, b, ALU.mult), sDV)
        bank_wait(4)
        W("pe", sDV, vd)
        E("pe", lambda e, o=psum[4].ap()[:, 0:TH], a=sb["ones1"].ap()[0:1, :],
          b=L[0:1, 2 * LS:2 * LS + TH]: e.matmul(o, a, b, start=True, stop=True))
        vpe = E("pe", lambda e, o=psum[3].ap()[:, 0:TH], a=sb["ones1"].ap()[0:1, :],
                b=L[0:1, 3 * LS:3 * LS + TH]: e.matmul(o, a, b, start=True, stop=True), sPE)
        W("dv", sPE, vpe)
        for dc in range(DC):
            E("dv", lambda e, o=out_fn(dc), a=src_fn(dc), b=psum[4].ap()[:, 0:TH]:
              e.tensor_tensor(o, a, b, ALU.mult))
            vd = E("dv", lambda e, o=out_fn(dc), a=out_fn(dc), b=psum[3].ap()[:, 0:TH]:
                   e.tensor_tensor(o, a, b, ALU.subtract), sDV)
        bank_free[3] = (sDV, vd)
        bank_free[4] = (sDV, vd)
        bank_free[5] = (sDV, vd)
        return vd

    vpe_sqh = [0, 0]
    va_h = 0
    vst = 0
    IC2 = IC // 2
    pw = Buf(af, IC2 * TH)
    qhw = Buf(af, IC2 * TH)
    for th in range(NTH):
        hs0 = th * TH
        for half in range(2):
            o0 = half * IC2
            W("sy", sDV, va_h)
            vld = DMA("sy", pw.ap().rearrange("p (c t) -> p c t", c=IC2),
                      hbm["sp_po"].rearrange("c p t -> p c t")[:, o0:o0 + IC2, hs0:hs0 + TH])
            W("ac", sW, vld)
            if th == 0 and half == 0:
                W("ac", sDV, scaleo_done)
            W("ac", sDV, va_h)
            va = E("ac", lambda e, o=pw.ap(), i=pw.ap(),
                   r=sb["qcol"].ap()[:, 7:8], c=sb["rndc"].ap()[:, 0:1]:
                   e.activation(o, i, AF.Identity, bias=c, scale=r), sAC)
            W("dv", sAC, va)
            E("dv", lambda e, o=qhw.ap(), i=pw.ap():
              e.tensor_scalar(o, i, -RND_C, 32767.0, ALU.add, ALU.min))
            E("dv", lambda e, o=qhw.ap(), i=qhw.ap():
              e.tensor_scalar(o, i, -32768.0, None, ALU.max))
            E("dv", lambda e, o=oq_hi.ap(o0 * TH, IC2 * TH), i=qhw.ap(),
              s=sb["qcol"].ap()[:, 1:2]: e.tensor_scalar(o, i, s, None, ALU.mult))
            vq = E("dv", lambda e, o=oq_lo.ap(o0 * TH, IC2 * TH), a=qhw.ap(),
                   s=sb["qcol"].ap()[:, 1:2], b=oq_hi.ap(o0 * TH, IC2 * TH):
                   e.scalar_tensor_tensor(o, a, s, b, ALU.mult, ALU.subtract), sDV)
            va_h = vq
        W("pe", sDV, vq)
        if th == 0:
            W("sy", sG, wdn_spilled)
        for mc in range(DC):
            bank = mc % 2
            par = mc % 2
            W("sy", sPE, vpe_wdnb[par])
            DMA("sy", wdnb_hi[par].ap().rearrange("p (k m) -> p k m", k=IC),
                hbm["sp_wdn_hi"].rearrange("c k p m -> p c k m")[:, mc, :, :])
            vwl = DMA("sy", wdnb_lo[par].ap().rearrange("p (k m) -> p k m", k=IC),
                      hbm["sp_wdn_lo"].rearrange("c k p m -> p c k m")[:, mc, :, :])
            W("pe", sW, vwl)
            bank_wait(bank)
            vpe_h = mm3(bank, IC,
                        lambda kc, par=par: (wdnb_hi[par].ap(kc * 128, 128),
                                             wdnb_lo[par].ap(kc * 128, 128)),
                        lambda kc: (oq_hi.ap(kc * TH, TH), oq_lo.ap(kc * TH, TH)),
                        nw=TH)
            vpe_wdnb[par] = vpe_h
            vldx = DMA("sy", x1row.ap(), hbm["sp_x1"][mc, :, hs0:hs0 + TH])
            W("dv", sPE, vpe_h)
            W("dv", sW, vldx)
            vd = E("dv", lambda e, o=res3.ap(mc * TH, TH), a=psum[bank].ap()[:, 0:TH],
                   b=x1row.ap(): e.tensor_tensor(o, a, b, ALU.add), sDV)
            bank_free[bank] = (sDV, vd)
            W("sy", sDV, vd)
        emit_ln_h(lambda dc: res3.ap(dc * TH, TH), lambda dc: x2_t.ap(dc * TH, TH),
                  src_ready=vd)
        vdl = sDV.n
        W("pe", sDV, vdl)
        for j in range(TH // 128):
            for dc in range(DC):
                bank = 6 + (dc % 2)
                bank_wait(bank)
                vpe = E("pe", lambda e, o=psum[bank].ap()[:, 0:128],
                        i=x2_t.ap(dc * TH + j * 128, 128), idn=sb["ident"].ap():
                        e.transpose(o, i, idn), sPE)
                W("ac", sPE, vpe)
                W("ac", sW, vst)
                va = E("ac", lambda e, o=orow.ap(dc * 128, 128),
                       i=psum[bank].ap()[:, 0:128]: e.copy(o, i), sAC)
                bank_free[bank] = (sAC, va)
            W("sy", sAC, va)
            trow = th * (TH // 128) + j
            vst = DMA("sy", hbm["out"][trow * 128:(trow + 1) * 128, :], orow.ap())

    ops["sy"].append(lambda e, v=sW.n: e.wait_ge(sW.h, v))
    ops["gp"].append(lambda e, v=sG.n: e.wait_ge(sG.h, v))

    with nc.Block() as block:
        @block.sync
        def _(eng):
            for f in ops["sy"]:
                f(eng)

        @block.tensor
        def _(eng):
            for f in ops["pe"]:
                f(eng)

        @block.vector
        def _(eng):
            for f in ops["dv"]:
                f(eng)

        @block.scalar
        def _(eng):
            for f in ops["ac"]:
                f(eng)

        @block.gpsimd
        def _(eng):
            for f in ops["gp"]:
                f(eng)

    es.close()
    return nc


def col_layout(v, DC):
    return np.ascontiguousarray(np.asarray(v, np.float32).reshape(DC, 128).T)


def kernel(L0_attn_x_in, L0_attn_W_in, L0_attn_W_x, L0_attn_W_dt, L0_attn_b_dt,
           L0_attn_A_log, L0_attn_D_param, L0_attn_W_out, L0_attn_ln_g,
           L0_attn_ln_b, L0_ffn_w_gate, L0_ffn_w_up, L0_ffn_w_down):
    cfg = Cfg()
    x = np.ascontiguousarray(np.asarray(L0_attn_x_in, dtype=np.float32))
    Bb, L, D = x.shape
    T, N, DC = cfg.T, cfg.N, cfg.DC

    sel = np.zeros((2 * N, 2 * N * 128), dtype=np.float32)
    for j in range(2 * N):
        sel[j, j * 128:(j + 1) * 128] = 1.0
    base = {
        "W_in": np.ascontiguousarray(L0_attn_W_in, dtype=np.float32),
        "W_x": np.ascontiguousarray(L0_attn_W_x, dtype=np.float32),
        "W_dt": np.ascontiguousarray(L0_attn_W_dt, dtype=np.float32),
        "W_out": np.ascontiguousarray(L0_attn_W_out, dtype=np.float32),
        "w_up": np.ascontiguousarray(L0_ffn_w_up, dtype=np.float32),
        "w_gate": np.ascontiguousarray(L0_ffn_w_gate, dtype=np.float32),
        "w_down": np.ascontiguousarray(L0_ffn_w_down, dtype=np.float32),
        "bdt_col": col_layout(L0_attn_b_dt, DC),
        "alog_col": np.ascontiguousarray(
            np.asarray(L0_attn_A_log, np.float32).reshape(DC, 128, N)
            .transpose(1, 0, 2).reshape(128, DC * N)),
        "dpar_col": col_layout(L0_attn_D_param, DC),
        "lng_col": col_layout(L0_attn_ln_g, DC),
        "lnb_col": col_layout(L0_attn_ln_b, DC),
        "ident": np.eye(128, dtype=np.float32),
        "ones1": np.ones((128, 128), dtype=np.float32),
        "identb": np.eye(128, dtype=ml_dtypes.bfloat16),
        "sel": sel.astype(ml_dtypes.bfloat16),
    }
    in_maps = []
    for c in range(8):
        b, h = c // 2, c % 2
        im = dict(base)
        im["x_in"] = np.ascontiguousarray(x[b, h * T:(h + 1) * T, :])
        mask = np.zeros((128, 2), np.float32)
        mask[:, 0] = 1.0 if h == 0 else 0.0
        mask[:, 1] = 1.0 if h == 1 else 0.0
        im["mask_col"] = mask
        in_maps.append(im)

    nc = build_nc(cfg)
    res = run_bass_kernel_spmd(nc, in_maps, core_ids=list(range(8)))
    out = np.empty((Bb, L, D), np.float32)
    for c in range(8):
        b, h = c // 2, c % 2
        out[b, h * T:(h + 1) * T, :] = res.results[c]["out"]
    return out


# version 08
# speedup vs baseline: 1.4817x; 1.1685x over previous
import numpy as np
import ml_dtypes
import concourse.bass as bass
import concourse.mybir as mybir
from concourse.bass_utils import run_bass_kernel_spmd

F32 = mybir.dt.float32
BF16 = mybir.dt.bfloat16
AF = mybir.ActivationFunctionType
ALU = mybir.AluOpType
AX = mybir.AxisListType

RND_C = float(1.5 * 2 ** 23)
LN_EPS = 1e-5


class Cfg:
    def __init__(self, T=2048, D=1024, N=16, DI=3072, TTL=512):
        self.T, self.D, self.N, self.DI, self.TTL = T, D, N, DI, TTL
        self.NTT = T // TTL
        self.DC = D // 128
        self.IC = DI // 128
        self.KC = D // 128


class Sem:
    def __init__(self, es, nc, name):
        self.h = es.enter_context(nc.semaphore(name))
        self.n = 0

    def inc(self, k=1):
        self.n += k
        return self.n


class Arena:
    def __init__(self, nc, name, cols, dtype):
        self.t = nc.alloc_sbuf_tensor(name, [128, cols], dtype)
        self.free_list = [(0, cols)]

    def alloc(self, n, align=8):
        for i, (s, e) in enumerate(self.free_list):
            a = (s + align - 1) // align * align
            if a + n <= e:
                segs = []
                if a > s:
                    segs.append((s, a))
                if a + n < e:
                    segs.append((a + n, e))
                self.free_list[i:i + 1] = segs
                return a
        raise RuntimeError(f"arena OOM: need {n}, free={self.free_list}")

    def free(self, off, n):
        self.free_list.append((off, off + n))
        self.free_list.sort()
        m = []
        for s, e in self.free_list:
            if m and m[-1][1] >= s:
                m[-1] = (m[-1][0], max(m[-1][1], e))
            else:
                m.append((s, e))
        self.free_list = [(s, e) for s, e in m]


class Buf:
    def __init__(self, arena, cols, align=8):
        self.arena = arena
        self.off = arena.alloc(cols, align)
        self.cols = cols

    def ap(self, c0=0, n=None, rows=128, row0=0):
        if n is None:
            n = self.cols - c0
        return self.arena.t.ap()[row0:row0 + rows, self.off + c0:self.off + c0 + n]

    def release(self):
        self.arena.free(self.off, self.cols)


def build_nc(cfg, dbg_spills=()):
    import contextlib
    es = contextlib.ExitStack()
    nc = bass.Bass("TRN2", target_bir_lowering=False, debug=False,
                   detect_race_conditions=False)
    T, D, N, DI, TTL = cfg.T, cfg.D, cfg.N, cfg.DI, cfg.TTL
    DC, IC, KC, NTT = cfg.DC, cfg.IC, cfg.KC, cfg.NTT
    TJ = TTL // 128

    hbm = {}
    hbm["x_in"] = nc.dram_tensor("x_in", [T, D], F32, kind="ExternalInput").ap()
    for wn, shape in [("W_in", [D, 2 * D]), ("W_x", [D, D + 2 * N]),
                      ("W_dt", [D, D]), ("W_out", [D, D]),
                      ("w_up", [DI, D]), ("w_gate", [DI, D]), ("w_down", [D, DI])]:
        hbm[wn] = nc.dram_tensor(wn, shape, F32, kind="ExternalInput").ap()
    for cn, shape, dt in [("ident", [128, 128], F32), ("ones1", [128, 128], F32),
                          ("identb", [128, 128], BF16),
                          ("sel", [2 * N, 2 * N * 128], BF16),
                          ("bdt_col", [128, DC], F32), ("alog_col", [128, DC * N], F32),
                          ("dpar_col", [128, DC], F32), ("lng_col", [128, DC], F32),
                          ("lnb_col", [128, DC], F32), ("mask_col", [128, 2], F32)]:
        hbm[cn] = nc.dram_tensor(cn, shape, dt, kind="ExternalInput").ap()
    hbm["out"] = nc.dram_tensor("out", [T, D], F32, kind="ExternalOutput").ap()

    for k, shape, dt in [
        ("sp_xt_f32", [DC, 128, T], F32), ("sp_xt_hi", [DC, 128, T], BF16),
        ("sp_xt_lo", [DC, 128, T], BF16),
        ("sp_x_f32", [DC, 128, T], F32), ("sp_x_hi", [DC, 128, T], BF16),
        ("sp_x_lo", [DC, 128, T], BF16),
        ("sp_sz", [DC, 128, T], F32),
        ("sp_dr_hi", [DC, 128, T], BF16), ("sp_dr_lo", [DC, 128, T], BF16),
        ("sp_S", [DC, 128, T], F32), ("sp_y", [DC, 128, T], F32),
        ("sp_x1", [DC, 128, T], F32),
        ("sp_up", [IC, 128, T], F32), ("sp_gate", [IC, 128, T], F32),
        ("sp_wdn_hi", [DC, IC, 128, 128], BF16), ("sp_wdn_lo", [DC, IC, 128, 128], BF16),
        ("sp_pg", [IC, 128, T], F32), ("sp_po", [IC, 128, T], F32),
    ]:
        kind = "ExternalOutput" if k in dbg_spills else "Internal"
        hbm[k] = nc.dram_tensor(k, list(shape), dt, kind=kind).ap()

    cc_carry_in = nc.dram_tensor("cc_carry_in", [128, N * DC], F32)
    cc_carry_out = nc.dram_tensor("cc_carry_out", [128, N * DC], F32)
    cc_max_in = nc.dram_tensor("cc_max_in", [1, 16], F32)
    cc_max_out = nc.dram_tensor("cc_max_out", [1, 16], F32)

    sb = {}
    for cn, shape, dt in [("ident", [128, 128], F32), ("ones1", [128, 128], F32),
                          ("identb", [128, 128], BF16),
                          ("sel", [2 * N, 2 * N * 128], BF16),
                          ("bdt_col", [128, DC], F32), ("alog_col", [128, DC * N], F32),
                          ("dpar_col", [128, DC], F32), ("lng_col", [128, DC], F32),
                          ("lnb_col", [128, DC], F32), ("mask_col", [128, 2], F32),
                          ("A", [128, DC * N], F32), ("zeros", [128, TTL], F32),
                          ("rndc", [128, 1], F32), ("state", [128, N * DC], F32),
                          ("scar", [128, DC], F32),
                          ("bch", [2 * N, T], BF16), ("bcl", [2 * N, T], BF16),
                          ("lns", [1, 4 * max(TTL, 256)], F32),
                          ("red", [128, 8], F32), ("redt", [128, 1], F32),
                          ("qs", [1, 16], F32), ("qcol", [128, 8], F32),
                          ("wbs", [128, 64], F32), ("wbr", [128, 64], F32)]:
        sb[cn] = nc.alloc_sbuf_tensor("sb_" + cn, shape, dt)

    psum = [es.enter_context(nc.psum_tensor(f"pb{i}", [128, 512], F32)) for i in range(8)]

    af = Arena(nc, "af32", 17600, F32)
    ab = Arena(nc, "abf16", 45000, BF16)

    ops = {"pe": [], "dv": [], "ac": [], "gp": [], "sy": []}
    sW = Sem(es, nc, "sW")
    sG = Sem(es, nc, "sG")
    sPE = Sem(es, nc, "sPE")
    sDV = Sem(es, nc, "sDV")
    sAC = Sem(es, nc, "sAC")
    sCC = Sem(es, nc, "sCC")

    def E(q, fn, sem=None, inc=1):
        if sem is None:
            ops[q].append(fn)
            return None
        ops[q].append(lambda e, f=fn, h=sem.h, k=inc: f(e).then_inc(h, k))
        return sem.inc(inc)

    def W(q, sem, v):
        if v and v > 0:
            ops[q].append(lambda e, h=sem.h, v=v: e.wait_ge(h, v))

    def DMA(q, dst, src):
        sem = sW if q == "sy" else sG
        ops[q].append(lambda e, d=dst, s=src: e.dma_start(out=d, in_=s).then_inc(sem.h, 16))
        return sem.inc(16)

    bank_free = [None] * 8

    def bank_wait(j):
        if bank_free[j] is not None:
            sem, v = bank_free[j]
            W("pe", sem, v)

    def phase_barrier():
        for q in ("sy", "pe", "dv", "ac", "gp"):
            for sem in (sW, sG, sPE, sDV, sAC, sCC):
                W(q, sem, sem.n)

    for name in ["ident", "ones1", "identb", "sel", "bdt_col", "alog_col",
                 "dpar_col", "lng_col", "lnb_col", "mask_col"]:
        DMA("sy", sb[name].ap(), hbm[name])
    w_consts = sW.n

    E("dv", lambda e: e.memset(sb["zeros"].ap(), 0.0))
    E("dv", lambda e: e.memset(sb["rndc"].ap(), RND_C))
    E("dv", lambda e: e.memset(sb["state"].ap(), 0.0))
    E("dv", lambda e: e.memset(sb["scar"].ap(), 0.0))
    E("dv", lambda e: e.memset(sb["qs"].ap(), 0.0))
    E("dv", lambda e: e.memset(sb["red"].ap(), 0.0), sDV)
    W("ac", sW, w_consts)
    va = E("ac", lambda e: e.activation(sb["A"].ap(), sb["alog_col"].ap(), AF.Exp), sAC)
    W("dv", sAC, va)
    E("dv", lambda e: e.tensor_scalar(sb["A"].ap(), sb["A"].ap(), -1.0, None, ALU.mult), sDV)

    def load_split_weight(wname, rows, cols, f32buf, hibuf, lobuf):
        nkc = rows // 128
        v = DMA("sy", f32buf.ap().rearrange("p (k m) -> p k m", k=nkc),
                hbm[wname].rearrange("(k p) m -> p k m", p=128))
        W("ac", sW, v)
        va = E("ac", lambda e, o=hibuf.ap(), i=f32buf.ap(): e.copy(o, i), sAC)
        W("dv", sAC, va)
        vd = E("dv", lambda e, o=lobuf.ap(), a=f32buf.ap(), b=hibuf.ap():
               e.tensor_tensor(o, a, b, ALU.subtract), sDV)
        W("sy", sDV, vd)
        W("gp", sDV, vd)
        return vd

    def mm3(bank, nkc, get_w, get_x, mw=128, nw=None):
        if nw is None:
            nw = TTL
        for kc in range(nkc):
            wh, wl = get_w(kc)
            xh, xl = get_x(kc)
            E("pe", lambda e, o=psum[bank].ap()[0:mw, 0:nw], a=wh, b=xh, s=(kc == 0):
              e.matmul(o, a, b, start=s, stop=False))
            E("pe", lambda e, o=psum[bank].ap()[0:mw, 0:nw], a=wh, b=xl:
              e.matmul(o, a, b, start=False, stop=False))
            v = E("pe", lambda e, o=psum[bank].ap()[0:mw, 0:nw], a=wl, b=xh,
                  st=(kc == nkc - 1): e.matmul(o, a, b, start=False, stop=st),
                  sPE if kc == nkc - 1 else None)
        return v

    w_in_f32 = Buf(af, 0) if False else None
    xt_f = Buf(af, DC * TTL)
    xt_h = Buf(ab, DC * TTL)
    xt_l = Buf(ab, DC * TTL)
    xrow = [Buf(af, D), Buf(af, D)]
    W("pe", sW, w_consts)
    vpe_row = [0, 0]
    va_evac = 0
    for tt in range(NTT):
        for j in range(TJ):
            par = j % 2
            trow = tt * TJ + j
            W("sy", sPE, vpe_row[par])
            vld = DMA("sy", xrow[par].ap(), hbm["x_in"][trow * 128:(trow + 1) * 128, :])
            W("pe", sW, vld)
            for dc in range(DC):
                bank = dc % 4
                bank_wait(bank)
                vpe = E("pe", lambda e, o=psum[bank].ap()[:, 0:128],
                        i=xrow[par].ap(dc * 128, 128), idn=sb["ident"].ap():
                        e.transpose(o, i, idn), sPE)
                W("ac", sPE, vpe)
                va = E("ac", lambda e, o=xt_f.ap(dc * TTL + j * 128, 128),
                       i=psum[bank].ap()[:, 0:128]: e.copy(o, i), sAC)
                bank_free[bank] = (sAC, va)
            vpe_row[par] = vpe
            va_evac = va
        W("dv", sAC, va_evac)
        E("dv", lambda e, o=xt_h.ap(), i=xt_f.ap(): e.tensor_copy(o, i))
        vd = E("dv", lambda e, o=xt_l.ap(), a=xt_f.ap(), b=xt_h.ap():
               e.tensor_tensor(o, a, b, ALU.subtract), sDV)
        W("gp", sDV, vd)
        DMA("gp", hbm["sp_xt_f32"].rearrange("c p t -> p c t")[:, :, tt * TTL:(tt + 1) * TTL],
            xt_f.ap().rearrange("p (c t) -> p c t", c=DC))
        DMA("gp", hbm["sp_xt_hi"].rearrange("c p t -> p c t")[:, :, tt * TTL:(tt + 1) * TTL],
            xt_h.ap().rearrange("p (c t) -> p c t", c=DC))
        vg = DMA("gp", hbm["sp_xt_lo"].rearrange("c p t -> p c t")[:, :, tt * TTL:(tt + 1) * TTL],
                 xt_l.ap().rearrange("p (c t) -> p c t", c=DC))
        W("ac", sG, vg)
        W("dv", sG, vg)
    for b_ in xrow:
        b_.release()
    xt_f.release()
    xt_h.release()
    xt_l.release()
    spill_xt = sG.n
    phase_barrier()

    wf = Buf(af, 0) if False else None
    w_f = Buf(af, KC * 128)
    w_f.release()

    w1f = Buf(af, 0) if False else None
    w1_f32 = Buf(af, 0) if False else None

    win_f = Buf(af, KC * 2 * D // 4) if False else None
    win_hi = Buf(ab, KC * 2 * D)
    win_lo = Buf(ab, KC * 2 * D)
    wstage = Buf(af, KC * D)
    for half in range(2):
        v = DMA("sy", wstage.ap().rearrange("p (k m) -> p k m", k=KC),
                hbm["W_in"][:, half * D:(half + 1) * D].rearrange("(k p) m -> p k m", p=128))
        W("ac", sW, v)
        va = E("ac", lambda e, o=win_hi.ap() if False else None: None) if False else None
        for kc in range(KC):
            va = E("ac", lambda e, o=win_hi.ap(kc * 2 * D + half * D, D),
                   i=wstage.ap(kc * D, D): e.copy(o, i), sAC)
        W("dv", sAC, va)
        for kc in range(KC):
            vd = E("dv", lambda e, o=win_lo.ap(kc * 2 * D + half * D, D),
                   a=wstage.ap(kc * D, D), b=win_hi.ap(kc * 2 * D + half * D, D):
                   e.tensor_tensor(o, a, b, ALU.subtract), sDV)
        W("sy", sDV, vd)
    v_win = vd
    W("gp", sDV, vd)
    wstage.release()

    xth = Buf(ab, DC * TTL)
    xtl = Buf(ab, DC * TTL)
    evf = [Buf(af, TTL), Buf(af, TTL)]
    evh = [Buf(ab, TTL), Buf(ab, TTL)]
    evl = [Buf(ab, TTL), Buf(ab, TTL)]
    vg_par = [0, 0]
    W("pe", sDV, v_win)
    W("sy", sG, spill_xt)
    vpe_tt = 0
    for tt in range(NTT):
        W("sy", sPE, vpe_tt)
        DMA("sy", xth.ap().rearrange("p (c t) -> p c t", c=DC),
            hbm["sp_xt_hi"].rearrange("c p t -> p c t")[:, :, tt * TTL:(tt + 1) * TTL])
        vld = DMA("sy", xtl.ap().rearrange("p (c t) -> p c t", c=DC),
                  hbm["sp_xt_lo"].rearrange("c p t -> p c t")[:, :, tt * TTL:(tt + 1) * TTL])
        W("pe", sW, vld)
        for mc in range(2 * DC):
            bank = mc % 2
            bank_wait(bank)
            vpe_tt = mm3(bank, KC,
                         lambda kc, mc=mc: (win_hi.ap(kc * 2 * D + mc * 128, 128),
                                            win_lo.ap(kc * 2 * D + mc * 128, 128)),
                         lambda kc: (xth.ap(kc * TTL, TTL), xtl.ap(kc * TTL, TTL)))
            W("ac", sPE, vpe_tt)
            par = mc % 2
            W("ac", sG, vg_par[par])
            W("dv", sG, vg_par[par])
            if mc < DC:
                va = E("ac", lambda e, o=evf[par].ap(), i=psum[bank].ap()[:, 0:TTL]: e.copy(o, i), sAC)
                W("dv", sAC, va)
                E("dv", lambda e, o=evh[par].ap(), i=evf[par].ap(): e.tensor_copy(o, i))
                vd = E("dv", lambda e, o=evl[par].ap(), a=evf[par].ap(), b=evh[par].ap():
                       e.tensor_tensor(o, a, b, ALU.subtract), sDV)
                bank_free[bank] = (sAC, va)
                W("gp", sDV, vd)
                DMA("gp", hbm["sp_x_f32"][mc, :, tt * TTL:(tt + 1) * TTL], evf[par].ap())
                DMA("gp", hbm["sp_x_hi"][mc, :, tt * TTL:(tt + 1) * TTL], evh[par].ap())
                vg_par[par] = DMA("gp", hbm["sp_x_lo"][mc, :, tt * TTL:(tt + 1) * TTL], evl[par].ap())
            else:
                va = E("ac", lambda e, o=evf[par].ap(), i=psum[bank].ap()[:, 0:TTL]:
                       e.activation(o, i, AF.Silu), sAC)
                bank_free[bank] = (sAC, va)
                W("gp", sAC, va)
                vg_par[par] = DMA("gp", hbm["sp_sz"][mc - DC, :, tt * TTL:(tt + 1) * TTL], evf[par].ap())
    win_hi.release()
    win_lo.release()
    spill_p1 = sG.n
    phase_barrier()

    MX = D + 2 * N
    wx_f32 = Buf(af, KC * MX)
    wx_hi = Buf(ab, KC * MX)
    wx_lo = Buf(ab, KC * MX)
    v_wx = load_split_weight("W_x", D, MX, wx_f32, wx_hi, wx_lo)
    wx_f32.release()
    bcf = Buf(af, T)
    W("pe", sDV, v_wx)
    W("sy", sG, spill_p1)
    vpe_tt = 0
    for tt in range(NTT):
        W("sy", sPE, vpe_tt)
        DMA("sy", xth.ap().rearrange("p (c t) -> p c t", c=DC),
            hbm["sp_x_hi"].rearrange("c p t -> p c t")[:, :, tt * TTL:(tt + 1) * TTL])
        vld = DMA("sy", xtl.ap().rearrange("p (c t) -> p c t", c=DC),
                  hbm["sp_x_lo"].rearrange("c p t -> p c t")[:, :, tt * TTL:(tt + 1) * TTL])
        W("pe", sW, vld)
        for mc in range(DC + 1):
            bank = mc % 2
            mw = 128 if mc < DC else 2 * N
            bank_wait(bank)
            vpe_tt = mm3(bank, KC,
                         lambda kc, mc=mc, mw=mw: (wx_hi.ap(kc * MX + mc * 128, mw),
                                                   wx_lo.ap(kc * MX + mc * 128, mw)),
                         lambda kc: (xth.ap(kc * TTL, TTL), xtl.ap(kc * TTL, TTL)),
                         mw=mw)
            W("ac", sPE, vpe_tt)
            par = mc % 2
            if mc < DC:
                W("ac", sG, vg_par[par])
                W("dv", sG, vg_par[par])
                va = E("ac", lambda e, o=evh[par].ap(), i=psum[bank].ap()[:, 0:TTL]: e.copy(o, i), sAC)
                W("dv", sAC, va)
                W("dv", sPE, vpe_tt)
                vd = E("dv", lambda e, o=evl[par].ap(), a=psum[bank].ap()[:, 0:TTL], b=evh[par].ap():
                       e.tensor_tensor(o, a, b, ALU.subtract), sDV)
                bank_free[bank] = (sDV, vd)
                W("gp", sDV, vd)
                DMA("gp", hbm["sp_dr_hi"][mc, :, tt * TTL:(tt + 1) * TTL], evh[par].ap())
                vg_par[par] = DMA("gp", hbm["sp_dr_lo"][mc, :, tt * TTL:(tt + 1) * TTL], evl[par].ap())
            else:
                va = E("ac", lambda e, o=bcf.ap(tt * TTL, TTL, rows=2 * N),
                       i=psum[bank].ap()[0:2 * N, 0:TTL]: e.copy(o, i), sAC)
                W("dv", sAC, va)
                E("dv", lambda e, o=sb["bch"].ap()[:, tt * TTL:(tt + 1) * TTL],
                  i=bcf.ap(tt * TTL, TTL, rows=2 * N): e.tensor_copy(o, i))
                vd = E("dv", lambda e, o=sb["bcl"].ap()[:, tt * TTL:(tt + 1) * TTL],
                       a=bcf.ap(tt * TTL, TTL, rows=2 * N),
                       b=sb["bch"].ap()[:, tt * TTL:(tt + 1) * TTL]:
                       e.tensor_tensor(o, a, b, ALU.subtract), sDV)
                bank_free[bank] = (sDV, vd)
    xth.release()
    xtl.release()
    wx_hi.release()
    wx_lo.release()
    bcf.release()
    bc_ready = sDV.n
    spill_p2 = sG.n
    phase_barrier()

    wdt_f32 = Buf(af, KC * D)
    wdt_hi = Buf(ab, KC * D)
    wdt_lo = Buf(ab, KC * D)
    v_wdt = load_split_weight("W_dt", D, D, wdt_f32, wdt_hi, wdt_lo)
    wdt_f32.release()

    drh = Buf(ab, DC * TTL)
    drl = Buf(ab, DC * TTL)
    delta = Buf(af, DC * TTL)
    dx = Buf(af, DC * TTL)
    y_tt = Buf(af, DC * TTL)
    s_tc = Buf(af, TTL)
    x_rl = Buf(af, TTL)
    e_tmp = Buf(af, TTL)
    dA_t = [Buf(af, TTL), Buf(af, TTL)]
    u_t = Buf(af, TTL)
    h_t = Buf(af, TTL)
    hc_t = Buf(af, TTL)

    W("pe", sDV, v_wdt)
    W("sy", sG, spill_p2)
    vpe_tt = 0
    vd_scan = [0, 0]
    for tt in range(NTT):
        ts0 = tt * TTL
        W("sy", sPE, vpe_tt)
        DMA("sy", drh.ap().rearrange("p (c t) -> p c t", c=DC),
            hbm["sp_dr_hi"].rearrange("c p t -> p c t")[:, :, ts0:ts0 + TTL])
        vld = DMA("sy", drl.ap().rearrange("p (c t) -> p c t", c=DC),
                  hbm["sp_dr_lo"].rearrange("c p t -> p c t")[:, :, ts0:ts0 + TTL])
        W("pe", sW, vld)
        for dc in range(DC):
            bank = dc % 2
            bank_wait(bank)
            vpe_tt = mm3(bank, KC,
                         lambda kc, dc=dc: (wdt_hi.ap(kc * D + dc * 128, 128),
                                            wdt_lo.ap(kc * D + dc * 128, 128)),
                         lambda kc: (drh.ap(kc * TTL, TTL), drl.ap(kc * TTL, TTL)))
            W("ac", sPE, vpe_tt)
            E("ac", lambda e, o=e_tmp.ap(), i=psum[bank].ap()[:, 0:TTL],
              b=sb["bdt_col"].ap()[:, dc:dc + 1]:
              e.activation(o, i, AF.Exp, bias=b))
            va = E("ac", lambda e, o=delta.ap(dc * TTL, TTL), i=e_tmp.ap():
                   e.activation(o, i, AF.Ln, bias=1.0), sAC)

            bank_free[bank] = (sAC, va)
            vx = DMA("sy", x_rl.ap(), hbm["sp_x_f32"][dc, :, ts0:ts0 + TTL])
            W("dv", sAC, va)
            W("dv", sW, vx)
            E("dv", lambda e, o=dx.ap(dc * TTL, TTL), a=delta.ap(dc * TTL, TTL),
              b=x_rl.ap(): e.tensor_tensor(o, a, b, ALU.mult))
            E("dv", lambda e, o=y_tt.ap(dc * TTL, TTL), a=x_rl.ap(),
              s=sb["dpar_col"].ap()[:, dc:dc + 1]:
              e.tensor_scalar(o, a, s, None, ALU.mult))
            vsc = E("dv", lambda e, o=s_tc.ap(), d0=delta.ap(dc * TTL, TTL),
                    z=sb["zeros"].ap(), ini=sb["scar"].ap()[:, dc:dc + 1]:
                    e.tensor_tensor_scan(o, d0, z, ini, ALU.add, ALU.add), sDV)
            W("dv", sDV, vsc)
            vd = E("dv", lambda e, o=sb["scar"].ap()[:, dc:dc + 1], i=s_tc.ap(TTL - 1, 1):
                   e.tensor_copy(o, i), sDV)
            W("sy", sDV, vd)
            W("gp", sDV, vd)
            vg = DMA("gp", hbm["sp_S"][dc, :, ts0:ts0 + TTL], s_tc.ap())
            W("dv", sG, vg)
        for n in range(N):
            pb, pc = 6, 7
            bank_wait(pb)
            bank_wait(pc)
            if tt == 0 and n == 0:
                W("pe", sDV, bc_ready)
            E("pe", lambda e, o=psum[pb].ap()[:, 0:TTL], a=sb["sel"].ap()[:, n * 128:(n + 1) * 128],
              b=sb["bch"].ap()[:, ts0:ts0 + TTL]: e.matmul(o, a, b, start=True, stop=False))
            E("pe", lambda e, o=psum[pb].ap()[:, 0:TTL], a=sb["sel"].ap()[:, n * 128:(n + 1) * 128],
              b=sb["bcl"].ap()[:, ts0:ts0 + TTL]: e.matmul(o, a, b, start=False, stop=True))
            E("pe", lambda e, o=psum[pc].ap()[:, 0:TTL], a=sb["sel"].ap()[:, (N + n) * 128:(N + n + 1) * 128],
              b=sb["bch"].ap()[:, ts0:ts0 + TTL]: e.matmul(o, a, b, start=True, stop=False))
            vbc = E("pe", lambda e, o=psum[pc].ap()[:, 0:TTL], a=sb["sel"].ap()[:, (N + n) * 128:(N + n + 1) * 128],
                    b=sb["bcl"].ap()[:, ts0:ts0 + TTL]: e.matmul(o, a, b, start=False, stop=True), sPE)
            for dc in range(DC):
                par = dc % 2
                W("ac", sDV, vd_scan[par])
                va = E("ac", lambda e, o=dA_t[par].ap(), i=delta.ap(dc * TTL, TTL),
                       s=sb["A"].ap()[:, dc * N + n:dc * N + n + 1]:
                       e.activation(o, i, AF.Exp, scale=s), sAC)
                W("dv", sPE, vbc)
                E("dv", lambda e, o=u_t.ap(), a=dx.ap(dc * TTL, TTL), b=psum[pb].ap()[:, 0:TTL]:
                  e.tensor_tensor(o, a, b, ALU.mult))
                W("dv", sAC, va)
                col = n * DC + dc
                vsc = E("dv", lambda e, o=h_t.ap(), d0=dA_t[par].ap(), d1=u_t.ap(),
                        ini=sb["state"].ap()[:, col:col + 1]:
                        e.tensor_tensor_scan(o, d0, d1, ini, ALU.mult, ALU.add), sDV)
                W("dv", sDV, vsc)
                E("dv", lambda e, o=sb["state"].ap()[:, col:col + 1], i=h_t.ap(TTL - 1, 1):
                  e.tensor_copy(o, i))
                E("dv", lambda e, o=hc_t.ap(), a=h_t.ap(), b=psum[pc].ap()[:, 0:TTL]:
                  e.tensor_tensor(o, a, b, ALU.mult))
                vd = E("dv", lambda e, o=y_tt.ap(dc * TTL, TTL), a=y_tt.ap(dc * TTL, TTL),
                       b=hc_t.ap(): e.tensor_tensor(o, a, b, ALU.add), sDV)
                vd_scan[par] = vd
            bank_free[pb] = (sDV, vd)
            bank_free[pc] = (sDV, vd)
        W("gp", sDV, vd)
        vg = DMA("gp", hbm["sp_y"].rearrange("c p t -> p c t")[:, :, ts0:ts0 + TTL],
                 y_tt.ap().rearrange("p (c t) -> p c t", c=DC))
        W("dv", sG, vg)
    drh.release()
    drl.release()
    wdt_hi.release()
    wdt_lo.release()
    delta.release()
    dx.release()
    s_tc.release()
    x_rl.release()
    e_tmp.release()
    u_t.release()
    h_t.release()
    hc_t.release()
    for b_ in dA_t:
        b_.release()
    y_tt.release()
    spill_y = sG.n
    phase_barrier()

    hfin = Buf(af, N * DC)
    phase_barrier()
    vd = E("dv", lambda e, o=hfin.ap(), a=sb["state"].ap(),
           s=sb["mask_col"].ap()[:, 0:1]: e.tensor_scalar(o, a, s, None, ALU.mult), sDV)
    W("gp", sDV, vd)
    vg = DMA("gp", cc_carry_in.ap(), hfin.ap())
    W("gp", sG, vg)
    vcc = E("gp", lambda e: e.collective_compute(
        "AllReduce", ALU.add,
        replica_groups=[[0, 1], [2, 3], [4, 5], [6, 7]],
        ins=[cc_carry_in.ap().opt()],
        outs=[cc_carry_out.ap().opt()]), sCC)
    W("gp", sCC, vcc)
    vg = DMA("gp", hfin.ap(), cc_carry_out.ap())
    W("dv", sG, vg)
    v_hin = E("dv", lambda e, o=hfin.ap(), a=hfin.ap(),
              s=sb["mask_col"].ap()[:, 1:2]: e.tensor_scalar(o, a, s, None, ALU.mult), sDV)

    y_c = Buf(af, DC * TTL)
    s_c = Buf(af, DC * TTL)
    p_w = [Buf(af, DC * TTL), Buf(af, DC * TTL)]
    W("dv", sDV, v_hin)
    W("sy", sG, spill_y)
    vd_cp = [0, 0]
    for tt in range(NTT):
        ts0 = tt * TTL
        DMA("sy", y_c.ap().rearrange("p (c t) -> p c t", c=DC),
            hbm["sp_y"].rearrange("c p t -> p c t")[:, :, ts0:ts0 + TTL])
        vld = DMA("sy", s_c.ap().rearrange("p (c t) -> p c t", c=DC),
                  hbm["sp_S"].rearrange("c p t -> p c t")[:, :, ts0:ts0 + TTL])
        W("ac", sW, vld)
        W("dv", sW, vld)
        for n in range(N):
            par = n % 2
            pc = 6 + par
            bank_wait(pc)
            E("pe", lambda e, o=psum[pc].ap()[:, 0:TTL], a=sb["sel"].ap()[:, (N + n) * 128:(N + n + 1) * 128],
              b=sb["bch"].ap()[:, ts0:ts0 + TTL]: e.matmul(o, a, b, start=True, stop=False))
            vbc = E("pe", lambda e, o=psum[pc].ap()[:, 0:TTL], a=sb["sel"].ap()[:, (N + n) * 128:(N + n + 1) * 128],
                    b=sb["bcl"].ap()[:, ts0:ts0 + TTL]: e.matmul(o, a, b, start=False, stop=True), sPE)
            W("ac", sDV, vd_cp[par])
            for dc in range(DC):
                va = E("ac", lambda e, o=p_w[par].ap(dc * TTL, TTL), i=s_c.ap(dc * TTL, TTL),
                       s=sb["A"].ap()[:, dc * N + n:dc * N + n + 1]:
                       e.activation(o, i, AF.Exp, scale=s), sAC)
            W("dv", sAC, va)
            W("dv", sPE, vbc)
            E("dv", lambda e, o=p_w[par].ap().rearrange("p (c t) -> p c t", c=DC),
              a=p_w[par].ap().rearrange("p (c t) -> p c t", c=DC),
              b=psum[pc].ap()[:, 0:TTL].unsqueeze(1).broadcast_to([128, DC, TTL]):
              e.tensor_tensor(o, a, b, ALU.mult))
            for dc in range(DC):
                col = n * DC + dc
                vd = E("dv", lambda e, o=y_c.ap(dc * TTL, TTL), a=p_w[par].ap(dc * TTL, TTL),
                       s=hfin.ap()[:, col:col + 1], b=y_c.ap(dc * TTL, TTL):
                       e.scalar_tensor_tensor(o, a, s, b, ALU.mult, ALU.add), sDV)
            vd_cp[par] = vd
            bank_free[pc] = (sDV, vd)
        W("gp", sDV, vd)
        vg = DMA("gp", hbm["sp_y"].rearrange("c p t -> p c t")[:, :, ts0:ts0 + TTL],
                 y_c.ap().rearrange("p (c t) -> p c t", c=DC))
        W("sy", sG, vg)
    y_c.release()
    s_c.release()
    for b_ in p_w:
        b_.release()
    hfin.release()
    spill_y2 = sG.n
    phase_barrier()

    phase_barrier()
    wo_f32 = Buf(af, KC * D)
    wo_hi = Buf(ab, KC * D)
    wo_lo = Buf(ab, KC * D)
    v_wo = load_split_weight("W_out", D, D, wo_f32, wo_hi, wo_lo)
    wo_f32.release()

    y2h = Buf(ab, DC * TTL)
    y2l = Buf(ab, DC * TTL)
    xt_rl = Buf(af, DC * TTL)
    res1 = Buf(af, DC * TTL)
    x1_t = Buf(af, DC * TTL)
    sq_t = [Buf(af, TTL), Buf(af, TTL)]

    def emit_ln(src_fn, out_fn, affine, src_ready=0):
        nonlocal va_sq, vpe_sq
        bank_wait(5)
        bank_wait(3)
        W("ac", sDV, src_ready)
        for dc in range(DC):
            par = dc % 2
            W("ac", sPE, vpe_sq[par])
            va = E("ac", lambda e, o=sq_t[par].ap(), i=src_fn(dc):
                   e.activation(o, i, AF.Square), sAC)
            W("pe", sAC, va)
            E("pe", lambda e, o=psum[5].ap()[0:1, 0:TTL], a=sb["ones1"].ap()[:, 0:1],
              b=src_fn(dc), s=(dc == 0), st=(dc == DC - 1): e.matmul(o, a, b, start=s, stop=st))
            vpe = E("pe", lambda e, o=psum[3].ap()[0:1, 0:TTL], a=sb["ones1"].ap()[:, 0:1],
                    b=sq_t[par].ap(), s=(dc == 0), st=(dc == DC - 1):
                    e.matmul(o, a, b, start=s, stop=st), sPE)
            vpe_sq[par] = vpe
        W("dv", sPE, vpe)
        L = sb["lns"].ap()
        E("dv", lambda e, o=L[0:1, 0:TTL], i=psum[5].ap()[0:1, 0:TTL]:
          e.tensor_scalar(o, i, 1.0 / D, None, ALU.mult))
        E("dv", lambda e, o=L[0:1, TTL:2 * TTL], i=psum[3].ap()[0:1, 0:TTL]:
          e.tensor_scalar(o, i, 1.0 / D, None, ALU.mult))
        E("dv", lambda e, o=L[0:1, 2 * TTL:3 * TTL], a=L[0:1, 0:TTL], b=L[0:1, 0:TTL]:
          e.tensor_tensor(o, a, b, ALU.mult))
        E("dv", lambda e, o=L[0:1, TTL:2 * TTL], a=L[0:1, TTL:2 * TTL],
          b=L[0:1, 2 * TTL:3 * TTL]: e.tensor_tensor(o, a, b, ALU.subtract))
        vd = E("dv", lambda e, o=L[0:1, TTL:2 * TTL], i=L[0:1, TTL:2 * TTL]:
               e.tensor_scalar(o, i, LN_EPS, None, ALU.add), sDV)
        W("ac", sDV, vd)
        va = E("ac", lambda e, o=L[0:1, TTL:2 * TTL], i=L[0:1, TTL:2 * TTL]:
               e.activation(o, i, AF.Sqrt), sAC)
        W("dv", sAC, va)
        vf = E("dv", lambda e, o=L[0:1, 2 * TTL:3 * TTL], i=L[0:1, TTL:2 * TTL]:
               e.reciprocal(o, i), sDV)
        W("dv", sDV, vf)
        vd = E("dv", lambda e, o=L[0:1, 3 * TTL:4 * TTL], a=L[0:1, 0:TTL],
               b=L[0:1, 2 * TTL:3 * TTL]: e.tensor_tensor(o, a, b, ALU.mult), sDV)
        bank_wait(4)
        W("pe", sDV, vd)
        E("pe", lambda e, o=psum[4].ap()[:, 0:TTL], a=sb["ones1"].ap()[0:1, :],
          b=L[0:1, 2 * TTL:3 * TTL]: e.matmul(o, a, b, start=True, stop=True))
        vpe = E("pe", lambda e, o=psum[3].ap()[:, 0:TTL], a=sb["ones1"].ap()[0:1, :],
                b=L[0:1, 3 * TTL:4 * TTL]: e.matmul(o, a, b, start=True, stop=True), sPE)
        W("dv", sPE, vpe)
        for dc in range(DC):
            E("dv", lambda e, o=out_fn(dc), a=src_fn(dc), b=psum[4].ap()[:, 0:TTL]:
              e.tensor_tensor(o, a, b, ALU.mult))
            E("dv", lambda e, o=out_fn(dc), a=out_fn(dc), b=psum[3].ap()[:, 0:TTL]:
              e.tensor_tensor(o, a, b, ALU.subtract))
            if affine:
                vd = E("dv", lambda e, o=out_fn(dc), i=out_fn(dc),
                       g=sb["lng_col"].ap()[:, dc:dc + 1], bb=sb["lnb_col"].ap()[:, dc:dc + 1]:
                       e.tensor_scalar(o, i, g, bb, ALU.mult, ALU.add), sDV)
            else:
                vd = sDV.n
        if not affine:
            vd = E("dv", lambda e: e.engine_nop(), sDV)
        bank_free[3] = (sDV, vd)
        bank_free[4] = (sDV, vd)
        bank_free[5] = (sDV, vd)
        return vd

    va_sq = 0
    vpe_sq = [0, 0]
    W("pe", sDV, v_wo)
    W("sy", sG, spill_y2)
    vpe_tt = 0
    for tt in range(NTT):
        ts0 = tt * TTL
        W("sy", sPE, vpe_tt)
        W("sy", sDV, sDV.n)
        DMA("sy", xt_rl.ap().rearrange("p (c t) -> p c t", c=DC),
            hbm["sp_xt_f32"].rearrange("c p t -> p c t")[:, :, ts0:ts0 + TTL])
        DMA("sy", res1.ap().rearrange("p (c t) -> p c t", c=DC),
            hbm["sp_y"].rearrange("c p t -> p c t")[:, :, ts0:ts0 + TTL])
        vsz = DMA("sy", x1_t.ap().rearrange("p (c t) -> p c t", c=DC),
                  hbm["sp_sz"].rearrange("c p t -> p c t")[:, :, ts0:ts0 + TTL])
        W("dv", sW, vsz)
        E("dv", lambda e, o=res1.ap(), a=res1.ap(), b=x1_t.ap():
          e.tensor_tensor(o, a, b, ALU.mult))
        E("dv", lambda e, o=y2h.ap(), i=res1.ap(): e.tensor_copy(o, i))
        vd = E("dv", lambda e, o=y2l.ap(), a=res1.ap(), b=y2h.ap():
               e.tensor_tensor(o, a, b, ALU.subtract), sDV)
        W("sy", sDV, vd)
        W("pe", sDV, vd)
        vldx = sW.n
        for mc in range(DC):
            bank = mc % 2
            bank_wait(bank)
            vpe_tt = mm3(bank, KC,
                         lambda kc, mc=mc: (wo_hi.ap(kc * D + mc * 128, 128),
                                            wo_lo.ap(kc * D + mc * 128, 128)),
                         lambda kc: (y2h.ap(kc * TTL, TTL), y2l.ap(kc * TTL, TTL)))
            W("dv", sPE, vpe_tt)
            W("dv", sW, vldx)
            vd = E("dv", lambda e, o=res1.ap(mc * TTL, TTL), a=psum[bank].ap()[:, 0:TTL],
                   b=xt_rl.ap(mc * TTL, TTL): e.tensor_tensor(o, a, b, ALU.add), sDV)
            bank_free[bank] = (sDV, vd)
        emit_ln(lambda dc: res1.ap(dc * TTL, TTL), lambda dc: res1.ap(dc * TTL, TTL),
                affine=True, src_ready=vd)
        for dc in range(DC):
            vd = E("dv", lambda e, o=res1.ap(dc * TTL, TTL), a=res1.ap(dc * TTL, TTL),
                   b=xt_rl.ap(dc * TTL, TTL): e.tensor_tensor(o, a, b, ALU.add), sDV)
        emit_ln(lambda dc: res1.ap(dc * TTL, TTL), lambda dc: x1_t.ap(dc * TTL, TTL),
                affine=False, src_ready=vd)
        vd = sDV.n
        W("gp", sDV, vd)
        vg = DMA("gp", hbm["sp_x1"].rearrange("c p t -> p c t")[:, :, ts0:ts0 + TTL],
                 x1_t.ap().rearrange("p (c t) -> p c t", c=DC))
        W("dv", sG, vg)
    wo_hi.release()
    wo_lo.release()
    y2h.release()
    y2l.release()
    xt_rl.release()
    res1.release()
    x1_t.release()
    for b_ in sq_t:
        b_.release()
    spill_x1 = sG.n
    phase_barrier()


    def absmax_init(col):
        E("dv", lambda e, o=sb["red"].ap()[:, col:col + 1]: e.memset(o, 0.0))

    def absmax_acc(src_ap, col):
        vf = E("dv", lambda e, o=sb["redt"].ap(), i=src_ap:
               e.tensor_reduce(o, i, AX.X, ALU.max, apply_absolute_value=True), sDV)
        W("dv", sDV, vf)
        E("dv", lambda e, o=sb["red"].ap()[:, col:col + 1], a=sb["red"].ap()[:, col:col + 1],
          b=sb["redt"].ap(): e.tensor_tensor(o, a, b, ALU.max))

    def gmax_finish(col, slot):
        bank_wait(2)
        vd = E("dv", lambda e: e.engine_nop(), sDV)
        W("pe", sDV, vd)
        vpe = E("pe", lambda e, o=psum[2].ap()[0:1, 0:128], i=sb["red"].ap()[:, col:col + 1],
                idn=sb["ident"].ap(): e.transpose(o, i, idn), sPE)
        W("dv", sPE, vpe)
        vd = E("dv", lambda e, o=sb["qs"].ap()[0:1, slot:slot + 1], i=psum[2].ap()[0:1, 0:128]:
               e.tensor_reduce(o, i, AX.X, ALU.max), sDV)
        W("dv", sDV, vd)
        bank_free[2] = (sDV, vd)
        return vd

    def coll_max():
        vd = sDV.n
        W("gp", sDV, vd)
        vg = DMA("gp", cc_max_in.ap(), sb["qs"].ap())
        W("gp", sG, vg)
        vcc = E("gp", lambda e: e.collective_compute(
            "AllReduce", ALU.max, replica_groups=[list(range(8))],
            ins=[cc_max_in.ap().opt()], outs=[cc_max_out.ap().opt()]), sCC)
        W("gp", sCC, vcc)
        return DMA("gp", sb["qs"].ap(), cc_max_out.ap())

    def scale_calc(slot_max, slot_s, slot_r, qmax, vbi):
        W("dv", sG, vbi)
        Q = sb["qs"].ap()
        vf = E("dv", lambda e, o=Q[0:1, slot_s:slot_s + 1], i=Q[0:1, slot_max:slot_max + 1]:
               e.tensor_scalar(o, i, 1.0 / qmax, 1e-12, ALU.mult, ALU.max), sDV)
        W("dv", sDV, vf)
        vr = E("dv", lambda e, o=Q[0:1, slot_r:slot_r + 1], i=Q[0:1, slot_s:slot_s + 1]:
               e.reciprocal(o, i), sDV)
        W("dv", sDV, vr)
        return vr

    def col_bcast(slot, col, vd):
        bank_wait(2)
        W("pe", sDV, vd)
        vpe = E("pe", lambda e, o=psum[2].ap()[:, 0:1], a=sb["ones1"].ap()[0:1, :],
                b=sb["qs"].ap()[0:1, slot:slot + 1]:
                e.matmul(o, a, b, start=True, stop=True), sPE)
        W("dv", sPE, vpe)
        vd2 = E("dv", lambda e, o=sb["qcol"].ap()[:, col:col + 1], i=psum[2].ap()[:, 0:1]:
                e.tensor_copy(o, i), sDV)
        bank_free[2] = (sDV, vd2)
        return vd2

    x1w = [Buf(af, DC * TTL), Buf(af, DC * TTL)]
    qw = Buf(af, DC * TTL)
    absmax_init(0)
    W("sy", sG, spill_x1)
    va_w = [0, 0]
    for tt in range(NTT):
        par = tt % 2
        W("sy", sDV, va_w[par])
        vld = DMA("sy", x1w[par].ap().rearrange("p (c t) -> p c t", c=DC),
                  hbm["sp_x1"].rearrange("c p t -> p c t")[:, :, tt * TTL:(tt + 1) * TTL])
        W("dv", sW, vld)
        absmax_acc(x1w[par].ap(), 0)
        va_w[par] = E("dv", lambda e: e.engine_nop(), sDV)
    gmax_finish(0, 0)
    vbi = coll_max()
    vd = scale_calc(0, 1, 2, 32767.0, vbi)
    vd = col_bcast(1, 0, vd)
    vd = col_bcast(2, 1, vd)
    scales_x1 = vd

    xq_hi = Buf(ab, DC * T)
    xq_lo = Buf(ab, DC * T)
    va_q = [sDV.n, sDV.n]
    for tt in range(NTT):
        par = tt % 2
        W("sy", sDV, va_q[par])
        vld = DMA("sy", x1w[par].ap().rearrange("p (c t) -> p c t", c=DC),
                  hbm["sp_x1"].rearrange("c p t -> p c t")[:, :, tt * TTL:(tt + 1) * TTL])
        W("ac", sW, vld)
        if tt == 0:
            W("ac", sDV, scales_x1)
        va = E("ac", lambda e, o=x1w[par].ap(), i=x1w[par].ap(), r=sb["qcol"].ap()[:, 1:2],
               c=sb["rndc"].ap()[:, 0:1]:
               e.activation(o, i, AF.Identity, bias=c, scale=r), sAC)
        W("dv", sAC, va)
        E("dv", lambda e, o=qw.ap(), i=x1w[par].ap():
          e.tensor_scalar(o, i, -RND_C, 32767.0, ALU.add, ALU.min))
        E("dv", lambda e, o=qw.ap(), i=qw.ap():
          e.tensor_scalar(o, i, -32768.0, None, ALU.max))
        E("dv", lambda e, o=xq_hi.ap().rearrange("p (c u t) -> p c u t", c=DC, u=NTT)[:, :, tt, :],
          i=qw.ap().rearrange("p (c t) -> p c t", c=DC),
          s=sb["qcol"].ap()[:, 0:1]: e.tensor_scalar(o, i, s, None, ALU.mult))
        vq = E("dv", lambda e, o=xq_lo.ap().rearrange("p (c u t) -> p c u t", c=DC, u=NTT)[:, :, tt, :],
               a=qw.ap().rearrange("p (c t) -> p c t", c=DC),
               s=sb["qcol"].ap()[:, 0:1],
               b=xq_hi.ap().rearrange("p (c u t) -> p c u t", c=DC, u=NTT)[:, :, tt, :]:
               e.scalar_tensor_tensor(o, a, s, b, ALU.mult, ALU.subtract), sDV)
        va_q[par] = vq
    for b_ in x1w:
        b_.release()
    qw.release()
    xq_done = sDV.n
    phase_barrier()

    wrow2 = [Buf(af, 1024), Buf(af, 1024)]
    wu2 = [Buf(af, 1024), Buf(af, 1024)]
    wbs2 = [sb["wbs"], sb["wbr"]]
    vd_deq = [0, 0]
    vpe_tr = [0, 0]
    vld_row = [0, 0]

    def prep_dequant(wname, oc, sec, par):
        nonlocal vd_deq, vld_row
        nb = 32
        W("sy", sDV, vd_deq[par])
        W("sy", sPE, vpe_tr[par])
        v = DMA("sy", wrow2[par].ap(), hbm[wname][oc * 128:(oc + 1) * 128,
                                                  sec * 1024:(sec + 1) * 1024])
        vld_row[par] = v
        W("dv", sW, v)
        W("dv", sPE, vpe_tr[par])
        bsl = sb["wbs"].ap()[:, par * 32:par * 32 + 32]
        brl = sb["wbr"].ap()[:, par * 32:par * 32 + 32]
        vf = E("dv", lambda e, o=bsl.unsqueeze(2),
               i=wrow2[par].ap().rearrange("p (b i) -> p b i", b=nb):
               e.tensor_reduce(o, i, AX.X, ALU.max, apply_absolute_value=True), sDV)
        W("dv", sDV, vf)
        vf = E("dv", lambda e, o=bsl, i=bsl:
               e.tensor_scalar(o, i, 1.0 / 7.0, 1e-12, ALU.mult, ALU.max), sDV)
        W("dv", sDV, vf)
        vf = E("dv", lambda e, o=brl, i=bsl: e.reciprocal(o, i), sDV)
        W("dv", sDV, vf)
        vd = E("dv", lambda e, o=wu2[par].ap().rearrange("p (b i) -> p b i", b=nb),
               a=wrow2[par].ap().rearrange("p (b i) -> p b i", b=nb),
               b=brl.unsqueeze(2).broadcast_to([128, nb, 32]):
               e.tensor_tensor(o, a, b, ALU.mult), sDV)
        W("ac", sDV, vd)
        va = E("ac", lambda e, o=wu2[par].ap(), i=wu2[par].ap(), c=sb["rndc"].ap()[:, 0:1]:
               e.activation(o, i, AF.Identity, bias=c), sAC)
        W("dv", sAC, va)
        E("dv", lambda e, o=wu2[par].ap(), i=wu2[par].ap():
          e.tensor_scalar(o, i, -RND_C, 7.0, ALU.add, ALU.min))
        vd = E("dv", lambda e, o=wu2[par].ap().rearrange("p (b i) -> p b i", b=nb),
               a=wu2[par].ap().rearrange("p (b i) -> p b i", b=nb),
               s=-8.0,
               b=bsl.unsqueeze(2).broadcast_to([128, nb, 32]):
               e.scalar_tensor_tensor(o, a, s, b, ALU.max, ALU.mult), sDV)
        vd_deq[par] = vd
        return vd

    def prep_transpose(par, dst_hi, dst_lo, dst_off_fn):
        nonlocal vpe_tr
        W("pe", sDV, vd_deq[par])
        for kcl in range(8):
            bank = 2 + (kcl % 2)
            bank_wait(bank)
            vpe = E("pe", lambda e, o=psum[bank].ap()[:, 0:128],
                    i=wu2[par].ap(kcl * 128, 128), idn=sb["ident"].ap():
                    e.transpose(o, i, idn), sPE)
            W("ac", sPE, vpe)
            va = E("ac", lambda e, o=dst_hi.ap(dst_off_fn(kcl), 128),
                   i=psum[bank].ap()[:, 0:128]: e.copy(o, i), sAC)
            W("dv", sAC, va)
            vdl = E("dv", lambda e, o=dst_lo.ap(dst_off_fn(kcl), 128),
                    a=psum[bank].ap()[:, 0:128], b=dst_hi.ap(dst_off_fn(kcl), 128):
                    e.tensor_tensor(o, a, b, ALU.subtract), sDV)
            bank_free[bank] = (sDV, vdl)
        vpe_tr[par] = vpe
        return vdl

    wq_hi2 = [Buf(ab, KC * 128), Buf(ab, KC * 128)]
    wq_lo2 = [Buf(ab, KC * 128), Buf(ab, KC * 128)]
    up_ev2 = [Buf(af, TTL), Buf(af, TTL)]
    vg_ev = [0, 0]
    absmax_init(1)
    absmax_init(2)
    vpe_b = 0
    jobs = [("w_up", "sp_up", 1, oc) for oc in range(IC)] + \
           [("w_gate", "sp_gate", 2, oc) for oc in range(IC)]
    vdl_tr = [0, 0]
    prep_dequant(jobs[0][0], jobs[0][3], 0, 0)
    vdl_tr[0] = prep_transpose(0, wq_hi2[0], wq_lo2[0], lambda kcl: kcl * 128)
    for j, (wname, spname, rcol, oc) in enumerate(jobs):
        par = j % 2
        if j + 1 < len(jobs):
            prep_dequant(jobs[j + 1][0], jobs[j + 1][3], 0, 1 - par)
        W("pe", sDV, vdl_tr[par])
        if j == 0:
            W("pe", sDV, xq_done)
        for tt in range(NTT):
            bank = tt % 2
            bank_wait(bank)
            vpe_b = mm3(bank, KC,
                        lambda kc, par=par: (wq_hi2[par].ap(kc * 128, 128),
                                             wq_lo2[par].ap(kc * 128, 128)),
                        lambda kc, tt=tt: (xq_hi.ap(kc * T + tt * TTL, TTL),
                                           xq_lo.ap(kc * T + tt * TTL, TTL)))
            epar = tt % 2
            W("ac", sPE, vpe_b)
            W("ac", sG, vg_ev[epar])
            va2 = E("ac", lambda e, o=up_ev2[epar].ap(), i=psum[bank].ap()[:, 0:TTL]:
                    e.copy(o, i), sAC)
            W("dv", sAC, va2)
            absmax_acc(up_ev2[epar].ap(), rcol)
            vd = E("dv", lambda e: e.engine_nop(), sDV)
            bank_free[bank] = (sDV, vd)
            W("gp", sDV, vd)
            vg_ev[epar] = DMA("gp", hbm[spname][oc, :, tt * TTL:(tt + 1) * TTL],
                              up_ev2[epar].ap())
        if j + 1 < len(jobs):
            vdl_tr[1 - par] = prep_transpose(1 - par, wq_hi2[1 - par], wq_lo2[1 - par],
                                             lambda kcl: kcl * 128)
        xq_hi.release()
    xq_lo.release()
    gmax_finish(1, 3)
    gmax_finish(2, 4)
    vbi = coll_max()
    vd = scale_calc(3, 5, 6, 32767.0, vbi)
    vd = scale_calc(4, 7, 8, 32767.0, vd * 0 + vbi)
    vd = col_bcast(5, 2, vd)
    vd = col_bcast(6, 3, vd)
    vd = col_bcast(7, 4, vd)
    vd = col_bcast(8, 5, vd)
    scales_ug = vd

    phase_barrier()

    djobs = [(oc, sec) for oc in range(DC) for sec in range(3)]
    prep_dequant("w_down", djobs[0][0], djobs[0][1], 0)
    vg_wdn = [0, 0]
    for j, (oc, sec) in enumerate(djobs):
        par = j % 2
        if j + 1 < len(djobs):
            prep_dequant("w_down", djobs[j + 1][0], djobs[j + 1][1], 1 - par)
        W("ac", sG, vg_wdn[par])
        W("dv", sG, vg_wdn[par])
        vprep = prep_transpose(par, wq_hi2[par], wq_lo2[par], lambda kcl: kcl * 128)
        W("gp", sDV, vprep)
        DMA("gp", hbm["sp_wdn_hi"].rearrange("c k p m -> p c k m")[:, oc, sec * 8:(sec + 1) * 8, :],
            wq_hi2[par].ap().rearrange("p (k m) -> p k m", k=8))
        vg_wdn[par] = DMA("gp", hbm["sp_wdn_lo"].rearrange("c k p m -> p c k m")[:, oc, sec * 8:(sec + 1) * 8, :],
                          wq_lo2[par].ap().rearrange("p (k m) -> p k m", k=8))
    W("gp", sG, vg_wdn[0])
    W("gp", sG, vg_wdn[1])
    phase_barrier()
    for b_ in wrow2 + wu2 + up_ev2 + wq_hi2 + wq_lo2:
        b_.release()
    wdn_spilled = sG.n
    phase_barrier()

    phase_barrier()
    gw = [Buf(af, T), Buf(af, T)]
    qgw = Buf(af, T)
    sigw = Buf(af, T)
    absmax_init(3)
    va_d = [0, 0]
    vg_d = [0, 0]
    for oc in range(IC):
        par = oc % 2
        W("sy", sAC, va_d[par])
        W("sy", sG, vg_d[par])
        vld = DMA("sy", gw[par].ap(), hbm["sp_gate"][oc])
        W("ac", sW, vld)
        if oc == 0:
            W("ac", sDV, scales_ug)
        va = E("ac", lambda e, o=gw[par].ap(), i=gw[par].ap(), r=sb["qcol"].ap()[:, 5:6],
               c=sb["rndc"].ap()[:, 0:1]:
               e.activation(o, i, AF.Identity, bias=c, scale=r), sAC)
        va_d[par] = va
        W("dv", sAC, va)
        E("dv", lambda e, o=qgw.ap(), i=gw[par].ap():
          e.tensor_scalar(o, i, -RND_C, 32767.0, ALU.add, ALU.min))
        vd = E("dv", lambda e, o=qgw.ap(), i=qgw.ap():
               e.tensor_scalar(o, i, -32768.0, None, ALU.max), sDV)
        W("ac", sDV, vd)
        E("ac", lambda e, o=sigw.ap(), i=qgw.ap(), s=sb["qcol"].ap()[:, 4:5]:
          e.activation(o, i, AF.Sigmoid, scale=s))
        va = E("ac", lambda e, o=sigw.ap(), i=sigw.ap(), c=sb["rndc"].ap()[:, 0:1]:
               e.activation(o, i, AF.Identity, bias=c, scale=65536.0), sAC)
        W("dv", sAC, va)
        E("dv", lambda e, o=sigw.ap(), i=sigw.ap():
          e.tensor_scalar(o, i, -RND_C, 65535.0, ALU.add, ALU.min))
        E("dv", lambda e, o=sigw.ap(), i=sigw.ap():
          e.tensor_scalar(o, i, 0.0, None, ALU.max))
        W("dv", sG, vg_d[par])
        E("dv", lambda e, o=gw[par].ap(), a=qgw.ap(), b=sigw.ap():
          e.tensor_tensor(o, a, b, ALU.mult))
        absmax_acc(gw[par].ap(), 3)
        vdn = E("dv", lambda e: e.engine_nop(), sDV)
        W("gp", sDV, vdn)
        vg_d[par] = DMA("gp", hbm["sp_pg"][oc], gw[par].ap())
    gmax_finish(3, 9)
    vbi = coll_max()
    Q = sb["qs"].ap()
    W("dv", sG, vbi)
    for fn in [
        lambda e, o=Q[0:1, 10:11], i=Q[0:1, 7:8]:
            e.tensor_scalar(o, i, 1.0 / 65536.0, None, ALU.mult),
        lambda e, o=Q[0:1, 11:12], a=Q[0:1, 9:10], b=Q[0:1, 10:11]:
            e.tensor_tensor(o, a, b, ALU.mult),
        lambda e, o=Q[0:1, 11:12], i=Q[0:1, 11:12]:
            e.tensor_scalar(o, i, 1.0 / 32767.0, 1e-12, ALU.mult, ALU.max),
        lambda e, o=Q[0:1, 12:13], i=Q[0:1, 11:12]: e.reciprocal(o, i),
        lambda e, o=Q[0:1, 12:13], a=Q[0:1, 10:11], b=Q[0:1, 12:13]:
            e.tensor_tensor(o, a, b, ALU.mult),
    ]:
        vd = E("dv", fn, sDV)
        W("dv", sDV, vd)
    vd = col_bcast(12, 6, vd)
    scale2_done = vd

    phase_barrier()

    phase_barrier()
    uw = [Buf(af, T), Buf(af, T)]
    quw = sigw
    absmax_init(4)
    va_f = [0, 0]
    va_fu = [0, 0]
    vg_f = [0, 0]
    for oc in range(IC):
        par = oc % 2
        W("sy", sAC, va_d[par])
        W("sy", sG, vg_d[par])
        vld1 = DMA("sy", gw[par].ap(), hbm["sp_pg"][oc])
        W("sy", sAC, va_fu[par])
        W("sy", sG, vg_f[par])
        vld2 = DMA("sy", uw[par].ap(), hbm["sp_up"][oc])
        W("ac", sW, vld1)
        if oc == 0:
            W("ac", sDV, scale2_done)
        va = E("ac", lambda e, o=gw[par].ap(), i=gw[par].ap(), r=sb["qcol"].ap()[:, 6:7],
               c=sb["rndc"].ap()[:, 0:1]:
               e.activation(o, i, AF.Identity, bias=c, scale=r), sAC)
        va_d[par] = va
        W("dv", sAC, va)
        E("dv", lambda e, o=qgw.ap(), i=gw[par].ap():
          e.tensor_scalar(o, i, -RND_C, 32767.0, ALU.add, ALU.min))
        vtt = E("dv", lambda e, o=qgw.ap(), i=qgw.ap():
                e.tensor_scalar(o, i, -32768.0, None, ALU.max), sDV)
        W("ac", sW, vld2)
        W("ac", sDV, vtt)
        va2 = E("ac", lambda e, o=uw[par].ap(), i=uw[par].ap(), r=sb["qcol"].ap()[:, 3:4],
                c=sb["rndc"].ap()[:, 0:1]:
                e.activation(o, i, AF.Identity, bias=c, scale=r), sAC)
        va_fu[par] = va2
        W("dv", sAC, va2)
        E("dv", lambda e, o=quw.ap(), i=uw[par].ap():
          e.tensor_scalar(o, i, -RND_C, 32767.0, ALU.add, ALU.min))
        E("dv", lambda e, o=quw.ap(), i=quw.ap():
          e.tensor_scalar(o, i, -32768.0, None, ALU.max))
        W("dv", sG, vg_f[par])
        E("dv", lambda e, o=uw[par].ap(), a=qgw.ap(), b=quw.ap():
          e.tensor_tensor(o, a, b, ALU.mult))
        absmax_acc(uw[par].ap(), 4)
        vdn = E("dv", lambda e: e.engine_nop(), sDV)
        W("gp", sDV, vdn)
        vg_f[par] = DMA("gp", hbm["sp_po"][oc], uw[par].ap())
    for b_ in gw + uw:
        b_.release()
    qgw.release()
    sigw.release()
    gmax_finish(4, 13)
    vbi = coll_max()
    W("dv", sG, vbi)
    for fn in [
        lambda e, o=Q[0:1, 14:15], a=Q[0:1, 11:12], b=Q[0:1, 5:6]:
            e.tensor_tensor(o, a, b, ALU.mult),
        lambda e, o=Q[0:1, 15:16], a=Q[0:1, 13:14], b=Q[0:1, 14:15]:
            e.tensor_tensor(o, a, b, ALU.mult),
        lambda e, o=Q[0:1, 15:16], i=Q[0:1, 15:16]:
            e.tensor_scalar(o, i, 1.0 / 32767.0, 1e-12, ALU.mult, ALU.max),
        lambda e, o=Q[0:1, 0:1], i=Q[0:1, 15:16]: e.reciprocal(o, i),
        lambda e, o=Q[0:1, 0:1], a=Q[0:1, 14:15], b=Q[0:1, 0:1]:
            e.tensor_tensor(o, a, b, ALU.mult),
    ]:
        vd = E("dv", fn, sDV)
        W("dv", sDV, vd)
    vd = col_bcast(0, 7, vd)
    vd = col_bcast(15, 1, vd)
    scaleo_done = vd

    phase_barrier()

    phase_barrier()
    TH = min(256, T)
    NTH = T // TH
    oq_hi = Buf(ab, IC * TH)
    oq_lo = Buf(ab, IC * TH)
    wdnb_hi = [Buf(ab, IC * 128), Buf(ab, IC * 128)]
    wdnb_lo = [Buf(ab, IC * 128), Buf(ab, IC * 128)]
    vpe_wdnb = [0, 0]
    res3 = Buf(af, DC * TH)
    x2_t = Buf(af, DC * TH)
    x1row = Buf(af, TH)
    orow = Buf(af, D)
    sqh = [Buf(af, TH), Buf(af, TH)]

    LS = max(TTL, 256)

    def emit_ln_h(src_fn, out_fn, src_ready=0):
        nonlocal vpe_sqh
        bank_wait(5)
        bank_wait(3)
        W("ac", sDV, src_ready)
        for dc in range(DC):
            par = dc % 2
            W("ac", sPE, vpe_sqh[par])
            va = E("ac", lambda e, o=sqh[par].ap(), i=src_fn(dc):
                   e.activation(o, i, AF.Square), sAC)
            W("pe", sAC, va)
            E("pe", lambda e, o=psum[5].ap()[0:1, 0:TH], a=sb["ones1"].ap()[:, 0:1],
              b=src_fn(dc), s=(dc == 0), st=(dc == DC - 1): e.matmul(o, a, b, start=s, stop=st))
            vpe = E("pe", lambda e, o=psum[3].ap()[0:1, 0:TH], a=sb["ones1"].ap()[:, 0:1],
                    b=sqh[par].ap(), s=(dc == 0), st=(dc == DC - 1):
                    e.matmul(o, a, b, start=s, stop=st), sPE)
            vpe_sqh[par] = vpe
        W("dv", sPE, vpe)
        L = sb["lns"].ap()
        E("dv", lambda e, o=L[0:1, 0:TH], i=psum[5].ap()[0:1, 0:TH]:
          e.tensor_scalar(o, i, 1.0 / D, None, ALU.mult))
        E("dv", lambda e, o=L[0:1, LS:LS + TH], i=psum[3].ap()[0:1, 0:TH]:
          e.tensor_scalar(o, i, 1.0 / D, None, ALU.mult))
        E("dv", lambda e, o=L[0:1, 2 * LS:2 * LS + TH], a=L[0:1, 0:TH], b=L[0:1, 0:TH]:
          e.tensor_tensor(o, a, b, ALU.mult))
        E("dv", lambda e, o=L[0:1, LS:LS + TH], a=L[0:1, LS:LS + TH],
          b=L[0:1, 2 * LS:2 * LS + TH]: e.tensor_tensor(o, a, b, ALU.subtract))
        vd = E("dv", lambda e, o=L[0:1, LS:LS + TH], i=L[0:1, LS:LS + TH]:
               e.tensor_scalar(o, i, LN_EPS, None, ALU.add), sDV)
        W("ac", sDV, vd)
        va = E("ac", lambda e, o=L[0:1, LS:LS + TH], i=L[0:1, LS:LS + TH]:
               e.activation(o, i, AF.Sqrt), sAC)
        W("dv", sAC, va)
        vf = E("dv", lambda e, o=L[0:1, 2 * LS:2 * LS + TH], i=L[0:1, LS:LS + TH]:
               e.reciprocal(o, i), sDV)
        W("dv", sDV, vf)
        vd = E("dv", lambda e, o=L[0:1, 3 * LS:3 * LS + TH], a=L[0:1, 0:TH],
               b=L[0:1, 2 * LS:2 * LS + TH]: e.tensor_tensor(o, a, b, ALU.mult), sDV)
        bank_wait(4)
        W("pe", sDV, vd)
        E("pe", lambda e, o=psum[4].ap()[:, 0:TH], a=sb["ones1"].ap()[0:1, :],
          b=L[0:1, 2 * LS:2 * LS + TH]: e.matmul(o, a, b, start=True, stop=True))
        vpe = E("pe", lambda e, o=psum[3].ap()[:, 0:TH], a=sb["ones1"].ap()[0:1, :],
                b=L[0:1, 3 * LS:3 * LS + TH]: e.matmul(o, a, b, start=True, stop=True), sPE)
        W("dv", sPE, vpe)
        for dc in range(DC):
            E("dv", lambda e, o=out_fn(dc), a=src_fn(dc), b=psum[4].ap()[:, 0:TH]:
              e.tensor_tensor(o, a, b, ALU.mult))
            vd = E("dv", lambda e, o=out_fn(dc), a=out_fn(dc), b=psum[3].ap()[:, 0:TH]:
                   e.tensor_tensor(o, a, b, ALU.subtract), sDV)
        bank_free[3] = (sDV, vd)
        bank_free[4] = (sDV, vd)
        bank_free[5] = (sDV, vd)
        return vd

    vpe_sqh = [0, 0]
    va_h = 0
    vst = 0
    IC2 = IC // 2
    pw = Buf(af, IC2 * TH)
    qhw = Buf(af, IC2 * TH)
    for th in range(NTH):
        hs0 = th * TH
        for half in range(2):
            o0 = half * IC2
            W("sy", sDV, va_h)
            vld = DMA("sy", pw.ap().rearrange("p (c t) -> p c t", c=IC2),
                      hbm["sp_po"].rearrange("c p t -> p c t")[:, o0:o0 + IC2, hs0:hs0 + TH])
            W("ac", sW, vld)
            if th == 0 and half == 0:
                W("ac", sDV, scaleo_done)
            W("ac", sDV, va_h)
            va = E("ac", lambda e, o=pw.ap(), i=pw.ap(),
                   r=sb["qcol"].ap()[:, 7:8], c=sb["rndc"].ap()[:, 0:1]:
                   e.activation(o, i, AF.Identity, bias=c, scale=r), sAC)
            W("dv", sAC, va)
            E("dv", lambda e, o=qhw.ap(), i=pw.ap():
              e.tensor_scalar(o, i, -RND_C, 32767.0, ALU.add, ALU.min))
            E("dv", lambda e, o=qhw.ap(), i=qhw.ap():
              e.tensor_scalar(o, i, -32768.0, None, ALU.max))
            E("dv", lambda e, o=oq_hi.ap(o0 * TH, IC2 * TH), i=qhw.ap(),
              s=sb["qcol"].ap()[:, 1:2]: e.tensor_scalar(o, i, s, None, ALU.mult))
            vq = E("dv", lambda e, o=oq_lo.ap(o0 * TH, IC2 * TH), a=qhw.ap(),
                   s=sb["qcol"].ap()[:, 1:2], b=oq_hi.ap(o0 * TH, IC2 * TH):
                   e.scalar_tensor_tensor(o, a, s, b, ALU.mult, ALU.subtract), sDV)
            va_h = vq
        W("pe", sDV, vq)
        if th == 0:
            W("sy", sG, wdn_spilled)

        def wdn_load(mc, par):
            W("sy", sPE, vpe_wdnb[par])
            DMA("sy", wdnb_hi[par].ap().rearrange("p (k m) -> p k m", k=IC),
                hbm["sp_wdn_hi"].rearrange("c k p m -> p c k m")[:, mc, :, :])
            return DMA("sy", wdnb_lo[par].ap().rearrange("p (k m) -> p k m", k=IC),
                       hbm["sp_wdn_lo"].rearrange("c k p m -> p c k m")[:, mc, :, :])

        vwl_par = [0, 0]
        vwl_par[0] = wdn_load(0, 0)
        for mc in range(DC):
            bank = mc % 2
            par = mc % 2
            W("pe", sW, vwl_par[par])
            bank_wait(bank)
            vpe_h = mm3(bank, IC,
                        lambda kc, par=par: (wdnb_hi[par].ap(kc * 128, 128),
                                             wdnb_lo[par].ap(kc * 128, 128)),
                        lambda kc: (oq_hi.ap(kc * TH, TH), oq_lo.ap(kc * TH, TH)),
                        nw=TH)
            vpe_wdnb[par] = vpe_h
            if mc + 1 < DC:
                vwl_par[1 - par] = wdn_load(mc + 1, 1 - par)
            vldx = DMA("sy", x1row.ap(), hbm["sp_x1"][mc, :, hs0:hs0 + TH])
            W("dv", sPE, vpe_h)
            W("dv", sW, vldx)
            vd = E("dv", lambda e, o=res3.ap(mc * TH, TH), a=psum[bank].ap()[:, 0:TH],
                   b=x1row.ap(): e.tensor_tensor(o, a, b, ALU.add), sDV)
            bank_free[bank] = (sDV, vd)
            W("sy", sDV, vd)
        emit_ln_h(lambda dc: res3.ap(dc * TH, TH), lambda dc: x2_t.ap(dc * TH, TH),
                  src_ready=vd)
        vdl = sDV.n
        W("pe", sDV, vdl)
        for j in range(TH // 128):
            for dc in range(DC):
                bank = 6 + (dc % 2)
                bank_wait(bank)
                vpe = E("pe", lambda e, o=psum[bank].ap()[:, 0:128],
                        i=x2_t.ap(dc * TH + j * 128, 128), idn=sb["ident"].ap():
                        e.transpose(o, i, idn), sPE)
                W("ac", sPE, vpe)
                W("ac", sW, vst)
                va = E("ac", lambda e, o=orow.ap(dc * 128, 128),
                       i=psum[bank].ap()[:, 0:128]: e.copy(o, i), sAC)
                bank_free[bank] = (sAC, va)
            W("sy", sAC, va)
            trow = th * (TH // 128) + j
            vst = DMA("sy", hbm["out"][trow * 128:(trow + 1) * 128, :], orow.ap())

    ops["sy"].append(lambda e, v=sW.n: e.wait_ge(sW.h, v))
    ops["gp"].append(lambda e, v=sG.n: e.wait_ge(sG.h, v))

    with nc.Block() as block:
        @block.sync
        def _(eng):
            for f in ops["sy"]:
                f(eng)

        @block.tensor
        def _(eng):
            for f in ops["pe"]:
                f(eng)

        @block.vector
        def _(eng):
            for f in ops["dv"]:
                f(eng)

        @block.scalar
        def _(eng):
            for f in ops["ac"]:
                f(eng)

        @block.gpsimd
        def _(eng):
            for f in ops["gp"]:
                f(eng)

    es.close()
    return nc


def col_layout(v, DC):
    return np.ascontiguousarray(np.asarray(v, np.float32).reshape(DC, 128).T)


def kernel(L0_attn_x_in, L0_attn_W_in, L0_attn_W_x, L0_attn_W_dt, L0_attn_b_dt,
           L0_attn_A_log, L0_attn_D_param, L0_attn_W_out, L0_attn_ln_g,
           L0_attn_ln_b, L0_ffn_w_gate, L0_ffn_w_up, L0_ffn_w_down):
    cfg = Cfg()
    x = np.ascontiguousarray(np.asarray(L0_attn_x_in, dtype=np.float32))
    Bb, L, D = x.shape
    T, N, DC = cfg.T, cfg.N, cfg.DC

    sel = np.zeros((2 * N, 2 * N * 128), dtype=np.float32)
    for j in range(2 * N):
        sel[j, j * 128:(j + 1) * 128] = 1.0
    base = {
        "W_in": np.ascontiguousarray(L0_attn_W_in, dtype=np.float32),
        "W_x": np.ascontiguousarray(L0_attn_W_x, dtype=np.float32),
        "W_dt": np.ascontiguousarray(L0_attn_W_dt, dtype=np.float32),
        "W_out": np.ascontiguousarray(L0_attn_W_out, dtype=np.float32),
        "w_up": np.ascontiguousarray(L0_ffn_w_up, dtype=np.float32),
        "w_gate": np.ascontiguousarray(L0_ffn_w_gate, dtype=np.float32),
        "w_down": np.ascontiguousarray(L0_ffn_w_down, dtype=np.float32),
        "bdt_col": col_layout(L0_attn_b_dt, DC),
        "alog_col": np.ascontiguousarray(
            np.asarray(L0_attn_A_log, np.float32).reshape(DC, 128, N)
            .transpose(1, 0, 2).reshape(128, DC * N)),
        "dpar_col": col_layout(L0_attn_D_param, DC),
        "lng_col": col_layout(L0_attn_ln_g, DC),
        "lnb_col": col_layout(L0_attn_ln_b, DC),
        "ident": np.eye(128, dtype=np.float32),
        "ones1": np.ones((128, 128), dtype=np.float32),
        "identb": np.eye(128, dtype=ml_dtypes.bfloat16),
        "sel": sel.astype(ml_dtypes.bfloat16),
    }
    in_maps = []
    for c in range(8):
        b, h = c // 2, c % 2
        im = dict(base)
        im["x_in"] = np.ascontiguousarray(x[b, h * T:(h + 1) * T, :])
        mask = np.zeros((128, 2), np.float32)
        mask[:, 0] = 1.0 if h == 0 else 0.0
        mask[:, 1] = 1.0 if h == 1 else 0.0
        im["mask_col"] = mask
        in_maps.append(im)

    nc = build_nc(cfg)
    res = run_bass_kernel_spmd(nc, in_maps, core_ids=list(range(8)))
    out = np.empty((Bb, L, D), np.float32)
    for c in range(8):
        b, h = c // 2, c % 2
        out[b, h * T:(h + 1) * T, :] = res.results[c]["out"]
    return out


# version 09
# speedup vs baseline: 1.4934x; 1.0079x over previous
import numpy as np
import ml_dtypes
import concourse.bass as bass
import concourse.mybir as mybir
from concourse.bass_utils import run_bass_kernel_spmd

F32 = mybir.dt.float32
BF16 = mybir.dt.bfloat16
AF = mybir.ActivationFunctionType
ALU = mybir.AluOpType
AX = mybir.AxisListType

RND_C = float(1.5 * 2 ** 23)
LN_EPS = 1e-5


class Cfg:
    def __init__(self, T=2048, D=1024, N=16, DI=3072, TTL=512):
        self.T, self.D, self.N, self.DI, self.TTL = T, D, N, DI, TTL
        self.NTT = T // TTL
        self.DC = D // 128
        self.IC = DI // 128
        self.KC = D // 128


class Sem:
    def __init__(self, es, nc, name):
        self.h = es.enter_context(nc.semaphore(name))
        self.n = 0

    def inc(self, k=1):
        self.n += k
        return self.n


class Arena:
    def __init__(self, nc, name, cols, dtype):
        self.t = nc.alloc_sbuf_tensor(name, [128, cols], dtype)
        self.free_list = [(0, cols)]

    def alloc(self, n, align=8):
        for i, (s, e) in enumerate(self.free_list):
            a = (s + align - 1) // align * align
            if a + n <= e:
                segs = []
                if a > s:
                    segs.append((s, a))
                if a + n < e:
                    segs.append((a + n, e))
                self.free_list[i:i + 1] = segs
                return a
        raise RuntimeError(f"arena OOM: need {n}, free={self.free_list}")

    def free(self, off, n):
        self.free_list.append((off, off + n))
        self.free_list.sort()
        m = []
        for s, e in self.free_list:
            if m and m[-1][1] >= s:
                m[-1] = (m[-1][0], max(m[-1][1], e))
            else:
                m.append((s, e))
        self.free_list = [(s, e) for s, e in m]


class Buf:
    def __init__(self, arena, cols, align=8):
        self.arena = arena
        self.off = arena.alloc(cols, align)
        self.cols = cols

    def ap(self, c0=0, n=None, rows=128, row0=0):
        if n is None:
            n = self.cols - c0
        return self.arena.t.ap()[row0:row0 + rows, self.off + c0:self.off + c0 + n]

    def release(self):
        self.arena.free(self.off, self.cols)


def build_nc(cfg, dbg_spills=()):
    import contextlib
    es = contextlib.ExitStack()
    nc = bass.Bass("TRN2", target_bir_lowering=False, debug=False,
                   detect_race_conditions=False)
    T, D, N, DI, TTL = cfg.T, cfg.D, cfg.N, cfg.DI, cfg.TTL
    DC, IC, KC, NTT = cfg.DC, cfg.IC, cfg.KC, cfg.NTT
    TJ = TTL // 128

    hbm = {}
    hbm["x_in"] = nc.dram_tensor("x_in", [T, D], F32, kind="ExternalInput").ap()
    for wn, shape in [("W_in", [D, 2 * D]), ("W_x", [D, D + 2 * N]),
                      ("W_dt", [D, D]), ("W_out", [D, D]),
                      ("w_up", [DI, D]), ("w_gate", [DI, D]), ("w_down", [D, DI])]:
        hbm[wn] = nc.dram_tensor(wn, shape, F32, kind="ExternalInput").ap()
    for cn, shape, dt in [("ident", [128, 128], F32), ("ones1", [128, 128], F32),
                          ("identb", [128, 128], BF16),
                          ("sel", [2 * N, 2 * N * 128], BF16),
                          ("bdt_col", [128, DC], F32), ("alog_col", [128, DC * N], F32),
                          ("dpar_col", [128, DC], F32), ("lng_col", [128, DC], F32),
                          ("lnb_col", [128, DC], F32), ("mask_col", [128, 2], F32)]:
        hbm[cn] = nc.dram_tensor(cn, shape, dt, kind="ExternalInput").ap()
    hbm["out"] = nc.dram_tensor("out", [T, D], F32, kind="ExternalOutput").ap()

    for k, shape, dt in [
        ("sp_xt_f32", [DC, 128, T], F32), ("sp_xt_hi", [DC, 128, T], BF16),
        ("sp_xt_lo", [DC, 128, T], BF16),
        ("sp_x_f32", [DC, 128, T], F32), ("sp_x_hi", [DC, 128, T], BF16),
        ("sp_x_lo", [DC, 128, T], BF16),
        ("sp_sz", [DC, 128, T], F32),
        ("sp_dr_hi", [DC, 128, T], BF16), ("sp_dr_lo", [DC, 128, T], BF16),
        ("sp_S", [DC, 128, T], F32), ("sp_y", [DC, 128, T], F32),
        ("sp_x1", [DC, 128, T], F32),
        ("sp_up", [IC, 128, T], F32), ("sp_gate", [IC, 128, T], F32),
        ("sp_wdn_hi", [DC, IC, 128, 128], BF16), ("sp_wdn_lo", [DC, IC, 128, 128], BF16),
        ("sp_pg", [IC, 128, T], F32), ("sp_po", [IC, 128, T], F32),
    ]:
        kind = "ExternalOutput" if k in dbg_spills else "Internal"
        hbm[k] = nc.dram_tensor(k, list(shape), dt, kind=kind).ap()

    cc_carry_in = nc.dram_tensor("cc_carry_in", [128, N * DC], F32)
    cc_carry_out = nc.dram_tensor("cc_carry_out", [128, N * DC], F32)
    cc_max_in = nc.dram_tensor("cc_max_in", [1, 16], F32)
    cc_max_out = nc.dram_tensor("cc_max_out", [1, 16], F32)

    sb = {}
    for cn, shape, dt in [("ident", [128, 128], F32), ("ones1", [128, 128], F32),
                          ("identb", [128, 128], BF16),
                          ("sel", [2 * N, 2 * N * 128], BF16),
                          ("bdt_col", [128, DC], F32), ("alog_col", [128, DC * N], F32),
                          ("dpar_col", [128, DC], F32), ("lng_col", [128, DC], F32),
                          ("lnb_col", [128, DC], F32), ("mask_col", [128, 2], F32),
                          ("A", [128, DC * N], F32), ("zeros", [128, TTL], F32),
                          ("rndc", [128, 1], F32), ("state", [128, N * DC], F32),
                          ("scar", [128, DC], F32),
                          ("bch", [2 * N, T], BF16), ("bcl", [2 * N, T], BF16),
                          ("lns", [1, 4 * max(TTL, 256)], F32),
                          ("red", [128, 8], F32), ("redt", [128, 1], F32),
                          ("qs", [1, 16], F32), ("qcol", [128, 8], F32),
                          ("wbs", [128, 64], F32), ("wbr", [128, 64], F32)]:
        sb[cn] = nc.alloc_sbuf_tensor("sb_" + cn, shape, dt)

    psum = [es.enter_context(nc.psum_tensor(f"pb{i}", [128, 512], F32)) for i in range(8)]

    af = Arena(nc, "af32", 24100, F32)
    ab = Arena(nc, "abf16", 43040, BF16)

    ops = {"pe": [], "dv": [], "ac": [], "gp": [], "sy": []}
    sW = Sem(es, nc, "sW")
    sG = Sem(es, nc, "sG")
    sPE = Sem(es, nc, "sPE")
    sDV = Sem(es, nc, "sDV")
    sAC = Sem(es, nc, "sAC")
    sCC = Sem(es, nc, "sCC")

    def E(q, fn, sem=None, inc=1):
        if sem is None:
            ops[q].append(fn)
            return None
        ops[q].append(lambda e, f=fn, h=sem.h, k=inc: f(e).then_inc(h, k))
        return sem.inc(inc)

    def W(q, sem, v):
        if v and v > 0:
            ops[q].append(lambda e, h=sem.h, v=v: e.wait_ge(h, v))

    def DMA(q, dst, src):
        sem = sW if q == "sy" else sG
        ops[q].append(lambda e, d=dst, s=src: e.dma_start(out=d, in_=s).then_inc(sem.h, 16))
        return sem.inc(16)

    bank_free = [None] * 8

    def bank_wait(j):
        if bank_free[j] is not None:
            sem, v = bank_free[j]
            W("pe", sem, v)

    def phase_barrier():
        for q in ("sy", "pe", "dv", "ac", "gp"):
            for sem in (sW, sG, sPE, sDV, sAC, sCC):
                W(q, sem, sem.n)

    for name in ["ident", "ones1", "identb", "sel", "bdt_col", "alog_col",
                 "dpar_col", "lng_col", "lnb_col", "mask_col"]:
        DMA("sy", sb[name].ap(), hbm[name])
    w_consts = sW.n

    E("dv", lambda e: e.memset(sb["zeros"].ap(), 0.0))
    E("dv", lambda e: e.memset(sb["rndc"].ap(), RND_C))
    E("dv", lambda e: e.memset(sb["state"].ap(), 0.0))
    E("dv", lambda e: e.memset(sb["scar"].ap(), 0.0))
    E("dv", lambda e: e.memset(sb["qs"].ap(), 0.0))
    E("dv", lambda e: e.memset(sb["red"].ap(), 0.0), sDV)
    W("ac", sW, w_consts)
    va = E("ac", lambda e: e.activation(sb["A"].ap(), sb["alog_col"].ap(), AF.Exp), sAC)
    W("dv", sAC, va)
    E("dv", lambda e: e.tensor_scalar(sb["A"].ap(), sb["A"].ap(), -1.0, None, ALU.mult), sDV)

    def load_split_weight(wname, rows, cols, f32buf, hibuf, lobuf):
        nkc = rows // 128
        v = DMA("sy", f32buf.ap().rearrange("p (k m) -> p k m", k=nkc),
                hbm[wname].rearrange("(k p) m -> p k m", p=128))
        W("ac", sW, v)
        va = E("ac", lambda e, o=hibuf.ap(), i=f32buf.ap(): e.copy(o, i), sAC)
        W("dv", sAC, va)
        vd = E("dv", lambda e, o=lobuf.ap(), a=f32buf.ap(), b=hibuf.ap():
               e.tensor_tensor(o, a, b, ALU.subtract), sDV)
        W("sy", sDV, vd)
        W("gp", sDV, vd)
        return vd

    def mm3(bank, nkc, get_w, get_x, mw=128, nw=None):
        if nw is None:
            nw = TTL
        for kc in range(nkc):
            wh, wl = get_w(kc)
            xh, xl = get_x(kc)
            E("pe", lambda e, o=psum[bank].ap()[0:mw, 0:nw], a=wh, b=xh, s=(kc == 0):
              e.matmul(o, a, b, start=s, stop=False))
            E("pe", lambda e, o=psum[bank].ap()[0:mw, 0:nw], a=wh, b=xl:
              e.matmul(o, a, b, start=False, stop=False))
            v = E("pe", lambda e, o=psum[bank].ap()[0:mw, 0:nw], a=wl, b=xh,
                  st=(kc == nkc - 1): e.matmul(o, a, b, start=False, stop=st),
                  sPE if kc == nkc - 1 else None)
        return v

    w_in_f32 = Buf(af, 0) if False else None
    xt_f = Buf(af, DC * TTL)
    xt_h = Buf(ab, DC * TTL)
    xt_l = Buf(ab, DC * TTL)
    xrow = [Buf(af, D), Buf(af, D)]
    W("pe", sW, w_consts)
    vpe_row = [0, 0]
    va_evac = 0
    for tt in range(NTT):
        for j in range(TJ):
            par = j % 2
            trow = tt * TJ + j
            W("sy", sPE, vpe_row[par])
            vld = DMA("sy", xrow[par].ap(), hbm["x_in"][trow * 128:(trow + 1) * 128, :])
            W("pe", sW, vld)
            for dc in range(DC):
                bank = dc % 4
                bank_wait(bank)
                vpe = E("pe", lambda e, o=psum[bank].ap()[:, 0:128],
                        i=xrow[par].ap(dc * 128, 128), idn=sb["ident"].ap():
                        e.transpose(o, i, idn), sPE)
                W("ac", sPE, vpe)
                va = E("ac", lambda e, o=xt_f.ap(dc * TTL + j * 128, 128),
                       i=psum[bank].ap()[:, 0:128]: e.copy(o, i), sAC)
                bank_free[bank] = (sAC, va)
            vpe_row[par] = vpe
            va_evac = va
        W("dv", sAC, va_evac)
        E("dv", lambda e, o=xt_h.ap(), i=xt_f.ap(): e.tensor_copy(o, i))
        vd = E("dv", lambda e, o=xt_l.ap(), a=xt_f.ap(), b=xt_h.ap():
               e.tensor_tensor(o, a, b, ALU.subtract), sDV)
        W("gp", sDV, vd)
        DMA("gp", hbm["sp_xt_f32"].rearrange("c p t -> p c t")[:, :, tt * TTL:(tt + 1) * TTL],
            xt_f.ap().rearrange("p (c t) -> p c t", c=DC))
        DMA("gp", hbm["sp_xt_hi"].rearrange("c p t -> p c t")[:, :, tt * TTL:(tt + 1) * TTL],
            xt_h.ap().rearrange("p (c t) -> p c t", c=DC))
        vg = DMA("gp", hbm["sp_xt_lo"].rearrange("c p t -> p c t")[:, :, tt * TTL:(tt + 1) * TTL],
                 xt_l.ap().rearrange("p (c t) -> p c t", c=DC))
        W("ac", sG, vg)
        W("dv", sG, vg)
    for b_ in xrow:
        b_.release()
    xt_f.release()
    xt_h.release()
    xt_l.release()
    spill_xt = sG.n
    phase_barrier()

    wf = Buf(af, 0) if False else None
    w_f = Buf(af, KC * 128)
    w_f.release()

    w1f = Buf(af, 0) if False else None
    w1_f32 = Buf(af, 0) if False else None

    win_f = Buf(af, KC * 2 * D // 4) if False else None
    win_hi = Buf(ab, KC * 2 * D)
    win_lo = Buf(ab, KC * 2 * D)
    wstage = Buf(af, KC * D)
    for half in range(2):
        v = DMA("sy", wstage.ap().rearrange("p (k m) -> p k m", k=KC),
                hbm["W_in"][:, half * D:(half + 1) * D].rearrange("(k p) m -> p k m", p=128))
        W("ac", sW, v)
        va = E("ac", lambda e, o=win_hi.ap() if False else None: None) if False else None
        for kc in range(KC):
            va = E("ac", lambda e, o=win_hi.ap(kc * 2 * D + half * D, D),
                   i=wstage.ap(kc * D, D): e.copy(o, i), sAC)
        W("dv", sAC, va)
        for kc in range(KC):
            vd = E("dv", lambda e, o=win_lo.ap(kc * 2 * D + half * D, D),
                   a=wstage.ap(kc * D, D), b=win_hi.ap(kc * 2 * D + half * D, D):
                   e.tensor_tensor(o, a, b, ALU.subtract), sDV)
        W("sy", sDV, vd)
    v_win = vd
    W("gp", sDV, vd)
    wstage.release()

    xth = Buf(ab, DC * TTL)
    xtl = Buf(ab, DC * TTL)
    evf = [Buf(af, TTL), Buf(af, TTL)]
    evh = [Buf(ab, TTL), Buf(ab, TTL)]
    evl = [Buf(ab, TTL), Buf(ab, TTL)]
    vg_par = [0, 0]
    W("pe", sDV, v_win)
    W("sy", sG, spill_xt)
    vpe_tt = 0
    for tt in range(NTT):
        W("sy", sPE, vpe_tt)
        DMA("sy", xth.ap().rearrange("p (c t) -> p c t", c=DC),
            hbm["sp_xt_hi"].rearrange("c p t -> p c t")[:, :, tt * TTL:(tt + 1) * TTL])
        vld = DMA("sy", xtl.ap().rearrange("p (c t) -> p c t", c=DC),
                  hbm["sp_xt_lo"].rearrange("c p t -> p c t")[:, :, tt * TTL:(tt + 1) * TTL])
        W("pe", sW, vld)
        for mc in range(2 * DC):
            bank = mc % 2
            bank_wait(bank)
            vpe_tt = mm3(bank, KC,
                         lambda kc, mc=mc: (win_hi.ap(kc * 2 * D + mc * 128, 128),
                                            win_lo.ap(kc * 2 * D + mc * 128, 128)),
                         lambda kc: (xth.ap(kc * TTL, TTL), xtl.ap(kc * TTL, TTL)))
            W("ac", sPE, vpe_tt)
            par = mc % 2
            W("ac", sG, vg_par[par])
            W("dv", sG, vg_par[par])
            if mc < DC:
                va = E("ac", lambda e, o=evf[par].ap(), i=psum[bank].ap()[:, 0:TTL]: e.copy(o, i), sAC)
                W("dv", sAC, va)
                E("dv", lambda e, o=evh[par].ap(), i=evf[par].ap(): e.tensor_copy(o, i))
                vd = E("dv", lambda e, o=evl[par].ap(), a=evf[par].ap(), b=evh[par].ap():
                       e.tensor_tensor(o, a, b, ALU.subtract), sDV)
                bank_free[bank] = (sAC, va)
                W("gp", sDV, vd)
                DMA("gp", hbm["sp_x_f32"][mc, :, tt * TTL:(tt + 1) * TTL], evf[par].ap())
                DMA("gp", hbm["sp_x_hi"][mc, :, tt * TTL:(tt + 1) * TTL], evh[par].ap())
                vg_par[par] = DMA("gp", hbm["sp_x_lo"][mc, :, tt * TTL:(tt + 1) * TTL], evl[par].ap())
            else:
                va = E("ac", lambda e, o=evf[par].ap(), i=psum[bank].ap()[:, 0:TTL]:
                       e.activation(o, i, AF.Silu), sAC)
                bank_free[bank] = (sAC, va)
                W("gp", sAC, va)
                vg_par[par] = DMA("gp", hbm["sp_sz"][mc - DC, :, tt * TTL:(tt + 1) * TTL], evf[par].ap())
    win_hi.release()
    win_lo.release()
    spill_p1 = sG.n
    phase_barrier()

    MX = D + 2 * N
    wx_f32 = Buf(af, KC * MX)
    wx_hi = Buf(ab, KC * MX)
    wx_lo = Buf(ab, KC * MX)
    v_wx = load_split_weight("W_x", D, MX, wx_f32, wx_hi, wx_lo)
    wx_f32.release()
    bcf = Buf(af, T)
    W("pe", sDV, v_wx)
    W("sy", sG, spill_p1)
    vpe_tt = 0
    for tt in range(NTT):
        W("sy", sPE, vpe_tt)
        DMA("sy", xth.ap().rearrange("p (c t) -> p c t", c=DC),
            hbm["sp_x_hi"].rearrange("c p t -> p c t")[:, :, tt * TTL:(tt + 1) * TTL])
        vld = DMA("sy", xtl.ap().rearrange("p (c t) -> p c t", c=DC),
                  hbm["sp_x_lo"].rearrange("c p t -> p c t")[:, :, tt * TTL:(tt + 1) * TTL])
        W("pe", sW, vld)
        for mc in range(DC + 1):
            bank = mc % 2
            mw = 128 if mc < DC else 2 * N
            bank_wait(bank)
            vpe_tt = mm3(bank, KC,
                         lambda kc, mc=mc, mw=mw: (wx_hi.ap(kc * MX + mc * 128, mw),
                                                   wx_lo.ap(kc * MX + mc * 128, mw)),
                         lambda kc: (xth.ap(kc * TTL, TTL), xtl.ap(kc * TTL, TTL)),
                         mw=mw)
            W("ac", sPE, vpe_tt)
            par = mc % 2
            if mc < DC:
                W("ac", sG, vg_par[par])
                W("dv", sG, vg_par[par])
                va = E("ac", lambda e, o=evh[par].ap(), i=psum[bank].ap()[:, 0:TTL]: e.copy(o, i), sAC)
                W("dv", sAC, va)
                W("dv", sPE, vpe_tt)
                vd = E("dv", lambda e, o=evl[par].ap(), a=psum[bank].ap()[:, 0:TTL], b=evh[par].ap():
                       e.tensor_tensor(o, a, b, ALU.subtract), sDV)
                bank_free[bank] = (sDV, vd)
                W("gp", sDV, vd)
                DMA("gp", hbm["sp_dr_hi"][mc, :, tt * TTL:(tt + 1) * TTL], evh[par].ap())
                vg_par[par] = DMA("gp", hbm["sp_dr_lo"][mc, :, tt * TTL:(tt + 1) * TTL], evl[par].ap())
            else:
                va = E("ac", lambda e, o=bcf.ap(tt * TTL, TTL, rows=2 * N),
                       i=psum[bank].ap()[0:2 * N, 0:TTL]: e.copy(o, i), sAC)
                W("dv", sAC, va)
                E("dv", lambda e, o=sb["bch"].ap()[:, tt * TTL:(tt + 1) * TTL],
                  i=bcf.ap(tt * TTL, TTL, rows=2 * N): e.tensor_copy(o, i))
                vd = E("dv", lambda e, o=sb["bcl"].ap()[:, tt * TTL:(tt + 1) * TTL],
                       a=bcf.ap(tt * TTL, TTL, rows=2 * N),
                       b=sb["bch"].ap()[:, tt * TTL:(tt + 1) * TTL]:
                       e.tensor_tensor(o, a, b, ALU.subtract), sDV)
                bank_free[bank] = (sDV, vd)
    xth.release()
    xtl.release()
    wx_hi.release()
    wx_lo.release()
    bcf.release()
    bc_ready = sDV.n
    spill_p2 = sG.n
    phase_barrier()

    wdt_f32 = Buf(af, KC * D)
    wdt_hi = Buf(ab, KC * D)
    wdt_lo = Buf(ab, KC * D)
    v_wdt = load_split_weight("W_dt", D, D, wdt_f32, wdt_hi, wdt_lo)
    wdt_f32.release()

    drh = Buf(ab, DC * TTL)
    drl = Buf(ab, DC * TTL)
    delta = Buf(af, DC * TTL)
    dx = Buf(af, DC * TTL)
    y_tt = Buf(af, DC * TTL)
    u_w = Buf(af, DC * TTL)
    h_w = Buf(af, DC * TTL)
    s_tc = Buf(af, TTL)
    x_rl = Buf(af, TTL)
    e_tmp = Buf(af, TTL)
    dA_t = [Buf(af, TTL), Buf(af, TTL)]

    W("pe", sDV, v_wdt)
    W("sy", sG, spill_p2)
    vpe_tt = 0
    vd_scan = [0, 0]
    for tt in range(NTT):
        ts0 = tt * TTL
        W("sy", sPE, vpe_tt)
        DMA("sy", drh.ap().rearrange("p (c t) -> p c t", c=DC),
            hbm["sp_dr_hi"].rearrange("c p t -> p c t")[:, :, ts0:ts0 + TTL])
        vld = DMA("sy", drl.ap().rearrange("p (c t) -> p c t", c=DC),
                  hbm["sp_dr_lo"].rearrange("c p t -> p c t")[:, :, ts0:ts0 + TTL])
        W("pe", sW, vld)
        for dc in range(DC):
            bank = dc % 2
            bank_wait(bank)
            vpe_tt = mm3(bank, KC,
                         lambda kc, dc=dc: (wdt_hi.ap(kc * D + dc * 128, 128),
                                            wdt_lo.ap(kc * D + dc * 128, 128)),
                         lambda kc: (drh.ap(kc * TTL, TTL), drl.ap(kc * TTL, TTL)))
            W("ac", sPE, vpe_tt)
            E("ac", lambda e, o=e_tmp.ap(), i=psum[bank].ap()[:, 0:TTL],
              b=sb["bdt_col"].ap()[:, dc:dc + 1]:
              e.activation(o, i, AF.Exp, bias=b))
            va = E("ac", lambda e, o=delta.ap(dc * TTL, TTL), i=e_tmp.ap():
                   e.activation(o, i, AF.Ln, bias=1.0), sAC)

            bank_free[bank] = (sAC, va)
            vx = DMA("sy", x_rl.ap(), hbm["sp_x_f32"][dc, :, ts0:ts0 + TTL])
            W("dv", sAC, va)
            W("dv", sW, vx)
            E("dv", lambda e, o=dx.ap(dc * TTL, TTL), a=delta.ap(dc * TTL, TTL),
              b=x_rl.ap(): e.tensor_tensor(o, a, b, ALU.mult))
            E("dv", lambda e, o=y_tt.ap(dc * TTL, TTL), a=x_rl.ap(),
              s=sb["dpar_col"].ap()[:, dc:dc + 1]:
              e.tensor_scalar(o, a, s, None, ALU.mult))
            vsc = E("dv", lambda e, o=s_tc.ap(), d0=delta.ap(dc * TTL, TTL),
                    z=sb["zeros"].ap(), ini=sb["scar"].ap()[:, dc:dc + 1]:
                    e.tensor_tensor_scan(o, d0, z, ini, ALU.add, ALU.add), sDV)
            W("dv", sDV, vsc)
            vd = E("dv", lambda e, o=sb["scar"].ap()[:, dc:dc + 1], i=s_tc.ap(TTL - 1, 1):
                   e.tensor_copy(o, i), sDV)
            W("sy", sDV, vd)
            W("gp", sDV, vd)
            vg = DMA("gp", hbm["sp_S"][dc, :, ts0:ts0 + TTL], s_tc.ap())
            W("dv", sG, vg)
        for n in range(N):
            pb, pc = 6, 7
            bank_wait(pb)
            bank_wait(pc)
            if tt == 0 and n == 0:
                W("pe", sDV, bc_ready)
            E("pe", lambda e, o=psum[pb].ap()[:, 0:TTL], a=sb["sel"].ap()[:, n * 128:(n + 1) * 128],
              b=sb["bch"].ap()[:, ts0:ts0 + TTL]: e.matmul(o, a, b, start=True, stop=False))
            E("pe", lambda e, o=psum[pb].ap()[:, 0:TTL], a=sb["sel"].ap()[:, n * 128:(n + 1) * 128],
              b=sb["bcl"].ap()[:, ts0:ts0 + TTL]: e.matmul(o, a, b, start=False, stop=True))
            E("pe", lambda e, o=psum[pc].ap()[:, 0:TTL], a=sb["sel"].ap()[:, (N + n) * 128:(N + n + 1) * 128],
              b=sb["bch"].ap()[:, ts0:ts0 + TTL]: e.matmul(o, a, b, start=True, stop=False))
            vbc = E("pe", lambda e, o=psum[pc].ap()[:, 0:TTL], a=sb["sel"].ap()[:, (N + n) * 128:(N + n + 1) * 128],
                    b=sb["bcl"].ap()[:, ts0:ts0 + TTL]: e.matmul(o, a, b, start=False, stop=True), sPE)
            W("dv", sPE, vbc)
            E("dv", lambda e, o=u_w.ap().rearrange("p (c t) -> p c t", c=DC),
              a=dx.ap().rearrange("p (c t) -> p c t", c=DC),
              b=psum[pb].ap()[:, 0:TTL].unsqueeze(1).broadcast_to([128, DC, TTL]):
              e.tensor_tensor(o, a, b, ALU.mult))
            for dc in range(DC):
                par = dc % 2
                W("ac", sDV, vd_scan[par])
                va = E("ac", lambda e, o=dA_t[par].ap(), i=delta.ap(dc * TTL, TTL),
                       s=sb["A"].ap()[:, dc * N + n:dc * N + n + 1]:
                       e.activation(o, i, AF.Exp, scale=s), sAC)
                W("dv", sAC, va)
                col = n * DC + dc
                vsc = E("dv", lambda e, o=h_w.ap(dc * TTL, TTL), d0=dA_t[par].ap(),
                        d1=u_w.ap(dc * TTL, TTL),
                        ini=sb["state"].ap()[:, col:col + 1]:
                        e.tensor_tensor_scan(o, d0, d1, ini, ALU.mult, ALU.add), sDV)
                W("dv", sDV, vsc)
                vd = E("dv", lambda e, o=sb["state"].ap()[:, col:col + 1],
                       i=h_w.ap(dc * TTL + TTL - 1, 1): e.tensor_copy(o, i), sDV)
                vd_scan[par] = vd
            E("dv", lambda e, o=h_w.ap().rearrange("p (c t) -> p c t", c=DC),
              a=h_w.ap().rearrange("p (c t) -> p c t", c=DC),
              b=psum[pc].ap()[:, 0:TTL].unsqueeze(1).broadcast_to([128, DC, TTL]):
              e.tensor_tensor(o, a, b, ALU.mult))
            vd = E("dv", lambda e, o=y_tt.ap(), a=y_tt.ap(), b=h_w.ap():
                   e.tensor_tensor(o, a, b, ALU.add), sDV)
            bank_free[pb] = (sDV, vd)
            bank_free[pc] = (sDV, vd)
        W("gp", sDV, vd)
        vg = DMA("gp", hbm["sp_y"].rearrange("c p t -> p c t")[:, :, ts0:ts0 + TTL],
                 y_tt.ap().rearrange("p (c t) -> p c t", c=DC))
        W("dv", sG, vg)
    drh.release()
    drl.release()
    wdt_hi.release()
    wdt_lo.release()
    delta.release()
    dx.release()
    s_tc.release()
    x_rl.release()
    e_tmp.release()
    u_w.release()
    h_w.release()
    for b_ in dA_t:
        b_.release()
    y_tt.release()
    spill_y = sG.n
    phase_barrier()

    hfin = Buf(af, N * DC)
    phase_barrier()
    vd = E("dv", lambda e, o=hfin.ap(), a=sb["state"].ap(),
           s=sb["mask_col"].ap()[:, 0:1]: e.tensor_scalar(o, a, s, None, ALU.mult), sDV)
    W("gp", sDV, vd)
    vg = DMA("gp", cc_carry_in.ap(), hfin.ap())
    W("gp", sG, vg)
    vcc = E("gp", lambda e: e.collective_compute(
        "AllReduce", ALU.add,
        replica_groups=[[0, 1], [2, 3], [4, 5], [6, 7]],
        ins=[cc_carry_in.ap().opt()],
        outs=[cc_carry_out.ap().opt()]), sCC)
    W("gp", sCC, vcc)
    vg = DMA("gp", hfin.ap(), cc_carry_out.ap())
    W("dv", sG, vg)
    v_hin = E("dv", lambda e, o=hfin.ap(), a=hfin.ap(),
              s=sb["mask_col"].ap()[:, 1:2]: e.tensor_scalar(o, a, s, None, ALU.mult), sDV)

    y_c = Buf(af, DC * TTL)
    s_c = Buf(af, DC * TTL)
    p_w = [Buf(af, DC * TTL), Buf(af, DC * TTL)]
    W("dv", sDV, v_hin)
    W("sy", sG, spill_y)
    vd_cp = [0, 0]
    for tt in range(NTT):
        ts0 = tt * TTL
        DMA("sy", y_c.ap().rearrange("p (c t) -> p c t", c=DC),
            hbm["sp_y"].rearrange("c p t -> p c t")[:, :, ts0:ts0 + TTL])
        vld = DMA("sy", s_c.ap().rearrange("p (c t) -> p c t", c=DC),
                  hbm["sp_S"].rearrange("c p t -> p c t")[:, :, ts0:ts0 + TTL])
        W("ac", sW, vld)
        W("dv", sW, vld)
        for n in range(N):
            par = n % 2
            pc = 6 + par
            bank_wait(pc)
            E("pe", lambda e, o=psum[pc].ap()[:, 0:TTL], a=sb["sel"].ap()[:, (N + n) * 128:(N + n + 1) * 128],
              b=sb["bch"].ap()[:, ts0:ts0 + TTL]: e.matmul(o, a, b, start=True, stop=False))
            vbc = E("pe", lambda e, o=psum[pc].ap()[:, 0:TTL], a=sb["sel"].ap()[:, (N + n) * 128:(N + n + 1) * 128],
                    b=sb["bcl"].ap()[:, ts0:ts0 + TTL]: e.matmul(o, a, b, start=False, stop=True), sPE)
            W("ac", sDV, vd_cp[par])
            for dc in range(DC):
                va = E("ac", lambda e, o=p_w[par].ap(dc * TTL, TTL), i=s_c.ap(dc * TTL, TTL),
                       s=sb["A"].ap()[:, dc * N + n:dc * N + n + 1]:
                       e.activation(o, i, AF.Exp, scale=s), sAC)
            W("dv", sAC, va)
            W("dv", sPE, vbc)
            E("dv", lambda e, o=p_w[par].ap().rearrange("p (c t) -> p c t", c=DC),
              a=p_w[par].ap().rearrange("p (c t) -> p c t", c=DC),
              b=psum[pc].ap()[:, 0:TTL].unsqueeze(1).broadcast_to([128, DC, TTL]):
              e.tensor_tensor(o, a, b, ALU.mult))
            for dc in range(DC):
                col = n * DC + dc
                vd = E("dv", lambda e, o=y_c.ap(dc * TTL, TTL), a=p_w[par].ap(dc * TTL, TTL),
                       s=hfin.ap()[:, col:col + 1], b=y_c.ap(dc * TTL, TTL):
                       e.scalar_tensor_tensor(o, a, s, b, ALU.mult, ALU.add), sDV)
            vd_cp[par] = vd
            bank_free[pc] = (sDV, vd)
        W("gp", sDV, vd)
        vg = DMA("gp", hbm["sp_y"].rearrange("c p t -> p c t")[:, :, ts0:ts0 + TTL],
                 y_c.ap().rearrange("p (c t) -> p c t", c=DC))
        W("sy", sG, vg)
    y_c.release()
    s_c.release()
    for b_ in p_w:
        b_.release()
    hfin.release()
    spill_y2 = sG.n
    phase_barrier()

    phase_barrier()
    wo_f32 = Buf(af, KC * D)
    wo_hi = Buf(ab, KC * D)
    wo_lo = Buf(ab, KC * D)
    v_wo = load_split_weight("W_out", D, D, wo_f32, wo_hi, wo_lo)
    wo_f32.release()

    y2h = Buf(ab, DC * TTL)
    y2l = Buf(ab, DC * TTL)
    xt_rl = Buf(af, DC * TTL)
    res1 = Buf(af, DC * TTL)
    x1_t = Buf(af, DC * TTL)
    sq_t = [Buf(af, TTL), Buf(af, TTL)]

    def emit_ln(src_fn, out_fn, affine, src_ready=0):
        nonlocal va_sq, vpe_sq
        bank_wait(5)
        bank_wait(3)
        W("ac", sDV, src_ready)
        for dc in range(DC):
            par = dc % 2
            W("ac", sPE, vpe_sq[par])
            va = E("ac", lambda e, o=sq_t[par].ap(), i=src_fn(dc):
                   e.activation(o, i, AF.Square), sAC)
            W("pe", sAC, va)
            E("pe", lambda e, o=psum[5].ap()[0:1, 0:TTL], a=sb["ones1"].ap()[:, 0:1],
              b=src_fn(dc), s=(dc == 0), st=(dc == DC - 1): e.matmul(o, a, b, start=s, stop=st))
            vpe = E("pe", lambda e, o=psum[3].ap()[0:1, 0:TTL], a=sb["ones1"].ap()[:, 0:1],
                    b=sq_t[par].ap(), s=(dc == 0), st=(dc == DC - 1):
                    e.matmul(o, a, b, start=s, stop=st), sPE)
            vpe_sq[par] = vpe
        W("dv", sPE, vpe)
        L = sb["lns"].ap()
        E("dv", lambda e, o=L[0:1, 0:TTL], i=psum[5].ap()[0:1, 0:TTL]:
          e.tensor_scalar(o, i, 1.0 / D, None, ALU.mult))
        E("dv", lambda e, o=L[0:1, TTL:2 * TTL], i=psum[3].ap()[0:1, 0:TTL]:
          e.tensor_scalar(o, i, 1.0 / D, None, ALU.mult))
        E("dv", lambda e, o=L[0:1, 2 * TTL:3 * TTL], a=L[0:1, 0:TTL], b=L[0:1, 0:TTL]:
          e.tensor_tensor(o, a, b, ALU.mult))
        E("dv", lambda e, o=L[0:1, TTL:2 * TTL], a=L[0:1, TTL:2 * TTL],
          b=L[0:1, 2 * TTL:3 * TTL]: e.tensor_tensor(o, a, b, ALU.subtract))
        vd = E("dv", lambda e, o=L[0:1, TTL:2 * TTL], i=L[0:1, TTL:2 * TTL]:
               e.tensor_scalar(o, i, LN_EPS, None, ALU.add), sDV)
        W("ac", sDV, vd)
        va = E("ac", lambda e, o=L[0:1, TTL:2 * TTL], i=L[0:1, TTL:2 * TTL]:
               e.activation(o, i, AF.Sqrt), sAC)
        W("dv", sAC, va)
        vf = E("dv", lambda e, o=L[0:1, 2 * TTL:3 * TTL], i=L[0:1, TTL:2 * TTL]:
               e.reciprocal(o, i), sDV)
        W("dv", sDV, vf)
        vd = E("dv", lambda e, o=L[0:1, 3 * TTL:4 * TTL], a=L[0:1, 0:TTL],
               b=L[0:1, 2 * TTL:3 * TTL]: e.tensor_tensor(o, a, b, ALU.mult), sDV)
        bank_wait(4)
        W("pe", sDV, vd)
        E("pe", lambda e, o=psum[4].ap()[:, 0:TTL], a=sb["ones1"].ap()[0:1, :],
          b=L[0:1, 2 * TTL:3 * TTL]: e.matmul(o, a, b, start=True, stop=True))
        vpe = E("pe", lambda e, o=psum[3].ap()[:, 0:TTL], a=sb["ones1"].ap()[0:1, :],
                b=L[0:1, 3 * TTL:4 * TTL]: e.matmul(o, a, b, start=True, stop=True), sPE)
        W("dv", sPE, vpe)
        for dc in range(DC):
            E("dv", lambda e, o=out_fn(dc), a=src_fn(dc), b=psum[4].ap()[:, 0:TTL]:
              e.tensor_tensor(o, a, b, ALU.mult))
            E("dv", lambda e, o=out_fn(dc), a=out_fn(dc), b=psum[3].ap()[:, 0:TTL]:
              e.tensor_tensor(o, a, b, ALU.subtract))
            if affine:
                vd = E("dv", lambda e, o=out_fn(dc), i=out_fn(dc),
                       g=sb["lng_col"].ap()[:, dc:dc + 1], bb=sb["lnb_col"].ap()[:, dc:dc + 1]:
                       e.tensor_scalar(o, i, g, bb, ALU.mult, ALU.add), sDV)
            else:
                vd = sDV.n
        if not affine:
            vd = E("dv", lambda e: e.engine_nop(), sDV)
        bank_free[3] = (sDV, vd)
        bank_free[4] = (sDV, vd)
        bank_free[5] = (sDV, vd)
        return vd

    va_sq = 0
    vpe_sq = [0, 0]
    W("pe", sDV, v_wo)
    W("sy", sG, spill_y2)
    vpe_tt = 0
    for tt in range(NTT):
        ts0 = tt * TTL
        W("sy", sPE, vpe_tt)
        W("sy", sDV, sDV.n)
        DMA("sy", xt_rl.ap().rearrange("p (c t) -> p c t", c=DC),
            hbm["sp_xt_f32"].rearrange("c p t -> p c t")[:, :, ts0:ts0 + TTL])
        DMA("sy", res1.ap().rearrange("p (c t) -> p c t", c=DC),
            hbm["sp_y"].rearrange("c p t -> p c t")[:, :, ts0:ts0 + TTL])
        vsz = DMA("sy", x1_t.ap().rearrange("p (c t) -> p c t", c=DC),
                  hbm["sp_sz"].rearrange("c p t -> p c t")[:, :, ts0:ts0 + TTL])
        W("dv", sW, vsz)
        E("dv", lambda e, o=res1.ap(), a=res1.ap(), b=x1_t.ap():
          e.tensor_tensor(o, a, b, ALU.mult))
        E("dv", lambda e, o=y2h.ap(), i=res1.ap(): e.tensor_copy(o, i))
        vd = E("dv", lambda e, o=y2l.ap(), a=res1.ap(), b=y2h.ap():
               e.tensor_tensor(o, a, b, ALU.subtract), sDV)
        W("sy", sDV, vd)
        W("pe", sDV, vd)
        vldx = sW.n
        for mc in range(DC):
            bank = mc % 2
            bank_wait(bank)
            vpe_tt = mm3(bank, KC,
                         lambda kc, mc=mc: (wo_hi.ap(kc * D + mc * 128, 128),
                                            wo_lo.ap(kc * D + mc * 128, 128)),
                         lambda kc: (y2h.ap(kc * TTL, TTL), y2l.ap(kc * TTL, TTL)))
            W("dv", sPE, vpe_tt)
            W("dv", sW, vldx)
            vd = E("dv", lambda e, o=res1.ap(mc * TTL, TTL), a=psum[bank].ap()[:, 0:TTL],
                   b=xt_rl.ap(mc * TTL, TTL): e.tensor_tensor(o, a, b, ALU.add), sDV)
            bank_free[bank] = (sDV, vd)
        emit_ln(lambda dc: res1.ap(dc * TTL, TTL), lambda dc: res1.ap(dc * TTL, TTL),
                affine=True, src_ready=vd)
        for dc in range(DC):
            vd = E("dv", lambda e, o=res1.ap(dc * TTL, TTL), a=res1.ap(dc * TTL, TTL),
                   b=xt_rl.ap(dc * TTL, TTL): e.tensor_tensor(o, a, b, ALU.add), sDV)
        emit_ln(lambda dc: res1.ap(dc * TTL, TTL), lambda dc: x1_t.ap(dc * TTL, TTL),
                affine=False, src_ready=vd)
        vd = sDV.n
        W("gp", sDV, vd)
        vg = DMA("gp", hbm["sp_x1"].rearrange("c p t -> p c t")[:, :, ts0:ts0 + TTL],
                 x1_t.ap().rearrange("p (c t) -> p c t", c=DC))
        W("dv", sG, vg)
    wo_hi.release()
    wo_lo.release()
    y2h.release()
    y2l.release()
    xt_rl.release()
    res1.release()
    x1_t.release()
    for b_ in sq_t:
        b_.release()
    spill_x1 = sG.n
    phase_barrier()


    def absmax_init(col):
        E("dv", lambda e, o=sb["red"].ap()[:, col:col + 1]: e.memset(o, 0.0))

    def absmax_acc(src_ap, col):
        vf = E("dv", lambda e, o=sb["redt"].ap(), i=src_ap:
               e.tensor_reduce(o, i, AX.X, ALU.max, apply_absolute_value=True), sDV)
        W("dv", sDV, vf)
        E("dv", lambda e, o=sb["red"].ap()[:, col:col + 1], a=sb["red"].ap()[:, col:col + 1],
          b=sb["redt"].ap(): e.tensor_tensor(o, a, b, ALU.max))

    def gmax_finish(col, slot):
        bank_wait(2)
        vd = E("dv", lambda e: e.engine_nop(), sDV)
        W("pe", sDV, vd)
        vpe = E("pe", lambda e, o=psum[2].ap()[0:1, 0:128], i=sb["red"].ap()[:, col:col + 1],
                idn=sb["ident"].ap(): e.transpose(o, i, idn), sPE)
        W("dv", sPE, vpe)
        vd = E("dv", lambda e, o=sb["qs"].ap()[0:1, slot:slot + 1], i=psum[2].ap()[0:1, 0:128]:
               e.tensor_reduce(o, i, AX.X, ALU.max), sDV)
        W("dv", sDV, vd)
        bank_free[2] = (sDV, vd)
        return vd

    def coll_max():
        vd = sDV.n
        W("gp", sDV, vd)
        vg = DMA("gp", cc_max_in.ap(), sb["qs"].ap())
        W("gp", sG, vg)
        vcc = E("gp", lambda e: e.collective_compute(
            "AllReduce", ALU.max, replica_groups=[list(range(8))],
            ins=[cc_max_in.ap().opt()], outs=[cc_max_out.ap().opt()]), sCC)
        W("gp", sCC, vcc)
        return DMA("gp", sb["qs"].ap(), cc_max_out.ap())

    def scale_calc(slot_max, slot_s, slot_r, qmax, vbi):
        W("dv", sG, vbi)
        Q = sb["qs"].ap()
        vf = E("dv", lambda e, o=Q[0:1, slot_s:slot_s + 1], i=Q[0:1, slot_max:slot_max + 1]:
               e.tensor_scalar(o, i, 1.0 / qmax, 1e-12, ALU.mult, ALU.max), sDV)
        W("dv", sDV, vf)
        vr = E("dv", lambda e, o=Q[0:1, slot_r:slot_r + 1], i=Q[0:1, slot_s:slot_s + 1]:
               e.reciprocal(o, i), sDV)
        W("dv", sDV, vr)
        return vr

    def col_bcast(slot, col, vd):
        bank_wait(2)
        W("pe", sDV, vd)
        vpe = E("pe", lambda e, o=psum[2].ap()[:, 0:1], a=sb["ones1"].ap()[0:1, :],
                b=sb["qs"].ap()[0:1, slot:slot + 1]:
                e.matmul(o, a, b, start=True, stop=True), sPE)
        W("dv", sPE, vpe)
        vd2 = E("dv", lambda e, o=sb["qcol"].ap()[:, col:col + 1], i=psum[2].ap()[:, 0:1]:
                e.tensor_copy(o, i), sDV)
        bank_free[2] = (sDV, vd2)
        return vd2

    x1w = [Buf(af, DC * TTL), Buf(af, DC * TTL)]
    qw = Buf(af, DC * TTL)
    absmax_init(0)
    W("sy", sG, spill_x1)
    va_w = [0, 0]
    for tt in range(NTT):
        par = tt % 2
        W("sy", sDV, va_w[par])
        vld = DMA("sy", x1w[par].ap().rearrange("p (c t) -> p c t", c=DC),
                  hbm["sp_x1"].rearrange("c p t -> p c t")[:, :, tt * TTL:(tt + 1) * TTL])
        W("dv", sW, vld)
        absmax_acc(x1w[par].ap(), 0)
        va_w[par] = E("dv", lambda e: e.engine_nop(), sDV)
    gmax_finish(0, 0)
    vbi = coll_max()
    vd = scale_calc(0, 1, 2, 32767.0, vbi)
    vd = col_bcast(1, 0, vd)
    vd = col_bcast(2, 1, vd)
    scales_x1 = vd

    xq_hi = Buf(ab, DC * T)
    xq_lo = Buf(ab, DC * T)
    va_q = [sDV.n, sDV.n]
    for tt in range(NTT):
        par = tt % 2
        W("sy", sDV, va_q[par])
        vld = DMA("sy", x1w[par].ap().rearrange("p (c t) -> p c t", c=DC),
                  hbm["sp_x1"].rearrange("c p t -> p c t")[:, :, tt * TTL:(tt + 1) * TTL])
        W("ac", sW, vld)
        if tt == 0:
            W("ac", sDV, scales_x1)
        va = E("ac", lambda e, o=x1w[par].ap(), i=x1w[par].ap(), r=sb["qcol"].ap()[:, 1:2],
               c=sb["rndc"].ap()[:, 0:1]:
               e.activation(o, i, AF.Identity, bias=c, scale=r), sAC)
        W("dv", sAC, va)
        E("dv", lambda e, o=qw.ap(), i=x1w[par].ap():
          e.tensor_scalar(o, i, -RND_C, None, ALU.add))
        E("dv", lambda e, o=xq_hi.ap().rearrange("p (c u t) -> p c u t", c=DC, u=NTT)[:, :, tt, :],
          i=qw.ap().rearrange("p (c t) -> p c t", c=DC),
          s=sb["qcol"].ap()[:, 0:1]: e.tensor_scalar(o, i, s, None, ALU.mult))
        vq = E("dv", lambda e, o=xq_lo.ap().rearrange("p (c u t) -> p c u t", c=DC, u=NTT)[:, :, tt, :],
               a=qw.ap().rearrange("p (c t) -> p c t", c=DC),
               s=sb["qcol"].ap()[:, 0:1],
               b=xq_hi.ap().rearrange("p (c u t) -> p c u t", c=DC, u=NTT)[:, :, tt, :]:
               e.scalar_tensor_tensor(o, a, s, b, ALU.mult, ALU.subtract), sDV)
        va_q[par] = vq
    for b_ in x1w:
        b_.release()
    qw.release()
    xq_done = sDV.n
    phase_barrier()

    wrow2 = [Buf(af, 1024), Buf(af, 1024)]
    wu2 = [Buf(af, 1024), Buf(af, 1024)]
    wbs2 = [sb["wbs"], sb["wbr"]]
    vd_deq = [0, 0]
    vpe_tr = [0, 0]
    vld_row = [0, 0]

    def prep_dequant(wname, oc, sec, par):
        nonlocal vd_deq, vld_row
        nb = 32
        W("sy", sDV, vd_deq[par])
        W("sy", sPE, vpe_tr[par])
        v = DMA("sy", wrow2[par].ap(), hbm[wname][oc * 128:(oc + 1) * 128,
                                                  sec * 1024:(sec + 1) * 1024])
        vld_row[par] = v
        W("dv", sW, v)
        W("dv", sPE, vpe_tr[par])
        bsl = sb["wbs"].ap()[:, par * 32:par * 32 + 32]
        brl = sb["wbr"].ap()[:, par * 32:par * 32 + 32]
        vf = E("dv", lambda e, o=bsl.unsqueeze(2),
               i=wrow2[par].ap().rearrange("p (b i) -> p b i", b=nb):
               e.tensor_reduce(o, i, AX.X, ALU.max, apply_absolute_value=True), sDV)
        W("dv", sDV, vf)
        vf = E("dv", lambda e, o=bsl, i=bsl:
               e.tensor_scalar(o, i, 1.0 / 7.0, 1e-12, ALU.mult, ALU.max), sDV)
        W("dv", sDV, vf)
        vf = E("dv", lambda e, o=brl, i=bsl: e.reciprocal(o, i), sDV)
        W("dv", sDV, vf)
        vd = E("dv", lambda e, o=wu2[par].ap().rearrange("p (b i) -> p b i", b=nb),
               a=wrow2[par].ap().rearrange("p (b i) -> p b i", b=nb),
               b=brl.unsqueeze(2).broadcast_to([128, nb, 32]):
               e.tensor_tensor(o, a, b, ALU.mult), sDV)
        W("ac", sDV, vd)
        va = E("ac", lambda e, o=wu2[par].ap(), i=wu2[par].ap(), c=sb["rndc"].ap()[:, 0:1]:
               e.activation(o, i, AF.Identity, bias=c), sAC)
        W("dv", sAC, va)
        vd = E("dv", lambda e, o=wu2[par].ap().rearrange("p (b i) -> p b i", b=nb),
               a=wu2[par].ap().rearrange("p (b i) -> p b i", b=nb),
               s=-RND_C,
               b=bsl.unsqueeze(2).broadcast_to([128, nb, 32]):
               e.scalar_tensor_tensor(o, a, s, b, ALU.add, ALU.mult), sDV)
        vd_deq[par] = vd
        return vd

    def prep_transpose(par, dst_hi, dst_lo, dst_off_fn):
        nonlocal vpe_tr
        W("pe", sDV, vd_deq[par])
        for kcl in range(8):
            bank = 2 + (kcl % 2)
            bank_wait(bank)
            vpe = E("pe", lambda e, o=psum[bank].ap()[:, 0:128],
                    i=wu2[par].ap(kcl * 128, 128), idn=sb["ident"].ap():
                    e.transpose(o, i, idn), sPE)
            W("ac", sPE, vpe)
            va = E("ac", lambda e, o=dst_hi.ap(dst_off_fn(kcl), 128),
                   i=psum[bank].ap()[:, 0:128]: e.copy(o, i), sAC)
            W("dv", sAC, va)
            vdl = E("dv", lambda e, o=dst_lo.ap(dst_off_fn(kcl), 128),
                    a=psum[bank].ap()[:, 0:128], b=dst_hi.ap(dst_off_fn(kcl), 128):
                    e.tensor_tensor(o, a, b, ALU.subtract), sDV)
            bank_free[bank] = (sDV, vdl)
        vpe_tr[par] = vpe
        return vdl

    wq_hi2 = [Buf(ab, KC * 128), Buf(ab, KC * 128)]
    wq_lo2 = [Buf(ab, KC * 128), Buf(ab, KC * 128)]
    up_ev2 = [Buf(af, TTL), Buf(af, TTL)]
    vg_ev = [0, 0]
    absmax_init(1)
    absmax_init(2)
    vpe_b = 0
    jobs = [("w_up", "sp_up", 1, oc) for oc in range(IC)] + \
           [("w_gate", "sp_gate", 2, oc) for oc in range(IC)]
    vdl_tr = [0, 0]
    prep_dequant(jobs[0][0], jobs[0][3], 0, 0)
    vdl_tr[0] = prep_transpose(0, wq_hi2[0], wq_lo2[0], lambda kcl: kcl * 128)
    for j, (wname, spname, rcol, oc) in enumerate(jobs):
        par = j % 2
        if j + 1 < len(jobs):
            prep_dequant(jobs[j + 1][0], jobs[j + 1][3], 0, 1 - par)
        W("pe", sDV, vdl_tr[par])
        if j == 0:
            W("pe", sDV, xq_done)
        for tt in range(NTT):
            bank = tt % 2
            bank_wait(bank)
            vpe_b = mm3(bank, KC,
                        lambda kc, par=par: (wq_hi2[par].ap(kc * 128, 128),
                                             wq_lo2[par].ap(kc * 128, 128)),
                        lambda kc, tt=tt: (xq_hi.ap(kc * T + tt * TTL, TTL),
                                           xq_lo.ap(kc * T + tt * TTL, TTL)))
            epar = tt % 2
            W("ac", sPE, vpe_b)
            W("ac", sG, vg_ev[epar])
            va2 = E("ac", lambda e, o=up_ev2[epar].ap(), i=psum[bank].ap()[:, 0:TTL]:
                    e.copy(o, i), sAC)
            W("dv", sAC, va2)
            absmax_acc(up_ev2[epar].ap(), rcol)
            vd = E("dv", lambda e: e.engine_nop(), sDV)
            bank_free[bank] = (sDV, vd)
            W("gp", sDV, vd)
            vg_ev[epar] = DMA("gp", hbm[spname][oc, :, tt * TTL:(tt + 1) * TTL],
                              up_ev2[epar].ap())
        if j + 1 < len(jobs):
            vdl_tr[1 - par] = prep_transpose(1 - par, wq_hi2[1 - par], wq_lo2[1 - par],
                                             lambda kcl: kcl * 128)
        xq_hi.release()
    xq_lo.release()
    gmax_finish(1, 3)
    gmax_finish(2, 4)
    vbi = coll_max()
    vd = scale_calc(3, 5, 6, 32767.0, vbi)
    vd = scale_calc(4, 7, 8, 32767.0, vd * 0 + vbi)
    vd = col_bcast(5, 2, vd)
    vd = col_bcast(6, 3, vd)
    vd = col_bcast(7, 4, vd)
    vd = col_bcast(8, 5, vd)
    scales_ug = vd

    phase_barrier()

    djobs = [(oc, sec) for oc in range(DC) for sec in range(3)]
    prep_dequant("w_down", djobs[0][0], djobs[0][1], 0)
    vg_wdn = [0, 0]
    for j, (oc, sec) in enumerate(djobs):
        par = j % 2
        if j + 1 < len(djobs):
            prep_dequant("w_down", djobs[j + 1][0], djobs[j + 1][1], 1 - par)
        W("ac", sG, vg_wdn[par])
        W("dv", sG, vg_wdn[par])
        vprep = prep_transpose(par, wq_hi2[par], wq_lo2[par], lambda kcl: kcl * 128)
        W("gp", sDV, vprep)
        DMA("gp", hbm["sp_wdn_hi"].rearrange("c k p m -> p c k m")[:, oc, sec * 8:(sec + 1) * 8, :],
            wq_hi2[par].ap().rearrange("p (k m) -> p k m", k=8))
        vg_wdn[par] = DMA("gp", hbm["sp_wdn_lo"].rearrange("c k p m -> p c k m")[:, oc, sec * 8:(sec + 1) * 8, :],
                          wq_lo2[par].ap().rearrange("p (k m) -> p k m", k=8))
    W("gp", sG, vg_wdn[0])
    W("gp", sG, vg_wdn[1])
    phase_barrier()
    for b_ in wrow2 + wu2 + up_ev2 + wq_hi2 + wq_lo2:
        b_.release()
    wdn_spilled = sG.n
    phase_barrier()

    phase_barrier()
    gw = [Buf(af, T), Buf(af, T)]
    qgw = Buf(af, T)
    sigw = Buf(af, T)
    absmax_init(3)
    va_d = [0, 0]
    vg_d = [0, 0]
    for oc in range(IC):
        par = oc % 2
        W("sy", sAC, va_d[par])
        W("sy", sG, vg_d[par])
        vld = DMA("sy", gw[par].ap(), hbm["sp_gate"][oc])
        W("ac", sW, vld)
        if oc == 0:
            W("ac", sDV, scales_ug)
        va = E("ac", lambda e, o=gw[par].ap(), i=gw[par].ap(), r=sb["qcol"].ap()[:, 5:6],
               c=sb["rndc"].ap()[:, 0:1]:
               e.activation(o, i, AF.Identity, bias=c, scale=r), sAC)
        va_d[par] = va
        W("dv", sAC, va)
        vd = E("dv", lambda e, o=qgw.ap(), i=gw[par].ap():
               e.tensor_scalar(o, i, -RND_C, None, ALU.add), sDV)
        W("ac", sDV, vd)
        E("ac", lambda e, o=sigw.ap(), i=qgw.ap(), s=sb["qcol"].ap()[:, 4:5]:
          e.activation(o, i, AF.Sigmoid, scale=s))
        va = E("ac", lambda e, o=sigw.ap(), i=sigw.ap(), c=sb["rndc"].ap()[:, 0:1]:
               e.activation(o, i, AF.Identity, bias=c, scale=65536.0), sAC)
        W("dv", sAC, va)
        E("dv", lambda e, o=sigw.ap(), i=sigw.ap():
          e.tensor_scalar(o, i, -RND_C, 65535.0, ALU.add, ALU.min))
        W("dv", sG, vg_d[par])
        E("dv", lambda e, o=gw[par].ap(), a=qgw.ap(), b=sigw.ap():
          e.tensor_tensor(o, a, b, ALU.mult))
        absmax_acc(gw[par].ap(), 3)
        vdn = E("dv", lambda e: e.engine_nop(), sDV)
        W("gp", sDV, vdn)
        vg_d[par] = DMA("gp", hbm["sp_pg"][oc], gw[par].ap())
    gmax_finish(3, 9)
    vbi = coll_max()
    Q = sb["qs"].ap()
    W("dv", sG, vbi)
    for fn in [
        lambda e, o=Q[0:1, 10:11], i=Q[0:1, 7:8]:
            e.tensor_scalar(o, i, 1.0 / 65536.0, None, ALU.mult),
        lambda e, o=Q[0:1, 11:12], a=Q[0:1, 9:10], b=Q[0:1, 10:11]:
            e.tensor_tensor(o, a, b, ALU.mult),
        lambda e, o=Q[0:1, 11:12], i=Q[0:1, 11:12]:
            e.tensor_scalar(o, i, 1.0 / 32767.0, 1e-12, ALU.mult, ALU.max),
        lambda e, o=Q[0:1, 12:13], i=Q[0:1, 11:12]: e.reciprocal(o, i),
        lambda e, o=Q[0:1, 12:13], a=Q[0:1, 10:11], b=Q[0:1, 12:13]:
            e.tensor_tensor(o, a, b, ALU.mult),
    ]:
        vd = E("dv", fn, sDV)
        W("dv", sDV, vd)
    vd = col_bcast(12, 6, vd)
    scale2_done = vd

    phase_barrier()

    phase_barrier()
    uw = [Buf(af, T), Buf(af, T)]
    quw = sigw
    absmax_init(4)
    va_f = [0, 0]
    va_fu = [0, 0]
    vg_f = [0, 0]
    for oc in range(IC):
        par = oc % 2
        W("sy", sAC, va_d[par])
        W("sy", sG, vg_d[par])
        vld1 = DMA("sy", gw[par].ap(), hbm["sp_pg"][oc])
        W("sy", sAC, va_fu[par])
        W("sy", sG, vg_f[par])
        vld2 = DMA("sy", uw[par].ap(), hbm["sp_up"][oc])
        W("ac", sW, vld1)
        if oc == 0:
            W("ac", sDV, scale2_done)
        va = E("ac", lambda e, o=gw[par].ap(), i=gw[par].ap(), r=sb["qcol"].ap()[:, 6:7],
               c=sb["rndc"].ap()[:, 0:1]:
               e.activation(o, i, AF.Identity, bias=c, scale=r), sAC)
        va_d[par] = va
        W("dv", sAC, va)
        vtt = E("dv", lambda e, o=qgw.ap(), i=gw[par].ap():
               e.tensor_scalar(o, i, -RND_C, None, ALU.add), sDV)
        W("ac", sW, vld2)
        W("ac", sDV, vtt)
        va2 = E("ac", lambda e, o=uw[par].ap(), i=uw[par].ap(), r=sb["qcol"].ap()[:, 3:4],
                c=sb["rndc"].ap()[:, 0:1]:
                e.activation(o, i, AF.Identity, bias=c, scale=r), sAC)
        va_fu[par] = va2
        W("dv", sAC, va2)
        E("dv", lambda e, o=quw.ap(), i=uw[par].ap():
          e.tensor_scalar(o, i, -RND_C, None, ALU.add))
        W("dv", sG, vg_f[par])
        E("dv", lambda e, o=uw[par].ap(), a=qgw.ap(), b=quw.ap():
          e.tensor_tensor(o, a, b, ALU.mult))
        absmax_acc(uw[par].ap(), 4)
        vdn = E("dv", lambda e: e.engine_nop(), sDV)
        W("gp", sDV, vdn)
        vg_f[par] = DMA("gp", hbm["sp_po"][oc], uw[par].ap())
    for b_ in gw + uw:
        b_.release()
    qgw.release()
    sigw.release()
    gmax_finish(4, 13)
    vbi = coll_max()
    W("dv", sG, vbi)
    for fn in [
        lambda e, o=Q[0:1, 14:15], a=Q[0:1, 11:12], b=Q[0:1, 5:6]:
            e.tensor_tensor(o, a, b, ALU.mult),
        lambda e, o=Q[0:1, 15:16], a=Q[0:1, 13:14], b=Q[0:1, 14:15]:
            e.tensor_tensor(o, a, b, ALU.mult),
        lambda e, o=Q[0:1, 15:16], i=Q[0:1, 15:16]:
            e.tensor_scalar(o, i, 1.0 / 32767.0, 1e-12, ALU.mult, ALU.max),
        lambda e, o=Q[0:1, 0:1], i=Q[0:1, 15:16]: e.reciprocal(o, i),
        lambda e, o=Q[0:1, 0:1], a=Q[0:1, 14:15], b=Q[0:1, 0:1]:
            e.tensor_tensor(o, a, b, ALU.mult),
    ]:
        vd = E("dv", fn, sDV)
        W("dv", sDV, vd)
    vd = col_bcast(0, 7, vd)
    vd = col_bcast(15, 1, vd)
    scaleo_done = vd

    phase_barrier()

    phase_barrier()
    TH = min(256, T)
    NTH = T // TH
    oq_hi = Buf(ab, IC * TH)
    oq_lo = Buf(ab, IC * TH)
    wdnb_hi = [Buf(ab, IC * 128), Buf(ab, IC * 128)]
    wdnb_lo = [Buf(ab, IC * 128), Buf(ab, IC * 128)]
    vpe_wdnb = [0, 0]
    res3 = Buf(af, DC * TH)
    x2_t = Buf(af, DC * TH)
    x1row = Buf(af, TH)
    orow = Buf(af, D)
    sqh = [Buf(af, TH), Buf(af, TH)]

    LS = max(TTL, 256)

    def emit_ln_h(src_fn, out_fn, src_ready=0):
        nonlocal vpe_sqh
        bank_wait(5)
        bank_wait(3)
        W("ac", sDV, src_ready)
        for dc in range(DC):
            par = dc % 2
            W("ac", sPE, vpe_sqh[par])
            va = E("ac", lambda e, o=sqh[par].ap(), i=src_fn(dc):
                   e.activation(o, i, AF.Square), sAC)
            W("pe", sAC, va)
            E("pe", lambda e, o=psum[5].ap()[0:1, 0:TH], a=sb["ones1"].ap()[:, 0:1],
              b=src_fn(dc), s=(dc == 0), st=(dc == DC - 1): e.matmul(o, a, b, start=s, stop=st))
            vpe = E("pe", lambda e, o=psum[3].ap()[0:1, 0:TH], a=sb["ones1"].ap()[:, 0:1],
                    b=sqh[par].ap(), s=(dc == 0), st=(dc == DC - 1):
                    e.matmul(o, a, b, start=s, stop=st), sPE)
            vpe_sqh[par] = vpe
        W("dv", sPE, vpe)
        L = sb["lns"].ap()
        E("dv", lambda e, o=L[0:1, 0:TH], i=psum[5].ap()[0:1, 0:TH]:
          e.tensor_scalar(o, i, 1.0 / D, None, ALU.mult))
        E("dv", lambda e, o=L[0:1, LS:LS + TH], i=psum[3].ap()[0:1, 0:TH]:
          e.tensor_scalar(o, i, 1.0 / D, None, ALU.mult))
        E("dv", lambda e, o=L[0:1, 2 * LS:2 * LS + TH], a=L[0:1, 0:TH], b=L[0:1, 0:TH]:
          e.tensor_tensor(o, a, b, ALU.mult))
        E("dv", lambda e, o=L[0:1, LS:LS + TH], a=L[0:1, LS:LS + TH],
          b=L[0:1, 2 * LS:2 * LS + TH]: e.tensor_tensor(o, a, b, ALU.subtract))
        vd = E("dv", lambda e, o=L[0:1, LS:LS + TH], i=L[0:1, LS:LS + TH]:
               e.tensor_scalar(o, i, LN_EPS, None, ALU.add), sDV)
        W("ac", sDV, vd)
        va = E("ac", lambda e, o=L[0:1, LS:LS + TH], i=L[0:1, LS:LS + TH]:
               e.activation(o, i, AF.Sqrt), sAC)
        W("dv", sAC, va)
        vf = E("dv", lambda e, o=L[0:1, 2 * LS:2 * LS + TH], i=L[0:1, LS:LS + TH]:
               e.reciprocal(o, i), sDV)
        W("dv", sDV, vf)
        vd = E("dv", lambda e, o=L[0:1, 3 * LS:3 * LS + TH], a=L[0:1, 0:TH],
               b=L[0:1, 2 * LS:2 * LS + TH]: e.tensor_tensor(o, a, b, ALU.mult), sDV)
        bank_wait(4)
        W("pe", sDV, vd)
        E("pe", lambda e, o=psum[4].ap()[:, 0:TH], a=sb["ones1"].ap()[0:1, :],
          b=L[0:1, 2 * LS:2 * LS + TH]: e.matmul(o, a, b, start=True, stop=True))
        vpe = E("pe", lambda e, o=psum[3].ap()[:, 0:TH], a=sb["ones1"].ap()[0:1, :],
                b=L[0:1, 3 * LS:3 * LS + TH]: e.matmul(o, a, b, start=True, stop=True), sPE)
        W("dv", sPE, vpe)
        for dc in range(DC):
            E("dv", lambda e, o=out_fn(dc), a=src_fn(dc), b=psum[4].ap()[:, 0:TH]:
              e.tensor_tensor(o, a, b, ALU.mult))
            vd = E("dv", lambda e, o=out_fn(dc), a=out_fn(dc), b=psum[3].ap()[:, 0:TH]:
                   e.tensor_tensor(o, a, b, ALU.subtract), sDV)
        bank_free[3] = (sDV, vd)
        bank_free[4] = (sDV, vd)
        bank_free[5] = (sDV, vd)
        return vd

    vpe_sqh = [0, 0]
    va_h = 0
    vst = 0
    IC2 = IC // 2
    pw = Buf(af, IC2 * TH)
    qhw = Buf(af, IC2 * TH)
    for th in range(NTH):
        hs0 = th * TH
        for half in range(2):
            o0 = half * IC2
            W("sy", sDV, va_h)
            vld = DMA("sy", pw.ap().rearrange("p (c t) -> p c t", c=IC2),
                      hbm["sp_po"].rearrange("c p t -> p c t")[:, o0:o0 + IC2, hs0:hs0 + TH])
            W("ac", sW, vld)
            if th == 0 and half == 0:
                W("ac", sDV, scaleo_done)
            W("ac", sDV, va_h)
            va = E("ac", lambda e, o=pw.ap(), i=pw.ap(),
                   r=sb["qcol"].ap()[:, 7:8], c=sb["rndc"].ap()[:, 0:1]:
                   e.activation(o, i, AF.Identity, bias=c, scale=r), sAC)
            W("dv", sAC, va)
            E("dv", lambda e, o=qhw.ap(), i=pw.ap():
              e.tensor_scalar(o, i, -RND_C, None, ALU.add))
            E("dv", lambda e, o=oq_hi.ap(o0 * TH, IC2 * TH), i=qhw.ap(),
              s=sb["qcol"].ap()[:, 1:2]: e.tensor_scalar(o, i, s, None, ALU.mult))
            vq = E("dv", lambda e, o=oq_lo.ap(o0 * TH, IC2 * TH), a=qhw.ap(),
                   s=sb["qcol"].ap()[:, 1:2], b=oq_hi.ap(o0 * TH, IC2 * TH):
                   e.scalar_tensor_tensor(o, a, s, b, ALU.mult, ALU.subtract), sDV)
            va_h = vq
        W("pe", sDV, vq)
        if th == 0:
            W("sy", sG, wdn_spilled)

        def wdn_load(mc, par):
            W("sy", sPE, vpe_wdnb[par])
            DMA("sy", wdnb_hi[par].ap().rearrange("p (k m) -> p k m", k=IC),
                hbm["sp_wdn_hi"].rearrange("c k p m -> p c k m")[:, mc, :, :])
            return DMA("sy", wdnb_lo[par].ap().rearrange("p (k m) -> p k m", k=IC),
                       hbm["sp_wdn_lo"].rearrange("c k p m -> p c k m")[:, mc, :, :])

        vwl_par = [0, 0]
        vwl_par[0] = wdn_load(0, 0)
        for mc in range(DC):
            bank = mc % 2
            par = mc % 2
            W("pe", sW, vwl_par[par])
            bank_wait(bank)
            vpe_h = mm3(bank, IC,
                        lambda kc, par=par: (wdnb_hi[par].ap(kc * 128, 128),
                                             wdnb_lo[par].ap(kc * 128, 128)),
                        lambda kc: (oq_hi.ap(kc * TH, TH), oq_lo.ap(kc * TH, TH)),
                        nw=TH)
            vpe_wdnb[par] = vpe_h
            if mc + 1 < DC:
                vwl_par[1 - par] = wdn_load(mc + 1, 1 - par)
            vldx = DMA("sy", x1row.ap(), hbm["sp_x1"][mc, :, hs0:hs0 + TH])
            W("dv", sPE, vpe_h)
            W("dv", sW, vldx)
            vd = E("dv", lambda e, o=res3.ap(mc * TH, TH), a=psum[bank].ap()[:, 0:TH],
                   b=x1row.ap(): e.tensor_tensor(o, a, b, ALU.add), sDV)
            bank_free[bank] = (sDV, vd)
            W("sy", sDV, vd)
        emit_ln_h(lambda dc: res3.ap(dc * TH, TH), lambda dc: x2_t.ap(dc * TH, TH),
                  src_ready=vd)
        vdl = sDV.n
        W("pe", sDV, vdl)
        for j in range(TH // 128):
            for dc in range(DC):
                bank = 6 + (dc % 2)
                bank_wait(bank)
                vpe = E("pe", lambda e, o=psum[bank].ap()[:, 0:128],
                        i=x2_t.ap(dc * TH + j * 128, 128), idn=sb["ident"].ap():
                        e.transpose(o, i, idn), sPE)
                W("ac", sPE, vpe)
                W("ac", sW, vst)
                va = E("ac", lambda e, o=orow.ap(dc * 128, 128),
                       i=psum[bank].ap()[:, 0:128]: e.copy(o, i), sAC)
                bank_free[bank] = (sAC, va)
            W("sy", sAC, va)
            trow = th * (TH // 128) + j
            vst = DMA("sy", hbm["out"][trow * 128:(trow + 1) * 128, :], orow.ap())

    ops["sy"].append(lambda e, v=sW.n: e.wait_ge(sW.h, v))
    ops["gp"].append(lambda e, v=sG.n: e.wait_ge(sG.h, v))

    with nc.Block() as block:
        @block.sync
        def _(eng):
            for f in ops["sy"]:
                f(eng)

        @block.tensor
        def _(eng):
            for f in ops["pe"]:
                f(eng)

        @block.vector
        def _(eng):
            for f in ops["dv"]:
                f(eng)

        @block.scalar
        def _(eng):
            for f in ops["ac"]:
                f(eng)

        @block.gpsimd
        def _(eng):
            for f in ops["gp"]:
                f(eng)

    es.close()
    return nc


def col_layout(v, DC):
    return np.ascontiguousarray(np.asarray(v, np.float32).reshape(DC, 128).T)


def kernel(L0_attn_x_in, L0_attn_W_in, L0_attn_W_x, L0_attn_W_dt, L0_attn_b_dt,
           L0_attn_A_log, L0_attn_D_param, L0_attn_W_out, L0_attn_ln_g,
           L0_attn_ln_b, L0_ffn_w_gate, L0_ffn_w_up, L0_ffn_w_down):
    cfg = Cfg()
    x = np.ascontiguousarray(np.asarray(L0_attn_x_in, dtype=np.float32))
    Bb, L, D = x.shape
    T, N, DC = cfg.T, cfg.N, cfg.DC

    sel = np.zeros((2 * N, 2 * N * 128), dtype=np.float32)
    for j in range(2 * N):
        sel[j, j * 128:(j + 1) * 128] = 1.0
    base = {
        "W_in": np.ascontiguousarray(L0_attn_W_in, dtype=np.float32),
        "W_x": np.ascontiguousarray(L0_attn_W_x, dtype=np.float32),
        "W_dt": np.ascontiguousarray(L0_attn_W_dt, dtype=np.float32),
        "W_out": np.ascontiguousarray(L0_attn_W_out, dtype=np.float32),
        "w_up": np.ascontiguousarray(L0_ffn_w_up, dtype=np.float32),
        "w_gate": np.ascontiguousarray(L0_ffn_w_gate, dtype=np.float32),
        "w_down": np.ascontiguousarray(L0_ffn_w_down, dtype=np.float32),
        "bdt_col": col_layout(L0_attn_b_dt, DC),
        "alog_col": np.ascontiguousarray(
            np.asarray(L0_attn_A_log, np.float32).reshape(DC, 128, N)
            .transpose(1, 0, 2).reshape(128, DC * N)),
        "dpar_col": col_layout(L0_attn_D_param, DC),
        "lng_col": col_layout(L0_attn_ln_g, DC),
        "lnb_col": col_layout(L0_attn_ln_b, DC),
        "ident": np.eye(128, dtype=np.float32),
        "ones1": np.ones((128, 128), dtype=np.float32),
        "identb": np.eye(128, dtype=ml_dtypes.bfloat16),
        "sel": sel.astype(ml_dtypes.bfloat16),
    }
    in_maps = []
    for c in range(8):
        b, h = c // 2, c % 2
        im = dict(base)
        im["x_in"] = np.ascontiguousarray(x[b, h * T:(h + 1) * T, :])
        mask = np.zeros((128, 2), np.float32)
        mask[:, 0] = 1.0 if h == 0 else 0.0
        mask[:, 1] = 1.0 if h == 1 else 0.0
        im["mask_col"] = mask
        in_maps.append(im)

    nc = build_nc(cfg)
    res = run_bass_kernel_spmd(nc, in_maps, core_ids=list(range(8)))
    out = np.empty((Bb, L, D), np.float32)
    for c in range(8):
        b, h = c // 2, c % 2
        out[b, h * T:(h + 1) * T, :] = res.results[c]["out"]
    return out


# version 10 (final)
# speedup vs baseline: 1.5074x; 1.0094x over previous
import numpy as np
import ml_dtypes
import concourse.bass as bass
import concourse.mybir as mybir
from concourse.bass_utils import run_bass_kernel_spmd

F32 = mybir.dt.float32
BF16 = mybir.dt.bfloat16
AF = mybir.ActivationFunctionType
ALU = mybir.AluOpType
AX = mybir.AxisListType

RND_C = float(1.5 * 2 ** 23)
LN_EPS = 1e-5


class Cfg:
    def __init__(self, T=2048, D=1024, N=16, DI=3072, TTL=512):
        self.T, self.D, self.N, self.DI, self.TTL = T, D, N, DI, TTL
        self.NTT = T // TTL
        self.DC = D // 128
        self.IC = DI // 128
        self.KC = D // 128


class Sem:
    def __init__(self, es, nc, name):
        self.h = es.enter_context(nc.semaphore(name))
        self.n = 0

    def inc(self, k=1):
        self.n += k
        return self.n


class Arena:
    def __init__(self, nc, name, cols, dtype):
        self.t = nc.alloc_sbuf_tensor(name, [128, cols], dtype)
        self.free_list = [(0, cols)]

    def alloc(self, n, align=8):
        for i, (s, e) in enumerate(self.free_list):
            a = (s + align - 1) // align * align
            if a + n <= e:
                segs = []
                if a > s:
                    segs.append((s, a))
                if a + n < e:
                    segs.append((a + n, e))
                self.free_list[i:i + 1] = segs
                return a
        raise RuntimeError(f"arena OOM: need {n}, free={self.free_list}")

    def free(self, off, n):
        self.free_list.append((off, off + n))
        self.free_list.sort()
        m = []
        for s, e in self.free_list:
            if m and m[-1][1] >= s:
                m[-1] = (m[-1][0], max(m[-1][1], e))
            else:
                m.append((s, e))
        self.free_list = [(s, e) for s, e in m]


class Buf:
    def __init__(self, arena, cols, align=8):
        self.arena = arena
        self.off = arena.alloc(cols, align)
        self.cols = cols

    def ap(self, c0=0, n=None, rows=128, row0=0):
        if n is None:
            n = self.cols - c0
        return self.arena.t.ap()[row0:row0 + rows, self.off + c0:self.off + c0 + n]

    def release(self):
        self.arena.free(self.off, self.cols)


def build_nc(cfg, dbg_spills=()):
    import contextlib
    es = contextlib.ExitStack()
    nc = bass.Bass("TRN2", target_bir_lowering=False, debug=False,
                   detect_race_conditions=False)
    T, D, N, DI, TTL = cfg.T, cfg.D, cfg.N, cfg.DI, cfg.TTL
    DC, IC, KC, NTT = cfg.DC, cfg.IC, cfg.KC, cfg.NTT
    TJ = TTL // 128

    hbm = {}
    hbm["x_in"] = nc.dram_tensor("x_in", [T, D], F32, kind="ExternalInput").ap()
    for wn, shape in [("W_in", [D, 2 * D]), ("W_x", [D, D + 2 * N]),
                      ("W_dt", [D, D]), ("W_out", [D, D]),
                      ("w_up", [DI, D]), ("w_gate", [DI, D]), ("w_down", [D, DI])]:
        hbm[wn] = nc.dram_tensor(wn, shape, F32, kind="ExternalInput").ap()
    for cn, shape, dt in [("ident", [128, 128], F32), ("ones1", [128, 128], F32),
                          ("identb", [128, 128], BF16),
                          ("sel", [2 * N, 2 * N * 128], BF16),
                          ("bdt_col", [128, DC], F32), ("alog_col", [128, DC * N], F32),
                          ("dpar_col", [128, DC], F32), ("lng_col", [128, DC], F32),
                          ("lnb_col", [128, DC], F32), ("mask_col", [128, 2], F32)]:
        hbm[cn] = nc.dram_tensor(cn, shape, dt, kind="ExternalInput").ap()
    hbm["out"] = nc.dram_tensor("out", [T, D], F32, kind="ExternalOutput").ap()

    for k, shape, dt in [
        ("sp_xt_f32", [DC, 128, T], F32), ("sp_xt_hi", [DC, 128, T], BF16),
        ("sp_xt_lo", [DC, 128, T], BF16),
        ("sp_x_f32", [DC, 128, T], F32), ("sp_x_hi", [DC, 128, T], BF16),
        ("sp_x_lo", [DC, 128, T], BF16),
        ("sp_sz", [DC, 128, T], F32),
        ("sp_dr_hi", [DC, 128, T], BF16), ("sp_dr_lo", [DC, 128, T], BF16),
        ("sp_S", [DC, 128, T], F32), ("sp_y", [DC, 128, T], F32),
        ("sp_x1", [DC, 128, T], F32),
        ("sp_up", [IC, 128, T], F32), ("sp_gate", [IC, 128, T], F32),
        ("sp_wdn_hi", [DC, IC, 128, 128], BF16), ("sp_wdn_lo", [DC, IC, 128, 128], BF16),
        ("sp_pg", [IC, 128, T], F32), ("sp_po", [IC, 128, T], F32),
    ]:
        kind = "ExternalOutput" if k in dbg_spills else "Internal"
        hbm[k] = nc.dram_tensor(k, list(shape), dt, kind=kind).ap()

    cc_carry_in = nc.dram_tensor("cc_carry_in", [128, N * DC], F32)
    cc_carry_out = nc.dram_tensor("cc_carry_out", [128, N * DC], F32)
    cc_max_in = nc.dram_tensor("cc_max_in", [1, 16], F32)
    cc_max_out = nc.dram_tensor("cc_max_out", [1, 16], F32)

    sb = {}
    for cn, shape, dt in [("ident", [128, 128], F32), ("ones1", [128, 128], F32),
                          ("identb", [128, 128], BF16),
                          ("sel", [2 * N, 2 * N * 128], BF16),
                          ("bdt_col", [128, DC], F32), ("alog_col", [128, DC * N], F32),
                          ("dpar_col", [128, DC], F32), ("lng_col", [128, DC], F32),
                          ("lnb_col", [128, DC], F32), ("mask_col", [128, 2], F32),
                          ("A", [128, DC * N], F32), ("zeros", [128, TTL], F32),
                          ("rndc", [128, 1], F32), ("state", [128, N * DC], F32),
                          ("scar", [128, DC], F32),
                          ("bch", [2 * N, T], BF16), ("bcl", [2 * N, T], BF16),
                          ("lns", [1, 4 * max(TTL, 256)], F32),
                          ("red", [128, 8], F32), ("redt", [128, 1], F32),
                          ("qs", [1, 16], F32), ("qcol", [128, 8], F32),
                          ("wbs", [128, 64], F32), ("wbr", [128, 64], F32)]:
        sb[cn] = nc.alloc_sbuf_tensor("sb_" + cn, shape, dt)

    psum = [es.enter_context(nc.psum_tensor(f"pb{i}", [128, 512], F32)) for i in range(8)]

    af = Arena(nc, "af32", 24100, F32)
    ab = Arena(nc, "abf16", 43040, BF16)

    ops = {"pe": [], "dv": [], "ac": [], "gp": [], "sy": []}
    sW = Sem(es, nc, "sW")
    sG = Sem(es, nc, "sG")
    sPE = Sem(es, nc, "sPE")
    sDV = Sem(es, nc, "sDV")
    sAC = Sem(es, nc, "sAC")
    sCC = Sem(es, nc, "sCC")

    def E(q, fn, sem=None, inc=1):
        if sem is None:
            ops[q].append(fn)
            return None
        ops[q].append(lambda e, f=fn, h=sem.h, k=inc: f(e).then_inc(h, k))
        return sem.inc(inc)

    def W(q, sem, v):
        if v and v > 0:
            ops[q].append(lambda e, h=sem.h, v=v: e.wait_ge(h, v))

    def DMA(q, dst, src):
        sem = sW if q == "sy" else sG
        ops[q].append(lambda e, d=dst, s=src: e.dma_start(out=d, in_=s).then_inc(sem.h, 16))
        return sem.inc(16)

    bank_free = [None] * 8

    def bank_wait(j):
        if bank_free[j] is not None:
            sem, v = bank_free[j]
            W("pe", sem, v)

    def phase_barrier():
        for q in ("sy", "pe", "dv", "ac", "gp"):
            for sem in (sW, sG, sPE, sDV, sAC, sCC):
                W(q, sem, sem.n)

    for name in ["ident", "ones1", "identb", "sel", "bdt_col", "alog_col",
                 "dpar_col", "lng_col", "lnb_col", "mask_col"]:
        DMA("sy", sb[name].ap(), hbm[name])
    w_consts = sW.n

    E("dv", lambda e: e.memset(sb["zeros"].ap(), 0.0))
    E("dv", lambda e: e.memset(sb["rndc"].ap(), RND_C))
    E("dv", lambda e: e.memset(sb["state"].ap(), 0.0))
    E("dv", lambda e: e.memset(sb["scar"].ap(), 0.0))
    E("dv", lambda e: e.memset(sb["qs"].ap(), 0.0))
    E("dv", lambda e: e.memset(sb["red"].ap(), 0.0), sDV)
    W("ac", sW, w_consts)
    va = E("ac", lambda e: e.activation(sb["A"].ap(), sb["alog_col"].ap(), AF.Exp), sAC)
    W("dv", sAC, va)
    E("dv", lambda e: e.tensor_scalar(sb["A"].ap(), sb["A"].ap(), -1.0, None, ALU.mult), sDV)

    def load_split_weight(wname, rows, cols, f32buf, hibuf, lobuf):
        nkc = rows // 128
        v = DMA("sy", f32buf.ap().rearrange("p (k m) -> p k m", k=nkc),
                hbm[wname].rearrange("(k p) m -> p k m", p=128))
        W("ac", sW, v)
        va = E("ac", lambda e, o=hibuf.ap(), i=f32buf.ap(): e.copy(o, i), sAC)
        W("dv", sAC, va)
        vd = E("dv", lambda e, o=lobuf.ap(), a=f32buf.ap(), b=hibuf.ap():
               e.tensor_tensor(o, a, b, ALU.subtract), sDV)
        W("sy", sDV, vd)
        W("gp", sDV, vd)
        return vd

    def mm3(bank, nkc, get_w, get_x, mw=128, nw=None):
        if nw is None:
            nw = TTL
        for kc in range(nkc):
            wh, wl = get_w(kc)
            xh, xl = get_x(kc)
            E("pe", lambda e, o=psum[bank].ap()[0:mw, 0:nw], a=wh, b=xh, s=(kc == 0):
              e.matmul(o, a, b, start=s, stop=False))
            E("pe", lambda e, o=psum[bank].ap()[0:mw, 0:nw], a=wh, b=xl:
              e.matmul(o, a, b, start=False, stop=False))
            v = E("pe", lambda e, o=psum[bank].ap()[0:mw, 0:nw], a=wl, b=xh,
                  st=(kc == nkc - 1): e.matmul(o, a, b, start=False, stop=st),
                  sPE if kc == nkc - 1 else None)
        return v

    w_in_f32 = Buf(af, 0) if False else None
    xt_f = Buf(af, DC * TTL)
    xt_h = Buf(ab, DC * TTL)
    xt_l = Buf(ab, DC * TTL)
    xrow = [Buf(af, D), Buf(af, D)]
    W("pe", sW, w_consts)
    vpe_row = [0, 0]
    va_evac = 0
    for tt in range(NTT):
        for j in range(TJ):
            par = j % 2
            trow = tt * TJ + j
            W("sy", sPE, vpe_row[par])
            vld = DMA("sy", xrow[par].ap(), hbm["x_in"][trow * 128:(trow + 1) * 128, :])
            W("pe", sW, vld)
            for dc in range(DC):
                bank = dc % 4
                bank_wait(bank)
                vpe = E("pe", lambda e, o=psum[bank].ap()[:, 0:128],
                        i=xrow[par].ap(dc * 128, 128), idn=sb["ident"].ap():
                        e.transpose(o, i, idn), sPE)
                W("ac", sPE, vpe)
                va = E("ac", lambda e, o=xt_f.ap(dc * TTL + j * 128, 128),
                       i=psum[bank].ap()[:, 0:128]: e.copy(o, i), sAC)
                bank_free[bank] = (sAC, va)
            vpe_row[par] = vpe
            va_evac = va
        W("dv", sAC, va_evac)
        E("dv", lambda e, o=xt_h.ap(), i=xt_f.ap(): e.tensor_copy(o, i))
        vd = E("dv", lambda e, o=xt_l.ap(), a=xt_f.ap(), b=xt_h.ap():
               e.tensor_tensor(o, a, b, ALU.subtract), sDV)
        W("gp", sDV, vd)
        DMA("gp", hbm["sp_xt_f32"].rearrange("c p t -> p c t")[:, :, tt * TTL:(tt + 1) * TTL],
            xt_f.ap().rearrange("p (c t) -> p c t", c=DC))
        DMA("gp", hbm["sp_xt_hi"].rearrange("c p t -> p c t")[:, :, tt * TTL:(tt + 1) * TTL],
            xt_h.ap().rearrange("p (c t) -> p c t", c=DC))
        vg = DMA("gp", hbm["sp_xt_lo"].rearrange("c p t -> p c t")[:, :, tt * TTL:(tt + 1) * TTL],
                 xt_l.ap().rearrange("p (c t) -> p c t", c=DC))
        W("ac", sG, vg)
        W("dv", sG, vg)
    for b_ in xrow:
        b_.release()
    xt_f.release()
    xt_h.release()
    xt_l.release()
    spill_xt = sG.n
    phase_barrier()

    wf = Buf(af, 0) if False else None
    w_f = Buf(af, KC * 128)
    w_f.release()

    w1f = Buf(af, 0) if False else None
    w1_f32 = Buf(af, 0) if False else None

    win_f = Buf(af, KC * 2 * D // 4) if False else None
    win_hi = Buf(ab, KC * 2 * D)
    win_lo = Buf(ab, KC * 2 * D)
    wstage = Buf(af, KC * D)
    for half in range(2):
        v = DMA("sy", wstage.ap().rearrange("p (k m) -> p k m", k=KC),
                hbm["W_in"][:, half * D:(half + 1) * D].rearrange("(k p) m -> p k m", p=128))
        W("ac", sW, v)
        va = E("ac", lambda e, o=win_hi.ap() if False else None: None) if False else None
        for kc in range(KC):
            va = E("ac", lambda e, o=win_hi.ap(kc * 2 * D + half * D, D),
                   i=wstage.ap(kc * D, D): e.copy(o, i), sAC)
        W("dv", sAC, va)
        for kc in range(KC):
            vd = E("dv", lambda e, o=win_lo.ap(kc * 2 * D + half * D, D),
                   a=wstage.ap(kc * D, D), b=win_hi.ap(kc * 2 * D + half * D, D):
                   e.tensor_tensor(o, a, b, ALU.subtract), sDV)
        W("sy", sDV, vd)
    v_win = vd
    W("gp", sDV, vd)
    wstage.release()

    xth = Buf(ab, DC * TTL)
    xtl = Buf(ab, DC * TTL)
    evf = [Buf(af, TTL), Buf(af, TTL)]
    evh = [Buf(ab, TTL), Buf(ab, TTL)]
    evl = [Buf(ab, TTL), Buf(ab, TTL)]
    vg_par = [0, 0]
    W("pe", sDV, v_win)
    W("sy", sG, spill_xt)
    vpe_tt = 0
    for tt in range(NTT):
        W("sy", sPE, vpe_tt)
        DMA("sy", xth.ap().rearrange("p (c t) -> p c t", c=DC),
            hbm["sp_xt_hi"].rearrange("c p t -> p c t")[:, :, tt * TTL:(tt + 1) * TTL])
        vld = DMA("sy", xtl.ap().rearrange("p (c t) -> p c t", c=DC),
                  hbm["sp_xt_lo"].rearrange("c p t -> p c t")[:, :, tt * TTL:(tt + 1) * TTL])
        W("pe", sW, vld)
        for mc in range(2 * DC):
            bank = mc % 2
            bank_wait(bank)
            vpe_tt = mm3(bank, KC,
                         lambda kc, mc=mc: (win_hi.ap(kc * 2 * D + mc * 128, 128),
                                            win_lo.ap(kc * 2 * D + mc * 128, 128)),
                         lambda kc: (xth.ap(kc * TTL, TTL), xtl.ap(kc * TTL, TTL)))
            W("ac", sPE, vpe_tt)
            par = mc % 2
            W("ac", sG, vg_par[par])
            W("dv", sG, vg_par[par])
            if mc < DC:
                va = E("ac", lambda e, o=evf[par].ap(), i=psum[bank].ap()[:, 0:TTL]: e.copy(o, i), sAC)
                W("dv", sAC, va)
                E("dv", lambda e, o=evh[par].ap(), i=evf[par].ap(): e.tensor_copy(o, i))
                vd = E("dv", lambda e, o=evl[par].ap(), a=evf[par].ap(), b=evh[par].ap():
                       e.tensor_tensor(o, a, b, ALU.subtract), sDV)
                bank_free[bank] = (sAC, va)
                W("gp", sDV, vd)
                DMA("gp", hbm["sp_x_f32"][mc, :, tt * TTL:(tt + 1) * TTL], evf[par].ap())
                DMA("gp", hbm["sp_x_hi"][mc, :, tt * TTL:(tt + 1) * TTL], evh[par].ap())
                vg_par[par] = DMA("gp", hbm["sp_x_lo"][mc, :, tt * TTL:(tt + 1) * TTL], evl[par].ap())
            else:
                va = E("ac", lambda e, o=evf[par].ap(), i=psum[bank].ap()[:, 0:TTL]:
                       e.activation(o, i, AF.Silu), sAC)
                bank_free[bank] = (sAC, va)
                W("gp", sAC, va)
                vg_par[par] = DMA("gp", hbm["sp_sz"][mc - DC, :, tt * TTL:(tt + 1) * TTL], evf[par].ap())
    win_hi.release()
    win_lo.release()
    spill_p1 = sG.n
    phase_barrier()

    MX = D + 2 * N
    wx_f32 = Buf(af, KC * MX)
    wx_hi = Buf(ab, KC * MX)
    wx_lo = Buf(ab, KC * MX)
    v_wx = load_split_weight("W_x", D, MX, wx_f32, wx_hi, wx_lo)
    wx_f32.release()
    bcf = Buf(af, T)
    W("pe", sDV, v_wx)
    W("sy", sG, spill_p1)
    vpe_tt = 0
    for tt in range(NTT):
        W("sy", sPE, vpe_tt)
        DMA("sy", xth.ap().rearrange("p (c t) -> p c t", c=DC),
            hbm["sp_x_hi"].rearrange("c p t -> p c t")[:, :, tt * TTL:(tt + 1) * TTL])
        vld = DMA("sy", xtl.ap().rearrange("p (c t) -> p c t", c=DC),
                  hbm["sp_x_lo"].rearrange("c p t -> p c t")[:, :, tt * TTL:(tt + 1) * TTL])
        W("pe", sW, vld)
        for mc in range(DC + 1):
            bank = mc % 2
            mw = 128 if mc < DC else 2 * N
            bank_wait(bank)
            vpe_tt = mm3(bank, KC,
                         lambda kc, mc=mc, mw=mw: (wx_hi.ap(kc * MX + mc * 128, mw),
                                                   wx_lo.ap(kc * MX + mc * 128, mw)),
                         lambda kc: (xth.ap(kc * TTL, TTL), xtl.ap(kc * TTL, TTL)),
                         mw=mw)
            W("ac", sPE, vpe_tt)
            par = mc % 2
            if mc < DC:
                W("ac", sG, vg_par[par])
                W("dv", sG, vg_par[par])
                va = E("ac", lambda e, o=evh[par].ap(), i=psum[bank].ap()[:, 0:TTL]: e.copy(o, i), sAC)
                W("dv", sAC, va)
                W("dv", sPE, vpe_tt)
                vd = E("dv", lambda e, o=evl[par].ap(), a=psum[bank].ap()[:, 0:TTL], b=evh[par].ap():
                       e.tensor_tensor(o, a, b, ALU.subtract), sDV)
                bank_free[bank] = (sDV, vd)
                W("gp", sDV, vd)
                DMA("gp", hbm["sp_dr_hi"][mc, :, tt * TTL:(tt + 1) * TTL], evh[par].ap())
                vg_par[par] = DMA("gp", hbm["sp_dr_lo"][mc, :, tt * TTL:(tt + 1) * TTL], evl[par].ap())
            else:
                va = E("ac", lambda e, o=bcf.ap(tt * TTL, TTL, rows=2 * N),
                       i=psum[bank].ap()[0:2 * N, 0:TTL]: e.copy(o, i), sAC)
                W("dv", sAC, va)
                E("dv", lambda e, o=sb["bch"].ap()[:, tt * TTL:(tt + 1) * TTL],
                  i=bcf.ap(tt * TTL, TTL, rows=2 * N): e.tensor_copy(o, i))
                vd = E("dv", lambda e, o=sb["bcl"].ap()[:, tt * TTL:(tt + 1) * TTL],
                       a=bcf.ap(tt * TTL, TTL, rows=2 * N),
                       b=sb["bch"].ap()[:, tt * TTL:(tt + 1) * TTL]:
                       e.tensor_tensor(o, a, b, ALU.subtract), sDV)
                bank_free[bank] = (sDV, vd)
    xth.release()
    xtl.release()
    wx_hi.release()
    wx_lo.release()
    bcf.release()
    bc_ready = sDV.n
    spill_p2 = sG.n
    phase_barrier()

    wdt_f32 = Buf(af, KC * D)
    wdt_hi = Buf(ab, KC * D)
    wdt_lo = Buf(ab, KC * D)
    v_wdt = load_split_weight("W_dt", D, D, wdt_f32, wdt_hi, wdt_lo)
    wdt_f32.release()

    drh = Buf(ab, DC * TTL)
    drl = Buf(ab, DC * TTL)
    delta = Buf(af, DC * TTL)
    dx = Buf(af, DC * TTL)
    y_tt = Buf(af, DC * TTL)
    u_w = Buf(af, DC * TTL)
    h_w = Buf(af, DC * TTL)
    s_tc = Buf(af, TTL)
    x_rl = Buf(af, TTL)
    e_tmp = Buf(af, TTL)
    dA_t = [Buf(af, TTL), Buf(af, TTL)]

    W("pe", sDV, v_wdt)
    W("sy", sG, spill_p2)
    vpe_tt = 0
    vd_scan = [0, 0]
    for tt in range(NTT):
        ts0 = tt * TTL
        W("sy", sPE, vpe_tt)
        DMA("sy", drh.ap().rearrange("p (c t) -> p c t", c=DC),
            hbm["sp_dr_hi"].rearrange("c p t -> p c t")[:, :, ts0:ts0 + TTL])
        vld = DMA("sy", drl.ap().rearrange("p (c t) -> p c t", c=DC),
                  hbm["sp_dr_lo"].rearrange("c p t -> p c t")[:, :, ts0:ts0 + TTL])
        W("pe", sW, vld)
        for dc in range(DC):
            bank = dc % 2
            bank_wait(bank)
            vpe_tt = mm3(bank, KC,
                         lambda kc, dc=dc: (wdt_hi.ap(kc * D + dc * 128, 128),
                                            wdt_lo.ap(kc * D + dc * 128, 128)),
                         lambda kc: (drh.ap(kc * TTL, TTL), drl.ap(kc * TTL, TTL)))
            W("ac", sPE, vpe_tt)
            E("ac", lambda e, o=e_tmp.ap(), i=psum[bank].ap()[:, 0:TTL],
              b=sb["bdt_col"].ap()[:, dc:dc + 1]:
              e.activation(o, i, AF.Exp, bias=b))
            va = E("ac", lambda e, o=delta.ap(dc * TTL, TTL), i=e_tmp.ap():
                   e.activation(o, i, AF.Ln, bias=1.0), sAC)

            bank_free[bank] = (sAC, va)
            vx = DMA("sy", x_rl.ap(), hbm["sp_x_f32"][dc, :, ts0:ts0 + TTL])
            W("dv", sAC, va)
            W("dv", sW, vx)
            E("dv", lambda e, o=dx.ap(dc * TTL, TTL), a=delta.ap(dc * TTL, TTL),
              b=x_rl.ap(): e.tensor_tensor(o, a, b, ALU.mult))
            E("dv", lambda e, o=y_tt.ap(dc * TTL, TTL), a=x_rl.ap(),
              s=sb["dpar_col"].ap()[:, dc:dc + 1]:
              e.tensor_scalar(o, a, s, None, ALU.mult))
            vsc = E("dv", lambda e, o=s_tc.ap(), d0=delta.ap(dc * TTL, TTL),
                    z=sb["zeros"].ap(), ini=sb["scar"].ap()[:, dc:dc + 1]:
                    e.tensor_tensor_scan(o, d0, z, ini, ALU.add, ALU.add), sDV)
            W("dv", sDV, vsc)
            vd = E("dv", lambda e, o=sb["scar"].ap()[:, dc:dc + 1], i=s_tc.ap(TTL - 1, 1):
                   e.tensor_copy(o, i), sDV)
            W("sy", sDV, vd)
            W("gp", sDV, vd)
            vg = DMA("gp", hbm["sp_S"][dc, :, ts0:ts0 + TTL], s_tc.ap())
            W("dv", sG, vg)
        for n in range(N):
            pb, pc = 6, 7
            bank_wait(pb)
            bank_wait(pc)
            if tt == 0 and n == 0:
                W("pe", sDV, bc_ready)
            E("pe", lambda e, o=psum[pb].ap()[:, 0:TTL], a=sb["sel"].ap()[:, n * 128:(n + 1) * 128],
              b=sb["bch"].ap()[:, ts0:ts0 + TTL]: e.matmul(o, a, b, start=True, stop=False))
            E("pe", lambda e, o=psum[pb].ap()[:, 0:TTL], a=sb["sel"].ap()[:, n * 128:(n + 1) * 128],
              b=sb["bcl"].ap()[:, ts0:ts0 + TTL]: e.matmul(o, a, b, start=False, stop=True))
            E("pe", lambda e, o=psum[pc].ap()[:, 0:TTL], a=sb["sel"].ap()[:, (N + n) * 128:(N + n + 1) * 128],
              b=sb["bch"].ap()[:, ts0:ts0 + TTL]: e.matmul(o, a, b, start=True, stop=False))
            vbc = E("pe", lambda e, o=psum[pc].ap()[:, 0:TTL], a=sb["sel"].ap()[:, (N + n) * 128:(N + n + 1) * 128],
                    b=sb["bcl"].ap()[:, ts0:ts0 + TTL]: e.matmul(o, a, b, start=False, stop=True), sPE)
            for dc in range(DC):
                par = dc % 2
                W("ac", sDV, vd_scan[par])
                va = E("ac", lambda e, o=dA_t[par].ap(), i=delta.ap(dc * TTL, TTL),
                       s=sb["A"].ap()[:, dc * N + n:dc * N + n + 1]:
                       e.activation(o, i, AF.Exp, scale=s), sAC)
                W("dv", sPE, vbc)
                E("dv", lambda e, o=u_w.ap(dc * TTL, TTL), a=dx.ap(dc * TTL, TTL),
                  b=psum[pb].ap()[:, 0:TTL]:
                  e.tensor_tensor(o, a, b, ALU.mult))
                W("dv", sAC, va)
                col = n * DC + dc
                vsc = E("dv", lambda e, o=h_w.ap(dc * TTL, TTL), d0=dA_t[par].ap(),
                        d1=u_w.ap(dc * TTL, TTL),
                        ini=sb["state"].ap()[:, col:col + 1]:
                        e.tensor_tensor_scan(o, d0, d1, ini, ALU.mult, ALU.add), sDV)
                W("dv", sDV, vsc)
                E("dv", lambda e, o=sb["state"].ap()[:, col:col + 1],
                  i=h_w.ap(dc * TTL + TTL - 1, 1): e.tensor_copy(o, i))
                E("dv", lambda e, o=h_w.ap(dc * TTL, TTL), a=h_w.ap(dc * TTL, TTL),
                  b=psum[pc].ap()[:, 0:TTL]:
                  e.tensor_tensor(o, a, b, ALU.mult))
                vd = E("dv", lambda e, o=y_tt.ap(dc * TTL, TTL), a=y_tt.ap(dc * TTL, TTL),
                       b=h_w.ap(dc * TTL, TTL): e.tensor_tensor(o, a, b, ALU.add), sDV)
                vd_scan[par] = vd
            bank_free[pb] = (sDV, vd)
            bank_free[pc] = (sDV, vd)
        W("gp", sDV, vd)
        vg = DMA("gp", hbm["sp_y"].rearrange("c p t -> p c t")[:, :, ts0:ts0 + TTL],
                 y_tt.ap().rearrange("p (c t) -> p c t", c=DC))
        W("dv", sG, vg)
    drh.release()
    drl.release()
    wdt_hi.release()
    wdt_lo.release()
    delta.release()
    dx.release()
    s_tc.release()
    x_rl.release()
    e_tmp.release()
    u_w.release()
    h_w.release()
    for b_ in dA_t:
        b_.release()
    y_tt.release()
    spill_y = sG.n
    phase_barrier()

    hfin = Buf(af, N * DC)
    phase_barrier()
    vd = E("dv", lambda e, o=hfin.ap(), a=sb["state"].ap(),
           s=sb["mask_col"].ap()[:, 0:1]: e.tensor_scalar(o, a, s, None, ALU.mult), sDV)
    W("gp", sDV, vd)
    vg = DMA("gp", cc_carry_in.ap(), hfin.ap())
    W("gp", sG, vg)
    vcc = E("gp", lambda e: e.collective_compute(
        "AllReduce", ALU.add,
        replica_groups=[[0, 1], [2, 3], [4, 5], [6, 7]],
        ins=[cc_carry_in.ap().opt()],
        outs=[cc_carry_out.ap().opt()]), sCC)
    W("gp", sCC, vcc)
    vg = DMA("gp", hfin.ap(), cc_carry_out.ap())
    W("dv", sG, vg)
    v_hin = E("dv", lambda e, o=hfin.ap(), a=hfin.ap(),
              s=sb["mask_col"].ap()[:, 1:2]: e.tensor_scalar(o, a, s, None, ALU.mult), sDV)

    y_c = Buf(af, DC * TTL)
    s_c = Buf(af, DC * TTL)
    p_w = [Buf(af, DC * TTL), Buf(af, DC * TTL)]
    W("dv", sDV, v_hin)
    W("sy", sG, spill_y)
    vd_cp = [0, 0]
    for tt in range(NTT):
        ts0 = tt * TTL
        DMA("sy", y_c.ap().rearrange("p (c t) -> p c t", c=DC),
            hbm["sp_y"].rearrange("c p t -> p c t")[:, :, ts0:ts0 + TTL])
        vld = DMA("sy", s_c.ap().rearrange("p (c t) -> p c t", c=DC),
                  hbm["sp_S"].rearrange("c p t -> p c t")[:, :, ts0:ts0 + TTL])
        W("ac", sW, vld)
        W("dv", sW, vld)
        for n in range(N):
            par = n % 2
            pc = 6 + par
            bank_wait(pc)
            E("pe", lambda e, o=psum[pc].ap()[:, 0:TTL], a=sb["sel"].ap()[:, (N + n) * 128:(N + n + 1) * 128],
              b=sb["bch"].ap()[:, ts0:ts0 + TTL]: e.matmul(o, a, b, start=True, stop=False))
            vbc = E("pe", lambda e, o=psum[pc].ap()[:, 0:TTL], a=sb["sel"].ap()[:, (N + n) * 128:(N + n + 1) * 128],
                    b=sb["bcl"].ap()[:, ts0:ts0 + TTL]: e.matmul(o, a, b, start=False, stop=True), sPE)
            W("ac", sDV, vd_cp[par])
            for dc in range(DC):
                va = E("ac", lambda e, o=p_w[par].ap(dc * TTL, TTL), i=s_c.ap(dc * TTL, TTL),
                       s=sb["A"].ap()[:, dc * N + n:dc * N + n + 1]:
                       e.activation(o, i, AF.Exp, scale=s), sAC)
            W("dv", sAC, va)
            W("dv", sPE, vbc)
            E("dv", lambda e, o=p_w[par].ap().rearrange("p (c t) -> p c t", c=DC),
              a=p_w[par].ap().rearrange("p (c t) -> p c t", c=DC),
              b=psum[pc].ap()[:, 0:TTL].unsqueeze(1).broadcast_to([128, DC, TTL]):
              e.tensor_tensor(o, a, b, ALU.mult))
            for dc in range(DC):
                col = n * DC + dc
                vd = E("dv", lambda e, o=y_c.ap(dc * TTL, TTL), a=p_w[par].ap(dc * TTL, TTL),
                       s=hfin.ap()[:, col:col + 1], b=y_c.ap(dc * TTL, TTL):
                       e.scalar_tensor_tensor(o, a, s, b, ALU.mult, ALU.add), sDV)
            vd_cp[par] = vd
            bank_free[pc] = (sDV, vd)
        W("gp", sDV, vd)
        vg = DMA("gp", hbm["sp_y"].rearrange("c p t -> p c t")[:, :, ts0:ts0 + TTL],
                 y_c.ap().rearrange("p (c t) -> p c t", c=DC))
        W("sy", sG, vg)
    y_c.release()
    s_c.release()
    for b_ in p_w:
        b_.release()
    hfin.release()
    spill_y2 = sG.n
    phase_barrier()

    phase_barrier()
    wo_f32 = Buf(af, KC * D)
    wo_hi = Buf(ab, KC * D)
    wo_lo = Buf(ab, KC * D)
    v_wo = load_split_weight("W_out", D, D, wo_f32, wo_hi, wo_lo)
    wo_f32.release()

    y2h = Buf(ab, DC * TTL)
    y2l = Buf(ab, DC * TTL)
    xt_rl = Buf(af, DC * TTL)
    res1 = Buf(af, DC * TTL)
    x1_t = Buf(af, DC * TTL)
    sq_t = [Buf(af, TTL), Buf(af, TTL)]

    def emit_ln(src_fn, out_fn, affine, src_ready=0):
        nonlocal va_sq, vpe_sq
        bank_wait(5)
        bank_wait(3)
        W("ac", sDV, src_ready)
        for dc in range(DC):
            par = dc % 2
            W("ac", sPE, vpe_sq[par])
            va = E("ac", lambda e, o=sq_t[par].ap(), i=src_fn(dc):
                   e.activation(o, i, AF.Square), sAC)
            W("pe", sAC, va)
            E("pe", lambda e, o=psum[5].ap()[0:1, 0:TTL], a=sb["ones1"].ap()[:, 0:1],
              b=src_fn(dc), s=(dc == 0), st=(dc == DC - 1): e.matmul(o, a, b, start=s, stop=st))
            vpe = E("pe", lambda e, o=psum[3].ap()[0:1, 0:TTL], a=sb["ones1"].ap()[:, 0:1],
                    b=sq_t[par].ap(), s=(dc == 0), st=(dc == DC - 1):
                    e.matmul(o, a, b, start=s, stop=st), sPE)
            vpe_sq[par] = vpe
        W("dv", sPE, vpe)
        L = sb["lns"].ap()
        E("dv", lambda e, o=L[0:1, 0:TTL], i=psum[5].ap()[0:1, 0:TTL]:
          e.tensor_scalar(o, i, 1.0 / D, None, ALU.mult))
        E("dv", lambda e, o=L[0:1, TTL:2 * TTL], i=psum[3].ap()[0:1, 0:TTL]:
          e.tensor_scalar(o, i, 1.0 / D, None, ALU.mult))
        E("dv", lambda e, o=L[0:1, 2 * TTL:3 * TTL], a=L[0:1, 0:TTL], b=L[0:1, 0:TTL]:
          e.tensor_tensor(o, a, b, ALU.mult))
        E("dv", lambda e, o=L[0:1, TTL:2 * TTL], a=L[0:1, TTL:2 * TTL],
          b=L[0:1, 2 * TTL:3 * TTL]: e.tensor_tensor(o, a, b, ALU.subtract))
        vd = E("dv", lambda e, o=L[0:1, TTL:2 * TTL], i=L[0:1, TTL:2 * TTL]:
               e.tensor_scalar(o, i, LN_EPS, None, ALU.add), sDV)
        W("ac", sDV, vd)
        va = E("ac", lambda e, o=L[0:1, TTL:2 * TTL], i=L[0:1, TTL:2 * TTL]:
               e.activation(o, i, AF.Sqrt), sAC)
        W("dv", sAC, va)
        vf = E("dv", lambda e, o=L[0:1, 2 * TTL:3 * TTL], i=L[0:1, TTL:2 * TTL]:
               e.reciprocal(o, i), sDV)
        W("dv", sDV, vf)
        vd = E("dv", lambda e, o=L[0:1, 3 * TTL:4 * TTL], a=L[0:1, 0:TTL],
               b=L[0:1, 2 * TTL:3 * TTL]: e.tensor_tensor(o, a, b, ALU.mult), sDV)
        bank_wait(4)
        W("pe", sDV, vd)
        E("pe", lambda e, o=psum[4].ap()[:, 0:TTL], a=sb["ones1"].ap()[0:1, :],
          b=L[0:1, 2 * TTL:3 * TTL]: e.matmul(o, a, b, start=True, stop=True))
        vpe = E("pe", lambda e, o=psum[3].ap()[:, 0:TTL], a=sb["ones1"].ap()[0:1, :],
                b=L[0:1, 3 * TTL:4 * TTL]: e.matmul(o, a, b, start=True, stop=True), sPE)
        W("dv", sPE, vpe)
        for dc in range(DC):
            E("dv", lambda e, o=out_fn(dc), a=src_fn(dc), b=psum[4].ap()[:, 0:TTL]:
              e.tensor_tensor(o, a, b, ALU.mult))
            E("dv", lambda e, o=out_fn(dc), a=out_fn(dc), b=psum[3].ap()[:, 0:TTL]:
              e.tensor_tensor(o, a, b, ALU.subtract))
            if affine:
                vd = E("dv", lambda e, o=out_fn(dc), i=out_fn(dc),
                       g=sb["lng_col"].ap()[:, dc:dc + 1], bb=sb["lnb_col"].ap()[:, dc:dc + 1]:
                       e.tensor_scalar(o, i, g, bb, ALU.mult, ALU.add), sDV)
            else:
                vd = sDV.n
        if not affine:
            vd = E("dv", lambda e: e.engine_nop(), sDV)
        bank_free[3] = (sDV, vd)
        bank_free[4] = (sDV, vd)
        bank_free[5] = (sDV, vd)
        return vd

    va_sq = 0
    vpe_sq = [0, 0]
    W("pe", sDV, v_wo)
    W("sy", sG, spill_y2)
    vpe_tt = 0
    for tt in range(NTT):
        ts0 = tt * TTL
        W("sy", sPE, vpe_tt)
        W("sy", sDV, sDV.n)
        DMA("sy", xt_rl.ap().rearrange("p (c t) -> p c t", c=DC),
            hbm["sp_xt_f32"].rearrange("c p t -> p c t")[:, :, ts0:ts0 + TTL])
        DMA("sy", res1.ap().rearrange("p (c t) -> p c t", c=DC),
            hbm["sp_y"].rearrange("c p t -> p c t")[:, :, ts0:ts0 + TTL])
        vsz = DMA("sy", x1_t.ap().rearrange("p (c t) -> p c t", c=DC),
                  hbm["sp_sz"].rearrange("c p t -> p c t")[:, :, ts0:ts0 + TTL])
        W("dv", sW, vsz)
        E("dv", lambda e, o=res1.ap(), a=res1.ap(), b=x1_t.ap():
          e.tensor_tensor(o, a, b, ALU.mult))
        E("dv", lambda e, o=y2h.ap(), i=res1.ap(): e.tensor_copy(o, i))
        vd = E("dv", lambda e, o=y2l.ap(), a=res1.ap(), b=y2h.ap():
               e.tensor_tensor(o, a, b, ALU.subtract), sDV)
        W("sy", sDV, vd)
        W("pe", sDV, vd)
        vldx = sW.n
        for mc in range(DC):
            bank = mc % 2
            bank_wait(bank)
            vpe_tt = mm3(bank, KC,
                         lambda kc, mc=mc: (wo_hi.ap(kc * D + mc * 128, 128),
                                            wo_lo.ap(kc * D + mc * 128, 128)),
                         lambda kc: (y2h.ap(kc * TTL, TTL), y2l.ap(kc * TTL, TTL)))
            W("dv", sPE, vpe_tt)
            W("dv", sW, vldx)
            vd = E("dv", lambda e, o=res1.ap(mc * TTL, TTL), a=psum[bank].ap()[:, 0:TTL],
                   b=xt_rl.ap(mc * TTL, TTL): e.tensor_tensor(o, a, b, ALU.add), sDV)
            bank_free[bank] = (sDV, vd)
        emit_ln(lambda dc: res1.ap(dc * TTL, TTL), lambda dc: res1.ap(dc * TTL, TTL),
                affine=True, src_ready=vd)
        for dc in range(DC):
            vd = E("dv", lambda e, o=res1.ap(dc * TTL, TTL), a=res1.ap(dc * TTL, TTL),
                   b=xt_rl.ap(dc * TTL, TTL): e.tensor_tensor(o, a, b, ALU.add), sDV)
        emit_ln(lambda dc: res1.ap(dc * TTL, TTL), lambda dc: x1_t.ap(dc * TTL, TTL),
                affine=False, src_ready=vd)
        vd = sDV.n
        W("gp", sDV, vd)
        vg = DMA("gp", hbm["sp_x1"].rearrange("c p t -> p c t")[:, :, ts0:ts0 + TTL],
                 x1_t.ap().rearrange("p (c t) -> p c t", c=DC))
        W("dv", sG, vg)
    wo_hi.release()
    wo_lo.release()
    y2h.release()
    y2l.release()
    xt_rl.release()
    res1.release()
    x1_t.release()
    for b_ in sq_t:
        b_.release()
    spill_x1 = sG.n
    phase_barrier()


    def absmax_init(col):
        E("dv", lambda e, o=sb["red"].ap()[:, col:col + 1]: e.memset(o, 0.0))

    def absmax_acc(src_ap, col):
        vf = E("dv", lambda e, o=sb["redt"].ap(), i=src_ap:
               e.tensor_reduce(o, i, AX.X, ALU.max, apply_absolute_value=True), sDV)
        W("dv", sDV, vf)
        E("dv", lambda e, o=sb["red"].ap()[:, col:col + 1], a=sb["red"].ap()[:, col:col + 1],
          b=sb["redt"].ap(): e.tensor_tensor(o, a, b, ALU.max))

    def gmax_finish(col, slot):
        bank_wait(2)
        vd = E("dv", lambda e: e.engine_nop(), sDV)
        W("pe", sDV, vd)
        vpe = E("pe", lambda e, o=psum[2].ap()[0:1, 0:128], i=sb["red"].ap()[:, col:col + 1],
                idn=sb["ident"].ap(): e.transpose(o, i, idn), sPE)
        W("dv", sPE, vpe)
        vd = E("dv", lambda e, o=sb["qs"].ap()[0:1, slot:slot + 1], i=psum[2].ap()[0:1, 0:128]:
               e.tensor_reduce(o, i, AX.X, ALU.max), sDV)
        W("dv", sDV, vd)
        bank_free[2] = (sDV, vd)
        return vd

    def coll_max():
        vd = sDV.n
        W("gp", sDV, vd)
        vg = DMA("gp", cc_max_in.ap(), sb["qs"].ap())
        W("gp", sG, vg)
        vcc = E("gp", lambda e: e.collective_compute(
            "AllReduce", ALU.max, replica_groups=[list(range(8))],
            ins=[cc_max_in.ap().opt()], outs=[cc_max_out.ap().opt()]), sCC)
        W("gp", sCC, vcc)
        return DMA("gp", sb["qs"].ap(), cc_max_out.ap())

    def scale_calc(slot_max, slot_s, slot_r, qmax, vbi):
        W("dv", sG, vbi)
        Q = sb["qs"].ap()
        vf = E("dv", lambda e, o=Q[0:1, slot_s:slot_s + 1], i=Q[0:1, slot_max:slot_max + 1]:
               e.tensor_scalar(o, i, 1.0 / qmax, 1e-12, ALU.mult, ALU.max), sDV)
        W("dv", sDV, vf)
        vr = E("dv", lambda e, o=Q[0:1, slot_r:slot_r + 1], i=Q[0:1, slot_s:slot_s + 1]:
               e.reciprocal(o, i), sDV)
        W("dv", sDV, vr)
        return vr

    def col_bcast(slot, col, vd):
        bank_wait(2)
        W("pe", sDV, vd)
        vpe = E("pe", lambda e, o=psum[2].ap()[:, 0:1], a=sb["ones1"].ap()[0:1, :],
                b=sb["qs"].ap()[0:1, slot:slot + 1]:
                e.matmul(o, a, b, start=True, stop=True), sPE)
        W("dv", sPE, vpe)
        vd2 = E("dv", lambda e, o=sb["qcol"].ap()[:, col:col + 1], i=psum[2].ap()[:, 0:1]:
                e.tensor_copy(o, i), sDV)
        bank_free[2] = (sDV, vd2)
        return vd2

    x1w = [Buf(af, DC * TTL), Buf(af, DC * TTL)]
    qw = Buf(af, DC * TTL)
    absmax_init(0)
    W("sy", sG, spill_x1)
    va_w = [0, 0]
    for tt in range(NTT):
        par = tt % 2
        W("sy", sDV, va_w[par])
        vld = DMA("sy", x1w[par].ap().rearrange("p (c t) -> p c t", c=DC),
                  hbm["sp_x1"].rearrange("c p t -> p c t")[:, :, tt * TTL:(tt + 1) * TTL])
        W("dv", sW, vld)
        absmax_acc(x1w[par].ap(), 0)
        va_w[par] = E("dv", lambda e: e.engine_nop(), sDV)
    gmax_finish(0, 0)
    vbi = coll_max()
    vd = scale_calc(0, 1, 2, 32767.0, vbi)
    vd = col_bcast(1, 0, vd)
    vd = col_bcast(2, 1, vd)
    scales_x1 = vd

    xq_hi = Buf(ab, DC * T)
    xq_lo = Buf(ab, DC * T)
    va_q = [sDV.n, sDV.n]
    for tt in range(NTT):
        par = tt % 2
        W("sy", sDV, va_q[par])
        vld = DMA("sy", x1w[par].ap().rearrange("p (c t) -> p c t", c=DC),
                  hbm["sp_x1"].rearrange("c p t -> p c t")[:, :, tt * TTL:(tt + 1) * TTL])
        W("ac", sW, vld)
        if tt == 0:
            W("ac", sDV, scales_x1)
        va = E("ac", lambda e, o=x1w[par].ap(), i=x1w[par].ap(), r=sb["qcol"].ap()[:, 1:2],
               c=sb["rndc"].ap()[:, 0:1]:
               e.activation(o, i, AF.Identity, bias=c, scale=r), sAC)
        W("dv", sAC, va)
        E("dv", lambda e, o=qw.ap(), i=x1w[par].ap():
          e.tensor_scalar(o, i, -RND_C, None, ALU.add))
        E("dv", lambda e, o=xq_hi.ap().rearrange("p (c u t) -> p c u t", c=DC, u=NTT)[:, :, tt, :],
          i=qw.ap().rearrange("p (c t) -> p c t", c=DC),
          s=sb["qcol"].ap()[:, 0:1]: e.tensor_scalar(o, i, s, None, ALU.mult))
        vq = E("dv", lambda e, o=xq_lo.ap().rearrange("p (c u t) -> p c u t", c=DC, u=NTT)[:, :, tt, :],
               a=qw.ap().rearrange("p (c t) -> p c t", c=DC),
               s=sb["qcol"].ap()[:, 0:1],
               b=xq_hi.ap().rearrange("p (c u t) -> p c u t", c=DC, u=NTT)[:, :, tt, :]:
               e.scalar_tensor_tensor(o, a, s, b, ALU.mult, ALU.subtract), sDV)
        va_q[par] = vq
    for b_ in x1w:
        b_.release()
    qw.release()
    xq_done = sDV.n
    phase_barrier()

    wrow2 = [Buf(af, 1024), Buf(af, 1024)]
    wu2 = [Buf(af, 1024), Buf(af, 1024)]
    wbs2 = [sb["wbs"], sb["wbr"]]
    vd_deq = [0, 0]
    vpe_tr = [0, 0]
    vld_row = [0, 0]

    def prep_dequant(wname, oc, sec, par):
        nonlocal vd_deq, vld_row
        nb = 32
        W("sy", sDV, vd_deq[par])
        W("sy", sPE, vpe_tr[par])
        v = DMA("sy", wrow2[par].ap(), hbm[wname][oc * 128:(oc + 1) * 128,
                                                  sec * 1024:(sec + 1) * 1024])
        vld_row[par] = v
        W("dv", sW, v)
        W("dv", sPE, vpe_tr[par])
        bsl = sb["wbs"].ap()[:, par * 32:par * 32 + 32]
        brl = sb["wbr"].ap()[:, par * 32:par * 32 + 32]
        vf = E("dv", lambda e, o=bsl.unsqueeze(2),
               i=wrow2[par].ap().rearrange("p (b i) -> p b i", b=nb):
               e.tensor_reduce(o, i, AX.X, ALU.max, apply_absolute_value=True), sDV)
        W("dv", sDV, vf)
        vf = E("dv", lambda e, o=bsl, i=bsl:
               e.tensor_scalar(o, i, 1.0 / 7.0, 1e-12, ALU.mult, ALU.max), sDV)
        W("dv", sDV, vf)
        vf = E("dv", lambda e, o=brl, i=bsl: e.reciprocal(o, i), sDV)
        W("dv", sDV, vf)
        vd = E("dv", lambda e, o=wu2[par].ap().rearrange("p (b i) -> p b i", b=nb),
               a=wrow2[par].ap().rearrange("p (b i) -> p b i", b=nb),
               b=brl.unsqueeze(2).broadcast_to([128, nb, 32]):
               e.tensor_tensor(o, a, b, ALU.mult), sDV)
        W("ac", sDV, vd)
        va = E("ac", lambda e, o=wu2[par].ap(), i=wu2[par].ap(), c=sb["rndc"].ap()[:, 0:1]:
               e.activation(o, i, AF.Identity, bias=c), sAC)
        W("dv", sAC, va)
        vd = E("dv", lambda e, o=wu2[par].ap().rearrange("p (b i) -> p b i", b=nb),
               a=wu2[par].ap().rearrange("p (b i) -> p b i", b=nb),
               s=-RND_C,
               b=bsl.unsqueeze(2).broadcast_to([128, nb, 32]):
               e.scalar_tensor_tensor(o, a, s, b, ALU.add, ALU.mult), sDV)
        vd_deq[par] = vd
        return vd

    def prep_transpose(par, dst_hi, dst_lo, dst_off_fn):
        nonlocal vpe_tr
        W("pe", sDV, vd_deq[par])
        for kcl in range(8):
            bank = 2 + (kcl % 2)
            bank_wait(bank)
            vpe = E("pe", lambda e, o=psum[bank].ap()[:, 0:128],
                    i=wu2[par].ap(kcl * 128, 128), idn=sb["ident"].ap():
                    e.transpose(o, i, idn), sPE)
            W("ac", sPE, vpe)
            va = E("ac", lambda e, o=dst_hi.ap(dst_off_fn(kcl), 128),
                   i=psum[bank].ap()[:, 0:128]: e.copy(o, i), sAC)
            W("dv", sAC, va)
            vdl = E("dv", lambda e, o=dst_lo.ap(dst_off_fn(kcl), 128),
                    a=psum[bank].ap()[:, 0:128], b=dst_hi.ap(dst_off_fn(kcl), 128):
                    e.tensor_tensor(o, a, b, ALU.subtract), sDV)
            bank_free[bank] = (sDV, vdl)
        vpe_tr[par] = vpe
        return vdl

    wq_hi2 = [Buf(ab, KC * 128), Buf(ab, KC * 128)]
    wq_lo2 = [Buf(ab, KC * 128), Buf(ab, KC * 128)]
    up_ev2 = [Buf(af, TTL), Buf(af, TTL)]
    vg_ev = [0, 0]
    absmax_init(1)
    absmax_init(2)
    vpe_b = 0
    jobs = [("w_up", "sp_up", 1, oc) for oc in range(IC)] + \
           [("w_gate", "sp_gate", 2, oc) for oc in range(IC)]
    vdl_tr = [0, 0]
    prep_dequant(jobs[0][0], jobs[0][3], 0, 0)
    vdl_tr[0] = prep_transpose(0, wq_hi2[0], wq_lo2[0], lambda kcl: kcl * 128)
    for j, (wname, spname, rcol, oc) in enumerate(jobs):
        par = j % 2
        if j + 1 < len(jobs):
            prep_dequant(jobs[j + 1][0], jobs[j + 1][3], 0, 1 - par)
        W("pe", sDV, vdl_tr[par])
        if j == 0:
            W("pe", sDV, xq_done)
        for tt in range(NTT):
            bank = tt % 2
            bank_wait(bank)
            vpe_b = mm3(bank, KC,
                        lambda kc, par=par: (wq_hi2[par].ap(kc * 128, 128),
                                             wq_lo2[par].ap(kc * 128, 128)),
                        lambda kc, tt=tt: (xq_hi.ap(kc * T + tt * TTL, TTL),
                                           xq_lo.ap(kc * T + tt * TTL, TTL)))
            epar = tt % 2
            W("ac", sPE, vpe_b)
            W("ac", sG, vg_ev[epar])
            va2 = E("ac", lambda e, o=up_ev2[epar].ap(), i=psum[bank].ap()[:, 0:TTL]:
                    e.copy(o, i), sAC)
            W("dv", sAC, va2)
            absmax_acc(up_ev2[epar].ap(), rcol)
            vd = E("dv", lambda e: e.engine_nop(), sDV)
            bank_free[bank] = (sDV, vd)
            W("gp", sDV, vd)
            vg_ev[epar] = DMA("gp", hbm[spname][oc, :, tt * TTL:(tt + 1) * TTL],
                              up_ev2[epar].ap())
        if j + 1 < len(jobs):
            vdl_tr[1 - par] = prep_transpose(1 - par, wq_hi2[1 - par], wq_lo2[1 - par],
                                             lambda kcl: kcl * 128)
        xq_hi.release()
    xq_lo.release()
    gmax_finish(1, 3)
    gmax_finish(2, 4)
    vbi = coll_max()
    vd = scale_calc(3, 5, 6, 32767.0, vbi)
    vd = scale_calc(4, 7, 8, 32767.0, vd * 0 + vbi)
    vd = col_bcast(5, 2, vd)
    vd = col_bcast(6, 3, vd)
    vd = col_bcast(7, 4, vd)
    vd = col_bcast(8, 5, vd)
    scales_ug = vd

    phase_barrier()

    djobs = [(oc, sec) for oc in range(DC) for sec in range(3)]
    prep_dequant("w_down", djobs[0][0], djobs[0][1], 0)
    vg_wdn = [0, 0]
    for j, (oc, sec) in enumerate(djobs):
        par = j % 2
        if j + 1 < len(djobs):
            prep_dequant("w_down", djobs[j + 1][0], djobs[j + 1][1], 1 - par)
        W("ac", sG, vg_wdn[par])
        W("dv", sG, vg_wdn[par])
        vprep = prep_transpose(par, wq_hi2[par], wq_lo2[par], lambda kcl: kcl * 128)
        W("gp", sDV, vprep)
        DMA("gp", hbm["sp_wdn_hi"].rearrange("c k p m -> p c k m")[:, oc, sec * 8:(sec + 1) * 8, :],
            wq_hi2[par].ap().rearrange("p (k m) -> p k m", k=8))
        vg_wdn[par] = DMA("gp", hbm["sp_wdn_lo"].rearrange("c k p m -> p c k m")[:, oc, sec * 8:(sec + 1) * 8, :],
                          wq_lo2[par].ap().rearrange("p (k m) -> p k m", k=8))
    W("gp", sG, vg_wdn[0])
    W("gp", sG, vg_wdn[1])
    phase_barrier()
    for b_ in wrow2 + wu2 + up_ev2 + wq_hi2 + wq_lo2:
        b_.release()
    wdn_spilled = sG.n
    phase_barrier()

    phase_barrier()
    gw = [Buf(af, T), Buf(af, T)]
    qgw = Buf(af, T)
    sigw = Buf(af, T)
    absmax_init(3)
    va_d = [0, 0]
    vg_d = [0, 0]
    for oc in range(IC):
        par = oc % 2
        W("sy", sAC, va_d[par])
        W("sy", sG, vg_d[par])
        vld = DMA("sy", gw[par].ap(), hbm["sp_gate"][oc])
        W("ac", sW, vld)
        if oc == 0:
            W("ac", sDV, scales_ug)
        va = E("ac", lambda e, o=gw[par].ap(), i=gw[par].ap(), r=sb["qcol"].ap()[:, 5:6],
               c=sb["rndc"].ap()[:, 0:1]:
               e.activation(o, i, AF.Identity, bias=c, scale=r), sAC)
        va_d[par] = va
        W("dv", sAC, va)
        vd = E("dv", lambda e, o=qgw.ap(), i=gw[par].ap():
               e.tensor_scalar(o, i, -RND_C, None, ALU.add), sDV)
        W("ac", sDV, vd)
        E("ac", lambda e, o=sigw.ap(), i=qgw.ap(), s=sb["qcol"].ap()[:, 4:5]:
          e.activation(o, i, AF.Sigmoid, scale=s))
        va = E("ac", lambda e, o=sigw.ap(), i=sigw.ap(), c=sb["rndc"].ap()[:, 0:1]:
               e.activation(o, i, AF.Identity, bias=c, scale=65536.0), sAC)
        W("dv", sAC, va)
        E("dv", lambda e, o=sigw.ap(), i=sigw.ap():
          e.tensor_scalar(o, i, -RND_C, 65535.0, ALU.add, ALU.min))
        W("dv", sG, vg_d[par])
        E("dv", lambda e, o=gw[par].ap(), a=qgw.ap(), b=sigw.ap():
          e.tensor_tensor(o, a, b, ALU.mult))
        absmax_acc(gw[par].ap(), 3)
        vdn = E("dv", lambda e: e.engine_nop(), sDV)
        W("gp", sDV, vdn)
        vg_d[par] = DMA("gp", hbm["sp_pg"][oc], gw[par].ap())
    gmax_finish(3, 9)
    vbi = coll_max()
    Q = sb["qs"].ap()
    W("dv", sG, vbi)
    for fn in [
        lambda e, o=Q[0:1, 10:11], i=Q[0:1, 7:8]:
            e.tensor_scalar(o, i, 1.0 / 65536.0, None, ALU.mult),
        lambda e, o=Q[0:1, 11:12], a=Q[0:1, 9:10], b=Q[0:1, 10:11]:
            e.tensor_tensor(o, a, b, ALU.mult),
        lambda e, o=Q[0:1, 11:12], i=Q[0:1, 11:12]:
            e.tensor_scalar(o, i, 1.0 / 32767.0, 1e-12, ALU.mult, ALU.max),
        lambda e, o=Q[0:1, 12:13], i=Q[0:1, 11:12]: e.reciprocal(o, i),
        lambda e, o=Q[0:1, 12:13], a=Q[0:1, 10:11], b=Q[0:1, 12:13]:
            e.tensor_tensor(o, a, b, ALU.mult),
    ]:
        vd = E("dv", fn, sDV)
        W("dv", sDV, vd)
    vd = col_bcast(12, 6, vd)
    scale2_done = vd

    phase_barrier()

    phase_barrier()
    uw = [Buf(af, T), Buf(af, T)]
    quw = sigw
    absmax_init(4)
    va_f = [0, 0]
    va_fu = [0, 0]
    vg_f = [0, 0]
    for oc in range(IC):
        par = oc % 2
        W("sy", sAC, va_d[par])
        W("sy", sG, vg_d[par])
        vld1 = DMA("sy", gw[par].ap(), hbm["sp_pg"][oc])
        W("sy", sAC, va_fu[par])
        W("sy", sG, vg_f[par])
        vld2 = DMA("sy", uw[par].ap(), hbm["sp_up"][oc])
        W("ac", sW, vld1)
        if oc == 0:
            W("ac", sDV, scale2_done)
        va = E("ac", lambda e, o=gw[par].ap(), i=gw[par].ap(), r=sb["qcol"].ap()[:, 6:7],
               c=sb["rndc"].ap()[:, 0:1]:
               e.activation(o, i, AF.Identity, bias=c, scale=r), sAC)
        va_d[par] = va
        W("dv", sAC, va)
        vtt = E("dv", lambda e, o=qgw.ap(), i=gw[par].ap():
               e.tensor_scalar(o, i, -RND_C, None, ALU.add), sDV)
        W("ac", sW, vld2)
        W("ac", sDV, vtt)
        va2 = E("ac", lambda e, o=uw[par].ap(), i=uw[par].ap(), r=sb["qcol"].ap()[:, 3:4],
                c=sb["rndc"].ap()[:, 0:1]:
                e.activation(o, i, AF.Identity, bias=c, scale=r), sAC)
        va_fu[par] = va2
        W("dv", sAC, va2)
        E("dv", lambda e, o=quw.ap(), i=uw[par].ap():
          e.tensor_scalar(o, i, -RND_C, None, ALU.add))
        W("dv", sG, vg_f[par])
        E("dv", lambda e, o=uw[par].ap(), a=qgw.ap(), b=quw.ap():
          e.tensor_tensor(o, a, b, ALU.mult))
        absmax_acc(uw[par].ap(), 4)
        vdn = E("dv", lambda e: e.engine_nop(), sDV)
        W("gp", sDV, vdn)
        vg_f[par] = DMA("gp", hbm["sp_po"][oc], uw[par].ap())
    for b_ in gw + uw:
        b_.release()
    qgw.release()
    sigw.release()
    gmax_finish(4, 13)
    vbi = coll_max()
    W("dv", sG, vbi)
    for fn in [
        lambda e, o=Q[0:1, 14:15], a=Q[0:1, 11:12], b=Q[0:1, 5:6]:
            e.tensor_tensor(o, a, b, ALU.mult),
        lambda e, o=Q[0:1, 15:16], a=Q[0:1, 13:14], b=Q[0:1, 14:15]:
            e.tensor_tensor(o, a, b, ALU.mult),
        lambda e, o=Q[0:1, 15:16], i=Q[0:1, 15:16]:
            e.tensor_scalar(o, i, 1.0 / 32767.0, 1e-12, ALU.mult, ALU.max),
        lambda e, o=Q[0:1, 0:1], i=Q[0:1, 15:16]: e.reciprocal(o, i),
        lambda e, o=Q[0:1, 0:1], a=Q[0:1, 14:15], b=Q[0:1, 0:1]:
            e.tensor_tensor(o, a, b, ALU.mult),
    ]:
        vd = E("dv", fn, sDV)
        W("dv", sDV, vd)
    vd = col_bcast(0, 7, vd)
    vd = col_bcast(15, 1, vd)
    scaleo_done = vd

    phase_barrier()

    phase_barrier()
    TH = min(256, T)
    NTH = T // TH
    oq_hi = Buf(ab, IC * TH)
    oq_lo = Buf(ab, IC * TH)
    wdnb_hi = [Buf(ab, IC * 128), Buf(ab, IC * 128)]
    wdnb_lo = [Buf(ab, IC * 128), Buf(ab, IC * 128)]
    vpe_wdnb = [0, 0]
    res3 = Buf(af, DC * TH)
    x2_t = Buf(af, DC * TH)
    x1row = Buf(af, TH)
    orow = Buf(af, D)
    sqh = [Buf(af, TH), Buf(af, TH)]

    LS = max(TTL, 256)

    def emit_ln_h(src_fn, out_fn, src_ready=0):
        nonlocal vpe_sqh
        bank_wait(5)
        bank_wait(3)
        W("ac", sDV, src_ready)
        for dc in range(DC):
            par = dc % 2
            W("ac", sPE, vpe_sqh[par])
            va = E("ac", lambda e, o=sqh[par].ap(), i=src_fn(dc):
                   e.activation(o, i, AF.Square), sAC)
            W("pe", sAC, va)
            E("pe", lambda e, o=psum[5].ap()[0:1, 0:TH], a=sb["ones1"].ap()[:, 0:1],
              b=src_fn(dc), s=(dc == 0), st=(dc == DC - 1): e.matmul(o, a, b, start=s, stop=st))
            vpe = E("pe", lambda e, o=psum[3].ap()[0:1, 0:TH], a=sb["ones1"].ap()[:, 0:1],
                    b=sqh[par].ap(), s=(dc == 0), st=(dc == DC - 1):
                    e.matmul(o, a, b, start=s, stop=st), sPE)
            vpe_sqh[par] = vpe
        W("dv", sPE, vpe)
        L = sb["lns"].ap()
        E("dv", lambda e, o=L[0:1, 0:TH], i=psum[5].ap()[0:1, 0:TH]:
          e.tensor_scalar(o, i, 1.0 / D, None, ALU.mult))
        E("dv", lambda e, o=L[0:1, LS:LS + TH], i=psum[3].ap()[0:1, 0:TH]:
          e.tensor_scalar(o, i, 1.0 / D, None, ALU.mult))
        E("dv", lambda e, o=L[0:1, 2 * LS:2 * LS + TH], a=L[0:1, 0:TH], b=L[0:1, 0:TH]:
          e.tensor_tensor(o, a, b, ALU.mult))
        E("dv", lambda e, o=L[0:1, LS:LS + TH], a=L[0:1, LS:LS + TH],
          b=L[0:1, 2 * LS:2 * LS + TH]: e.tensor_tensor(o, a, b, ALU.subtract))
        vd = E("dv", lambda e, o=L[0:1, LS:LS + TH], i=L[0:1, LS:LS + TH]:
               e.tensor_scalar(o, i, LN_EPS, None, ALU.add), sDV)
        W("ac", sDV, vd)
        va = E("ac", lambda e, o=L[0:1, LS:LS + TH], i=L[0:1, LS:LS + TH]:
               e.activation(o, i, AF.Sqrt), sAC)
        W("dv", sAC, va)
        vf = E("dv", lambda e, o=L[0:1, 2 * LS:2 * LS + TH], i=L[0:1, LS:LS + TH]:
               e.reciprocal(o, i), sDV)
        W("dv", sDV, vf)
        vd = E("dv", lambda e, o=L[0:1, 3 * LS:3 * LS + TH], a=L[0:1, 0:TH],
               b=L[0:1, 2 * LS:2 * LS + TH]: e.tensor_tensor(o, a, b, ALU.mult), sDV)
        bank_wait(4)
        W("pe", sDV, vd)
        E("pe", lambda e, o=psum[4].ap()[:, 0:TH], a=sb["ones1"].ap()[0:1, :],
          b=L[0:1, 2 * LS:2 * LS + TH]: e.matmul(o, a, b, start=True, stop=True))
        vpe = E("pe", lambda e, o=psum[3].ap()[:, 0:TH], a=sb["ones1"].ap()[0:1, :],
                b=L[0:1, 3 * LS:3 * LS + TH]: e.matmul(o, a, b, start=True, stop=True), sPE)
        W("dv", sPE, vpe)
        for dc in range(DC):
            E("dv", lambda e, o=out_fn(dc), a=src_fn(dc), b=psum[4].ap()[:, 0:TH]:
              e.tensor_tensor(o, a, b, ALU.mult))
            vd = E("dv", lambda e, o=out_fn(dc), a=out_fn(dc), b=psum[3].ap()[:, 0:TH]:
                   e.tensor_tensor(o, a, b, ALU.subtract), sDV)
        bank_free[3] = (sDV, vd)
        bank_free[4] = (sDV, vd)
        bank_free[5] = (sDV, vd)
        return vd

    vpe_sqh = [0, 0]
    va_h = 0
    vst = 0
    IC2 = IC // 2
    pw = Buf(af, IC2 * TH)
    qhw = Buf(af, IC2 * TH)
    for th in range(NTH):
        hs0 = th * TH
        for half in range(2):
            o0 = half * IC2
            W("sy", sDV, va_h)
            vld = DMA("sy", pw.ap().rearrange("p (c t) -> p c t", c=IC2),
                      hbm["sp_po"].rearrange("c p t -> p c t")[:, o0:o0 + IC2, hs0:hs0 + TH])
            W("ac", sW, vld)
            if th == 0 and half == 0:
                W("ac", sDV, scaleo_done)
            W("ac", sDV, va_h)
            va = E("ac", lambda e, o=pw.ap(), i=pw.ap(),
                   r=sb["qcol"].ap()[:, 7:8], c=sb["rndc"].ap()[:, 0:1]:
                   e.activation(o, i, AF.Identity, bias=c, scale=r), sAC)
            W("dv", sAC, va)
            E("dv", lambda e, o=qhw.ap(), i=pw.ap():
              e.tensor_scalar(o, i, -RND_C, None, ALU.add))
            E("dv", lambda e, o=oq_hi.ap(o0 * TH, IC2 * TH), i=qhw.ap(),
              s=sb["qcol"].ap()[:, 1:2]: e.tensor_scalar(o, i, s, None, ALU.mult))
            vq = E("dv", lambda e, o=oq_lo.ap(o0 * TH, IC2 * TH), a=qhw.ap(),
                   s=sb["qcol"].ap()[:, 1:2], b=oq_hi.ap(o0 * TH, IC2 * TH):
                   e.scalar_tensor_tensor(o, a, s, b, ALU.mult, ALU.subtract), sDV)
            va_h = vq
        W("pe", sDV, vq)
        if th == 0:
            W("sy", sG, wdn_spilled)

        def wdn_load(mc, par):
            W("sy", sPE, vpe_wdnb[par])
            DMA("sy", wdnb_hi[par].ap().rearrange("p (k m) -> p k m", k=IC),
                hbm["sp_wdn_hi"].rearrange("c k p m -> p c k m")[:, mc, :, :])
            return DMA("sy", wdnb_lo[par].ap().rearrange("p (k m) -> p k m", k=IC),
                       hbm["sp_wdn_lo"].rearrange("c k p m -> p c k m")[:, mc, :, :])

        vwl_par = [0, 0]
        vwl_par[0] = wdn_load(0, 0)
        for mc in range(DC):
            bank = mc % 2
            par = mc % 2
            W("pe", sW, vwl_par[par])
            bank_wait(bank)
            vpe_h = mm3(bank, IC,
                        lambda kc, par=par: (wdnb_hi[par].ap(kc * 128, 128),
                                             wdnb_lo[par].ap(kc * 128, 128)),
                        lambda kc: (oq_hi.ap(kc * TH, TH), oq_lo.ap(kc * TH, TH)),
                        nw=TH)
            vpe_wdnb[par] = vpe_h
            if mc + 1 < DC:
                vwl_par[1 - par] = wdn_load(mc + 1, 1 - par)
            vldx = DMA("sy", x1row.ap(), hbm["sp_x1"][mc, :, hs0:hs0 + TH])
            W("dv", sPE, vpe_h)
            W("dv", sW, vldx)
            vd = E("dv", lambda e, o=res3.ap(mc * TH, TH), a=psum[bank].ap()[:, 0:TH],
                   b=x1row.ap(): e.tensor_tensor(o, a, b, ALU.add), sDV)
            bank_free[bank] = (sDV, vd)
            W("sy", sDV, vd)
        emit_ln_h(lambda dc: res3.ap(dc * TH, TH), lambda dc: x2_t.ap(dc * TH, TH),
                  src_ready=vd)
        vdl = sDV.n
        W("pe", sDV, vdl)
        for j in range(TH // 128):
            for dc in range(DC):
                bank = 6 + (dc % 2)
                bank_wait(bank)
                vpe = E("pe", lambda e, o=psum[bank].ap()[:, 0:128],
                        i=x2_t.ap(dc * TH + j * 128, 128), idn=sb["ident"].ap():
                        e.transpose(o, i, idn), sPE)
                W("ac", sPE, vpe)
                W("ac", sW, vst)
                va = E("ac", lambda e, o=orow.ap(dc * 128, 128),
                       i=psum[bank].ap()[:, 0:128]: e.copy(o, i), sAC)
                bank_free[bank] = (sAC, va)
            W("sy", sAC, va)
            trow = th * (TH // 128) + j
            vst = DMA("sy", hbm["out"][trow * 128:(trow + 1) * 128, :], orow.ap())

    ops["sy"].append(lambda e, v=sW.n: e.wait_ge(sW.h, v))
    ops["gp"].append(lambda e, v=sG.n: e.wait_ge(sG.h, v))

    with nc.Block() as block:
        @block.sync
        def _(eng):
            for f in ops["sy"]:
                f(eng)

        @block.tensor
        def _(eng):
            for f in ops["pe"]:
                f(eng)

        @block.vector
        def _(eng):
            for f in ops["dv"]:
                f(eng)

        @block.scalar
        def _(eng):
            for f in ops["ac"]:
                f(eng)

        @block.gpsimd
        def _(eng):
            for f in ops["gp"]:
                f(eng)

    es.close()
    return nc


def col_layout(v, DC):
    return np.ascontiguousarray(np.asarray(v, np.float32).reshape(DC, 128).T)


def kernel(L0_attn_x_in, L0_attn_W_in, L0_attn_W_x, L0_attn_W_dt, L0_attn_b_dt,
           L0_attn_A_log, L0_attn_D_param, L0_attn_W_out, L0_attn_ln_g,
           L0_attn_ln_b, L0_ffn_w_gate, L0_ffn_w_up, L0_ffn_w_down):
    cfg = Cfg()
    x = np.ascontiguousarray(np.asarray(L0_attn_x_in, dtype=np.float32))
    Bb, L, D = x.shape
    T, N, DC = cfg.T, cfg.N, cfg.DC

    sel = np.zeros((2 * N, 2 * N * 128), dtype=np.float32)
    for j in range(2 * N):
        sel[j, j * 128:(j + 1) * 128] = 1.0
    base = {
        "W_in": np.ascontiguousarray(L0_attn_W_in, dtype=np.float32),
        "W_x": np.ascontiguousarray(L0_attn_W_x, dtype=np.float32),
        "W_dt": np.ascontiguousarray(L0_attn_W_dt, dtype=np.float32),
        "W_out": np.ascontiguousarray(L0_attn_W_out, dtype=np.float32),
        "w_up": np.ascontiguousarray(L0_ffn_w_up, dtype=np.float32),
        "w_gate": np.ascontiguousarray(L0_ffn_w_gate, dtype=np.float32),
        "w_down": np.ascontiguousarray(L0_ffn_w_down, dtype=np.float32),
        "bdt_col": col_layout(L0_attn_b_dt, DC),
        "alog_col": np.ascontiguousarray(
            np.asarray(L0_attn_A_log, np.float32).reshape(DC, 128, N)
            .transpose(1, 0, 2).reshape(128, DC * N)),
        "dpar_col": col_layout(L0_attn_D_param, DC),
        "lng_col": col_layout(L0_attn_ln_g, DC),
        "lnb_col": col_layout(L0_attn_ln_b, DC),
        "ident": np.eye(128, dtype=np.float32),
        "ones1": np.ones((128, 128), dtype=np.float32),
        "identb": np.eye(128, dtype=ml_dtypes.bfloat16),
        "sel": sel.astype(ml_dtypes.bfloat16),
    }
    in_maps = []
    for c in range(8):
        b, h = c // 2, c % 2
        im = dict(base)
        im["x_in"] = np.ascontiguousarray(x[b, h * T:(h + 1) * T, :])
        mask = np.zeros((128, 2), np.float32)
        mask[:, 0] = 1.0 if h == 0 else 0.0
        mask[:, 1] = 1.0 if h == 1 else 0.0
        im["mask_col"] = mask
        in_maps.append(im)

    nc = build_nc(cfg)
    res = run_bass_kernel_spmd(nc, in_maps, core_ids=list(range(8)))
    out = np.empty((Bb, L, D), np.float32)
    for c in range(8):
        b, h = c // 2, c % 2
        out[b, h * T:(h + 1) * T, :] = res.results[c]["out"]
    return out
